# version 15
# baseline (speedup 1.0000x reference)
import contextlib
import os as _os
import numpy as np
import concourse.bass as bass
import concourse.mybir as mybir
from concourse.bass_utils import run_bass_kernel_spmd
import bass_rust
from concourse import library_config

RED = bass_rust.ReduceOp

F32 = mybir.dt.float32
BF16 = mybir.dt.bfloat16
I8 = mybir.dt.int8
U32 = mybir.dt.uint32
I32 = mybir.dt.int32
ALU = mybir.AluOpType
AF = mybir.ActivationFunctionType

D = 1024
NCH = D // 128
MAGIC = 12582912.0


class Ph:

    def __init__(self, sems, cnt):
        self.sems = sems
        self.cnt = cnt
        self.base = {k: cnt.get(k, 0) for k in sems}

    def inc(self, inst, name, n=1):
        inst.then_inc(self.sems[name], n)
        self.cnt[name] = self.cnt.get(name, 0) + n
        return self.cnt[name]

    def tok(self, name, k):
        return self.base[name] + k


def build_program(S=4096, want=("s4", "s5")):
    NT = S // 128
    NQS = S // 512
    nc = bass.Bass("TRN2", target_bir_lowering=False, debug=False)

    x_in = nc.dram_tensor("x_in", [S, D], F32, kind="ExternalInput").ap()
    y_out = nc.dram_tensor("y_out", [S, D], F32, kind="ExternalOutput").ap()
    w1qT = nc.dram_tensor("w1qT", [D, D], I8, kind="ExternalInput").ap()
    w1scale = nc.dram_tensor("w1scale", [128, D], F32, kind="ExternalInput").ap()
    w1bias = nc.dram_tensor("w1bias", [128, D], F32, kind="ExternalInput").ap()
    wqkv_l0 = nc.dram_tensor("wqkv_l0", [8, D, 384], F32, kind="ExternalInput").ap()
    wot_l0 = nc.dram_tensor("wot_l0", [D, D], F32, kind="ExternalInput").ap()
    wqkv_l1 = nc.dram_tensor("wqkv_l1", [8, D, 384], F32, kind="ExternalInput").ap()
    wot_l1 = nc.dram_tensor("wot_l1", [D, D], F32, kind="ExternalInput").ap()
    ropec = nc.dram_tensor("ropec", [S, 64], F32, kind="ExternalInput").ap()
    ropes = nc.dram_tensor("ropes", [S, 64], F32, kind="ExternalInput").ap()
    lnT_d = nc.dram_tensor("lnT_d", [NCH, 128, S + 128], BF16, kind="Internal").ap()
    aoT_d = nc.dram_tensor("aoT_d", [D, S], BF16, kind="Internal").ap()
    xsnap_d = nc.dram_tensor("xsnap_d", [S, D], F32, kind="Internal").ap()
    rtq_b = nc.dram_tensor("rtq_b", [128, D], F32, kind="ExternalInput").ap()
    rtkv_b = nc.dram_tensor("rtkv_b", [128, D], F32, kind="ExternalInput").ap()
    lnw_b = nc.dram_tensor("lnw_b", [128, D], F32, kind="ExternalInput").ap()
    lnb_b = nc.dram_tensor("lnb_b", [128, D], F32, kind="ExternalInput").ap()
    gamma_b = nc.dram_tensor("gamma_b", [128, D], F32, kind="ExternalInput").ap()
    nullq_b = nc.dram_tensor("nullq_b", [128, D], F32, kind="ExternalInput").ap()
    wlqkvT = nc.dram_tensor("wlqkvT", [D, 1536], F32, kind="ExternalInput").ap()
    wloT = nc.dram_tensor("wloT", [512, D], F32, kind="ExternalInput").ap()
    whqT = nc.dram_tensor("whqT", [D, 512], F32, kind="ExternalInput").ap()
    whkvT = nc.dram_tensor("whkvT", [D, D], F32, kind="ExternalInput").ap()
    whoT = nc.dram_tensor("whoT", [512, D], F32, kind="ExternalInput").ap()
    nullk_dup = nc.dram_tensor("nullk_dup", [128, 8], F32, kind="ExternalInput").ap()
    nullv1 = nc.dram_tensor("nullv1", [8, 65], F32, kind="ExternalInput").ap()
    heavy_d = nc.dram_tensor("heavy_d", [513, D], F32, kind="Internal").ap()
    dbg_na = nc.dram_tensor("dbg_na", [128, 2], F32, kind="Internal").ap()
    dbg_sel = nc.dram_tensor("dbg_sel", [128, 2, S // 128], F32, kind="Internal").ap()
    dbg_idx = nc.dram_tensor("dbg_idx", [128, 12], U32, kind="Internal").ap()
    dbg_g = nc.dram_tensor("dbg_g", [128, S // 128], U32, kind="Internal").ap()
    dbg_x = nc.dram_tensor("dbg_x", [128, 10], F32, kind="Internal").ap()

    stack = contextlib.ExitStack()
    SEM_NAMES = ["s_in", "s_w", "s_aux", "s_vq", "s_tr", "s_te", "s_mm",
                 "s_me", "s_xd", "s_g", "s_out", "s_v1", "s_s1",
                 "a_vx", "a_tt", "a_se", "a_dw0", "a_dw1", "b_dl0", "b_dl1",
                 "b_gw", "b_mm", "b_qe", "b_vr", "b_tq", "b_fe", "b_rp0", "b_rp1",
                 "c_sc", "c_ex", "c_mk", "c_av", "c_rc", "c_ae", "c_tr", "c_te",
                 "c_dw", "d_dl0", "d_dl1", "d_mm", "d_xd", "x_sn",
                 "p_vx", "p_sv", "p_tt", "p_se", "p_dw0", "p_dw1", "p_cw",
                 "r_v", "r_s", "r_g", "r_t", "r_dw", "g_g", "h_v", "h_s", "h_t",
                 "h_dl", "h_dw", "g_c", "h_w0", "h_w1", "l_dl0", "l_dl1", "l_gw", "l_mm", "l_qe", "l_sc",
                 "l_ex", "l_mk", "l_av", "l_rc", "l_ae", "l_tr", "l_xd", "f_g0", "f_g1", "f_xd"]
    sems = {n: stack.enter_context(nc.semaphore(n)) for n in SEM_NAMES}
    cnt = {}

    xres = stack.enter_context(nc.sbuf_tensor("xres", [128, NT, D], F32))
    ident = stack.enter_context(nc.sbuf_tensor("ident", [128, 128], BF16))
    trimask = stack.enter_context(nc.sbuf_tensor("trimask", [128, 128], BF16))

    ph0 = Ph(sems, cnt)
    with nc.Block() as block:
        @block.sync
        def _(sync):
            for t in range(NT):
                ph0.inc(sync.dma_start(out=xres[:, t, :], in_=x_in[t * 128:(t + 1) * 128, :]),
                        "s_in", 16)

        @block.gpsimd
        def _(g):
            g.memset(ident[:], 0.0)
            g.memset(trimask[:], 1.0)
            g.drain()
            g.affine_select(out=ident[:], in_=ident[:], compare_op=ALU.not_equal,
                            fill=1.0, base=0, pattern=[[-1, 128]], channel_multiplier=1)
            ph0.inc(g.affine_select(out=trimask[:], in_=trimask[:], compare_op=ALU.is_ge,
                                    fill=0.0, base=0, pattern=[[1, 128]], channel_multiplier=-1),
                    "s_aux", 1)
    X_READY = ("s_in", 16 * NT)

    def emit_ln_stats(block, ph, stats, mv, rr, nmr, xwait):
        @block.vector
        def _(v):
            v.wait_ge(sems[xwait[0]], xwait[1])
            for t in range(NT):
                v.bn_stats(stats[:, 12 * t:12 * t + 6], xres[:, t, 0:512])
                v.bn_stats(stats[:, 12 * t + 6:12 * t + 12], xres[:, t, 512:1024])
            v.drain()
            for t in range(NT):
                v.bn_aggr(mv[:, 2 * t:2 * (t + 1)], stats[:, 12 * t:12 * (t + 1)])
            v.drain()
            mvr = mv.ap().rearrange("p (t two) -> p t two", two=2)
            ph.inc(v.tensor_scalar(rr[:], mvr[:, :, 1:2], 1e-5, None, op0=ALU.add), "s_v1", 1)

        @block.scalar
        def _(s):
            s.wait_ge(sems["s_v1"], ph.tok("s_v1", 1))
            s.activation(out=rr[:], in_=rr[:], func=AF.Ln)
            s.drain()
            ph.inc(s.activation(out=rr[:], in_=rr[:], func=AF.Exp, scale=-0.5), "s_s1", 1)

        @block.vector
        def _(v):
            v.wait_ge(sems["s_s1"], ph.tok("s_s1", 1))
            mvr = mv.ap().rearrange("p (t two) -> p t two", two=2)
            v.scalar_tensor_tensor(out=nmr[:], in0=mvr[:, :, 0:1], scalar=-1.0,
                                   in1=rr[:], op0=ALU.mult, op1=ALU.mult)
            ph.inc(v.drain(), "s_v1", 1)

    def emit_l1_ffn(tag, pre):
        ph = Ph(sems, cnt)
        with (
            nc.sbuf_tensor(f"{tag}_bc", [128, 2 * D], F32) as bcast,
            nc.sbuf_tensor(f"{tag}_wq", [128, NCH * D], BF16) as wq,
            nc.sbuf_tensor(f"{tag}_stats", [128, NT * 12], F32) as stats,
            nc.sbuf_tensor(f"{tag}_mv", [128, NT * 2], F32) as mv,
            nc.sbuf_tensor(f"{tag}_r", [128, NT], F32) as rr,
            nc.sbuf_tensor(f"{tag}_nmr", [128, NT], F32) as nmr,
            nc.sbuf_tensor(f"{tag}_xs", [128, NT], F32) as xs,
            nc.sbuf_tensor(f"{tag}_rxs", [128, NT], F32) as rxs,
            nc.sbuf_tensor(f"{tag}_xm", [128, 2 * D], F32) as xm,
            nc.sbuf_tensor(f"{tag}_xqb", [128, 2 * D], BF16) as xqb,
            nc.sbuf_tensor(f"{tag}_xqT", [128, 2 * D], BF16) as xqT,
            nc.sbuf_tensor(f"{tag}_t2", [128, 2 * D], F32) as t2,
            nc.psum_tensor(f"{tag}_ptr", [128, 2 * D], BF16) as ptr,
            nc.psum_tensor(f"{tag}_pmm", [128, 2 * D], F32) as pmm,
            nc.Block() as block,
        ):
            @block.gpsimd
            def _(g):
                for c in range(NCH):
                    ph.inc(g.dma_start(out=wq[:, c * D:(c + 1) * D],
                                       in_=w1qT[c * 128:(c + 1) * 128, :]), "s_g", 16)

            @block.sync
            def _(sy):
                sy.wait_ge(sems[pre[0]], pre[1])
                ph.inc(sy.dma_start(out=bcast[:, 0:D], in_=w1scale[:]), "s_w", 16)
                ph.inc(sy.dma_start(out=bcast[:, D:2 * D], in_=w1bias[:]), "s_w", 16)

            emit_ln_stats(block, ph, stats, mv, rr, nmr, xwait=pre)

            @block.vector
            def _(v):
                v.wait_ge(sems["s_w"], ph.tok("s_w", 32))

                def dequant(j):
                    oj = (j % 2) * D
                    v.wait_ge(sems["s_mm"], ph.tok("s_mm", j + 1))
                    ph.inc(v.tensor_scalar(t2[:, oj:oj + D], pmm[:, oj:oj + D], xs[:, j:j + 1],
                                           0.5, op0=ALU.mult, op1=ALU.mult), "s_me", 1)
                    v.drain()
                    v.tensor_tensor(t2[:, oj:oj + D], t2[:, oj:oj + D], bcast[:, 0:D], op=ALU.mult)
                    v.drain()
                    v.tensor_tensor(xres[:, j, :], xres[:, j, :], t2[:, oj:oj + D], op=ALU.add)
                    v.drain()
                    ph.inc(v.scalar_tensor_tensor(out=xres[:, j, :], in0=bcast[:, D:2 * D],
                                                  scalar=0.5, in1=xres[:, j, :],
                                                  op0=ALU.mult, op1=ALU.add), "s_xd", 1)

                for t in range(NT + 2):
                    if t >= 2:
                        dequant(t - 2)
                    if t >= NT:
                        continue
                    o = (t % 2) * D
                    if t >= 2:
                        v.wait_ge(sems["s_tr"], ph.tok("s_tr", t - 1))
                    v.tensor_scalar(xm[:, o:o + D], xres[:, t, :], rr[:, t:t + 1],
                                    nmr[:, t:t + 1], op0=ALU.mult, op1=ALU.add)
                    v.drain()
                    v.reduce_max(xs[:, t:t + 1], xm[:, o:o + D], axis=mybir.AxisListType.X,
                                 apply_absolute_value=True)
                    v.drain()
                    v.tensor_scalar(xs[:, t:t + 1], xs[:, t:t + 1], 1e-6, 1.0 / 127.0,
                                    op0=ALU.max, op1=ALU.mult)
                    v.drain()
                    v.reciprocal(rxs[:, t:t + 1], xs[:, t:t + 1])
                    v.drain()
                    v.tensor_scalar(xm[:, o:o + D], xm[:, o:o + D], rxs[:, t:t + 1], MAGIC,
                                    op0=ALU.mult, op1=ALU.add)
                    v.drain()
                    ph.inc(v.tensor_scalar(xqb[:, o:o + D], xm[:, o:o + D], -MAGIC, None,
                                           op0=ALU.add), "s_vq", 1)

            @block.tensor
            def _(te):
                te.wait_ge(sems["s_aux"], 1)
                te.wait_ge(sems["s_g"], ph.tok("s_g", 16 * NCH))
                for t in range(NT):
                    o = (t % 2) * D
                    te.wait_ge(sems["s_vq"], ph.tok("s_vq", t + 1))
                    if t >= 2:
                        te.wait_ge(sems["s_te"], ph.tok("s_te", t - 1))
                    for c in range(NCH):
                        i = te.transpose(ptr[:, o + c * 128:o + (c + 1) * 128],
                                         xqb[:, o + c * 128:o + (c + 1) * 128], ident[:])
                    ph.inc(i, "s_tr", 1)
                    te.wait_ge(sems["s_te"], ph.tok("s_te", t + 1))
                    if t >= 2:
                        te.wait_ge(sems["s_me"], ph.tok("s_me", t - 1))
                    for c in range(NCH):
                        for h in range(2):
                            i = te.matmul(pmm[:, o + h * 512:o + (h + 1) * 512],
                                          xqT[:, o + c * 128:o + (c + 1) * 128],
                                          wq[:, c * D + h * 512:c * D + (h + 1) * 512],
                                          start=(c == 0), stop=(c == NCH - 1))
                    ph.inc(i, "s_mm", 1)

            @block.scalar
            def _(s):
                for t in range(NT):
                    o = (t % 2) * D
                    s.wait_ge(sems["s_tr"], ph.tok("s_tr", t + 1))
                    ph.inc(s.copy(xqT[:, o:o + D], ptr[:, o:o + D]), "s_te", 1)
        return ("s_xd", ph.base["s_xd"])

    def emit_dense_attn(tag, pre, wqkv_d, wot_d, rope, snapshot_x):
        ph = Ph(sems, cnt)
        with (
            nc.sbuf_tensor(f"{tag}_stats", [128, NT * 12], F32) as stats,
            nc.sbuf_tensor(f"{tag}_mv", [128, NT * 2], F32) as mv,
            nc.sbuf_tensor(f"{tag}_r", [128, NT], F32) as rr,
            nc.sbuf_tensor(f"{tag}_nmr", [128, NT], F32) as nmr,
        ):
            with (
                nc.sbuf_tensor(f"{tag}_xmb", [128, 2 * D], BF16) as xmb,
                nc.sbuf_tensor(f"{tag}_lnt", [128, 2 * D], BF16) as lnt,
                nc.psum_tensor(f"{tag}_ptA", [128, 2 * D], BF16) as ptA,
                nc.Block() as block,
            ):
                emit_ln_stats(block, ph, stats, mv, rr, nmr, xwait=pre)

                @block.vector
                def _(v):
                    for t in range(NT):
                        o = (t % 2) * D
                        if t >= 2:
                            v.wait_ge(sems["a_tt"], ph.tok("a_tt", t - 1))
                        ph.inc(v.tensor_scalar(xmb[:, o:o + D], xres[:, t, :], rr[:, t:t + 1],
                                               nmr[:, t:t + 1], op0=ALU.mult, op1=ALU.add),
                               "a_vx", 1)

                @block.tensor
                def _(te):
                    te.wait_ge(sems["s_aux"], 1)
                    for t in range(NT):
                        o = (t % 2) * D
                        te.wait_ge(sems["a_vx"], ph.tok("a_vx", t + 1))
                        if t >= 2:
                            te.wait_ge(sems["a_se"], ph.tok("a_se", t - 1))
                        for c in range(NCH):
                            i = te.transpose(ptA[:, o + c * 128:o + (c + 1) * 128],
                                             xmb[:, o + c * 128:o + (c + 1) * 128], ident[:])
                        ph.inc(i, "a_tt", 1)

                @block.scalar
                def _(s):
                    for t in range(NT):
                        o = (t % 2) * D
                        s.wait_ge(sems["a_tt"], ph.tok("a_tt", t + 1))
                        if t >= 2:
                            sd = "a_dw0" if t % 2 == 0 else "a_dw1"
                            s.wait_ge(sems[sd], ph.tok(sd, 16 * (t // 2)))
                        ph.inc(s.copy(lnt[:, o:o + D], ptA[:, o:o + D]), "a_se", 1)

                @block.sync
                def _(sy):
                    for t in range(NT):
                        o = (t % 2) * D
                        sy.wait_ge(sems["a_se"], ph.tok("a_se", t + 1))
                        dst = lnT_d[:, :, 64 + t * 128:64 + (t + 1) * 128].rearrange("c p j -> p c j")
                        srcv = lnt[:, o:o + D].rearrange("p (c j) -> p c j", j=128)
                        ph.inc(sy.dma_start(out=dst, in_=srcv),
                               "a_dw0" if t % 2 == 0 else "a_dw1", 16)

            with contextlib.ExitStack() as bstk:
                qfm = bstk.enter_context(nc.sbuf_tensor(f"{tag}_qfm", [128, S], BF16))
                kfm = bstk.enter_context(nc.sbuf_tensor(f"{tag}_kfm", [128, S], BF16))
                vt = bstk.enter_context(nc.sbuf_tensor(f"{tag}_vt", [128, NT, 2, 65], BF16))
                lntb = bstk.enter_context(nc.sbuf_tensor(f"{tag}_lntb", [128, 2, NCH, 128], BF16))
                wsb = bstk.enter_context(nc.sbuf_tensor(f"{tag}_wsb", [128, NCH, 384], BF16))
                qk32 = bstk.enter_context(nc.sbuf_tensor(f"{tag}_qk32", [128, 2, 256], F32))
                rv1 = bstk.enter_context(nc.sbuf_tensor(f"{tag}_rv1", [128, 256], F32))
                rv2 = bstk.enter_context(nc.sbuf_tensor(f"{tag}_rv2", [128, 256], F32))
                qkb = bstk.enter_context(nc.sbuf_tensor(f"{tag}_qkb", [128, 2, 256], BF16))
                rp = bstk.enter_context(nc.sbuf_tensor(f"{tag}_rp", [128, 2, 2, 64], F32))
                at = bstk.enter_context(nc.sbuf_tensor(f"{tag}_at", [128, 2, 2, 512], BF16))
                rs = bstk.enter_context(nc.sbuf_tensor(f"{tag}_rs", [128, 8], F32))
                aoq = bstk.enter_context(nc.sbuf_tensor(f"{tag}_aoq", [128, 4, 128], BF16))
                aot = bstk.enter_context(nc.sbuf_tensor(f"{tag}_aot", [128, 512], BF16))
                ptB = bstk.enter_context(nc.psum_tensor(f"{tag}_ptB", [128, 2, 1024], BF16))
                psS = bstk.enter_context(nc.psum_tensor(f"{tag}_psS", [128, 2, 2, 512], F32))
                psAV = bstk.enter_context(nc.psum_tensor(f"{tag}_psAV", [128, 2, 512], F32))
                ph_i = Ph(sems, cnt)
                with nc.Block() as block:
                    @block.gpsimd
                    def _(g_):
                        g_.wait_ge(sems["a_dw0"], ph_i.base["a_dw0"])
                        g_.wait_ge(sems["a_dw1"], ph_i.base["a_dw1"])
                        ph_i.inc(g_.memset(vt[:, :, :, 64:65], 1.0), "s_aux", 1)
                ones_tok = cnt["s_aux"]

                NKC = [4 * q + 4 for q in range(NQS)]
                for g in range(8):
                    ph_g = Ph(sems, cnt)
                    with nc.Block() as block:
                        @block.gpsimd
                        def _(g_):
                            g_.wait_ge(sems["b_mm"], ph_g.base["b_mm"])
                            ph_g.inc(g_.dma_start(out=wsb[:, :, :],
                                                  in_=wqkv_d[g].rearrange("(c p) n -> p c n", p=128)),
                                     "b_gw", 16)

                        @block.sync
                        def _(sy):
                            if g == 0:
                                sy.wait_ge(sems["a_dw0"], ph.tok("a_dw0", 16 * (NT // 2)))
                                sy.wait_ge(sems["a_dw1"], ph.tok("a_dw1", 16 * (NT // 2)))
                            for t in range(NT):
                                tb = t % 2
                                sl = "b_dl0" if tb == 0 else "b_dl1"
                                sr = "b_rp0" if tb == 0 else "b_rp1"
                                if t >= 2:
                                    sy.wait_ge(sems["b_mm"], ph_g.tok("b_mm", t - 1))
                                    if rope:
                                        sy.wait_ge(sems["b_vr"], ph_g.tok("b_vr", t - 1))
                                src = lnT_d[:, :, 64 + t * 128:64 + (t + 1) * 128].rearrange("c p j -> p c j")
                                ph_g.inc(sy.dma_start(out=lntb[:, tb, :, :], in_=src), sl, 16)
                                if rope:
                                    ph_g.inc(sy.dma_start(out=rp[:, tb, 0, :],
                                                          in_=ropec[t * 128:(t + 1) * 128, :]),
                                             sr, 16)
                                    ph_g.inc(sy.dma_start(out=rp[:, tb, 1, :],
                                                          in_=ropes[t * 128:(t + 1) * 128, :]),
                                             sr, 16)

                        @block.tensor
                        def _(te):
                            te.wait_ge(sems["b_gw"], ph_g.tok("b_gw", 16))
                            te.wait_ge(sems["s_aux"], ones_tok)
                            for t in range(NT):
                                tb = t % 2
                                sl = "b_dl0" if tb == 0 else "b_dl1"
                                te.wait_ge(sems[sl], ph_g.tok(sl, 16 * (t // 2 + 1)))
                                if t >= 2:
                                    te.wait_ge(sems["b_qe"], ph_g.tok("b_qe", 2 * (t - 1)))
                                for c in range(NCH):
                                    i = te.matmul(psS[:, tb, 0, 0:384], lntb[:, tb, c, :],
                                                  wsb[:, c, :], start=(c == 0), stop=(c == NCH - 1))
                                ph_g.inc(i, "b_mm", 1)
                                if rope:
                                    te.wait_ge(sems["b_vr"], ph_g.tok("b_vr", t + 1))
                                else:
                                    te.wait_ge(sems["b_qe"], ph_g.tok("b_qe", 2 * t + 1))
                                if t >= 2:
                                    te.wait_ge(sems["b_fe"], ph_g.tok("b_fe", t - 1))
                                i = te.transpose(ptB[:, tb, 0:128], qkb[:, tb, 0:128], ident[:])
                                i = te.transpose(ptB[:, tb, 128:256], qkb[:, tb, 128:256], ident[:])
                                ph_g.inc(i, "b_tq", 1)

                        @block.scalar
                        def _(s):
                            if g == 0:
                                s.wait_ge(sems["a_dw0"], ph_g.base["a_dw0"])
                                s.wait_ge(sems["a_dw1"], ph_g.base["a_dw1"])
                            if g > 0:
                                s.wait_ge(sems["c_av"], ph_g.base["c_av"])
                            for t in range(NT):
                                tb = t % 2
                                s.wait_ge(sems["b_mm"], ph_g.tok("b_mm", t + 1))
                                if rope:
                                    if t >= 2:
                                        s.wait_ge(sems["b_vr"], ph_g.tok("b_vr", t - 1))
                                    i = s.copy(qk32[:, tb, :], psS[:, tb, 0, 0:256])
                                else:
                                    if t >= 2:
                                        s.wait_ge(sems["b_tq"], ph_g.tok("b_tq", t - 1))
                                    i = s.copy(qkb[:, tb, :], psS[:, tb, 0, 0:256])
                                ph_g.inc(i, "b_qe", 1)
                                ph_g.inc(s.copy(vt[:, t, :, 0:64], psS[:, tb, 0, 256:384]),
                                         "b_qe", 1)
                                s.wait_ge(sems["b_tq"], ph_g.tok("b_tq", t + 1))
                                i = s.copy(qfm[:, t * 128:(t + 1) * 128], ptB[:, tb, 0:128])
                                i = s.copy(kfm[:, t * 128:(t + 1) * 128], ptB[:, tb, 128:256])
                                ph_g.inc(i, "b_fe", 1)

                        if rope:
                            @block.vector
                            def _(v):
                                for t in range(NT):
                                    tb = t % 2
                                    v.wait_ge(sems["b_qe"], ph_g.tok("b_qe", 2 * t + 1))
                                    sr = "b_rp0" if tb == 0 else "b_rp1"
                                    v.wait_ge(sems[sr], ph_g.tok(sr, 32 * (t // 2 + 1)))
                                    if t >= 2:
                                        v.wait_ge(sems["b_tq"], ph_g.tok("b_tq", t - 1))
                                    qk3 = qk32[:, tb, :].rearrange("p (a n) -> p a n", n=64)
                                    cosb = rp[:, tb, 0:1, :].to_broadcast([128, 4, 64])
                                    sinb = rp[:, tb, 1:2, :].to_broadcast([128, 4, 64])
                                    v.tensor_tensor(rv1.ap().rearrange("p (a n) -> p a n", n=64),
                                                    qk3, cosb, op=ALU.mult)
                                    v.tensor_tensor(rv2.ap().rearrange("p (a n) -> p a n", n=64),
                                                    qk3, sinb, op=ALU.mult)
                                    v.drain()
                                    r1 = rv1.ap().rearrange("p (a h n) -> p a h n", h=2, n=32)
                                    r2 = rv2.ap().rearrange("p (a h n) -> p a h n", h=2, n=32)
                                    ob = qkb[:, tb, :].rearrange("p (a h n) -> p a h n", h=2, n=32)
                                    v.tensor_tensor(ob[:, :, 0:1, :], r1[:, :, 0:1, :],
                                                    r2[:, :, 1:2, :], op=ALU.subtract)
                                    i = v.tensor_tensor(ob[:, :, 1:2, :], r1[:, :, 1:2, :],
                                                        r2[:, :, 0:1, :], op=ALU.add)
                                    ph_g.inc(i, "b_vr", 1)

                    ph_a = Ph(sems, cnt)
                    n_mask = {}
                    cm = 0
                    idx0 = 0
                    av_at_qsec = {}
                    for q in range(NQS):
                        for kc in range(NKC[q]):
                            if kc >= 4 * q:
                                cm += 2
                            n_mask[idx0] = cm
                            idx0 += 1
                        av_at_qsec[q] = idx0

                    with nc.Block() as block:
                        @block.tensor
                        def _(te):
                            te.wait_ge(sems["b_fe"], ph_g.tok("b_fe", NT))
                            idx = 0
                            for q in range(NQS):
                                for kc in range(NKC[q]):
                                    pp = idx % 2
                                    j0 = max(0, kc - 4 * q)
                                    qo = j0 * 128
                                    if idx >= 2:
                                        te.wait_ge(sems["c_ex"], ph_a.tok("c_ex", 2 * (idx - 1)))
                                    i = te.matmul(psS[:, pp, 0, qo:512],
                                                  kfm[0:64, kc * 128:(kc + 1) * 128],
                                                  qfm[0:64, q * 512 + qo:(q + 1) * 512],
                                                  start=True, stop=True)
                                    i = te.matmul(psS[:, pp, 1, qo:512],
                                                  kfm[64:128, kc * 128:(kc + 1) * 128],
                                                  qfm[64:128, q * 512 + qo:(q + 1) * 512],
                                                  start=True, stop=True, tile_position=(64, 0))
                                    ph_a.inc(i, "c_sc", 2)
                                    te.wait_ge(sems["c_ex"], ph_a.tok("c_ex", 2 * (idx + 1)))
                                    if kc >= 4 * q:
                                        te.wait_ge(sems["c_mk"], ph_a.tok("c_mk", n_mask[idx]))
                                    for h in range(2):
                                        for j in range(j0, 4):
                                            i = te.matmul(psAV[:, h, j * 65:(j + 1) * 65],
                                                          at[:, pp, h, j * 128:(j + 1) * 128],
                                                          vt[:, kc, h, :],
                                                          start=(kc == 0 and j == 0),
                                                          stop=(kc == NKC[q] - 1 and j == 3),
                                                          skip_group_check=True)
                                    ph_a.inc(i, "c_av", 1)
                                    idx += 1
                                te.wait_ge(sems["c_ae"], ph_a.tok("c_ae", q + 1))
                                for j in range(4):
                                    i = te.transpose(ptB[:, 1, j * 128:(j + 1) * 128],
                                                     aoq[:, j, :], ident[:])
                                ph_a.inc(i, "c_tr", 1)

                        @block.scalar
                        def _(s):
                            idx = 0
                            for q in range(NQS):
                                for kc in range(NKC[q]):
                                    pp = idx % 2
                                    j0 = max(0, kc - 4 * q)
                                    qo = j0 * 128
                                    s.wait_ge(sems["c_sc"], ph_a.tok("c_sc", 2 * (idx + 1)))
                                    s.wait_ge(sems["c_av"], ph_a.tok("c_av", max(0, idx - 1)))
                                    i = s.activation(out=at[:, pp, 0, qo:512],
                                                     in_=psS[:, pp, 0, qo:512],
                                                     func=AF.Exp, scale=0.125)
                                    i = s.activation(out=at[:, pp, 1, qo:512],
                                                     in_=psS[:, pp, 1, qo:512],
                                                     func=AF.Exp, scale=0.125)
                                    ph_a.inc(i, "c_ex", 2)
                                    idx += 1
                                s.wait_ge(sems["c_rc"], ph_a.tok("c_rc", q + 1))
                                if q >= 1:
                                    s.wait_ge(sems["c_tr"], ph_a.tok("c_tr", q))
                                for h in range(2):
                                    for j in range(4):
                                        i = s.activation(out=aoq[:, j, h * 64:(h + 1) * 64],
                                                         in_=psAV[:, h, j * 65:j * 65 + 64],
                                                         func=AF.Identity,
                                                         scale=rs[:, h * 4 + j:h * 4 + j + 1])
                                ph_a.inc(i, "c_ae", 1)
                                s.wait_ge(sems["c_tr"], ph_a.tok("c_tr", q + 1))
                                if ph_a.tok("c_dw", 16 * q) > 0:
                                    s.wait_ge(sems["c_dw"], ph_a.tok("c_dw", 16 * q))
                                ph_a.inc(s.copy(aot[:], ptB[:, 1, 0:512]), "c_te", 1)

                        @block.vector
                        def _(v):
                            idx = 0
                            for q in range(NQS):
                                for kc in range(NKC[q]):
                                    pp = idx % 2
                                    if kc >= 4 * q:
                                        j = kc - 4 * q
                                        v.wait_ge(sems["c_ex"], ph_a.tok("c_ex", 2 * (idx + 1)))
                                        v.tensor_tensor(at[:, pp, 0, j * 128:(j + 1) * 128],
                                                        at[:, pp, 0, j * 128:(j + 1) * 128],
                                                        trimask[:], op=ALU.mult)
                                        i = v.tensor_tensor(at[:, pp, 1, j * 128:(j + 1) * 128],
                                                            at[:, pp, 1, j * 128:(j + 1) * 128],
                                                            trimask[:], op=ALU.mult)
                                        ph_a.inc(i, "c_mk", 2)
                                    idx += 1
                                v.wait_ge(sems["c_av"], ph_a.tok("c_av", av_at_qsec[q]))
                                if q >= 1:
                                    v.wait_ge(sems["c_ae"], ph_a.tok("c_ae", q))
                                i = v.reciprocal(
                                    rs.ap().rearrange("p (h j) -> p h j", j=4),
                                    psAV[:, :, 0:260].rearrange("p h (j n) -> p h j n", n=65)[:, :, :, 64:65])
                                ph_a.inc(i, "c_rc", 1)

                        @block.sync
                        def _(sy):
                            for q in range(NQS):
                                sy.wait_ge(sems["c_te"], ph_a.tok("c_te", q + 1))
                                ph_a.inc(sy.dma_start(out=aoT_d[g * 128:(g + 1) * 128,
                                                               q * 512:(q + 1) * 512],
                                                      in_=aot[:]), "c_dw", 16)

            ph_c = Ph(sems, cnt)
            with (
                nc.sbuf_tensor(f"{tag}_wot", [128, NCH, D], BF16) as wotb,
                nc.sbuf_tensor(f"{tag}_aotb", [128, 2, NCH, 128], BF16) as aotb,
                nc.psum_tensor(f"{tag}_psO", [128, 2, D], F32) as psO,
                nc.Block() as block,
            ):
                @block.gpsimd
                def _(g_):
                    g_.wait_ge(sems["c_av"], ph_c.base["c_av"])
                    ph_c.inc(g_.dma_start(out=wotb[:, :, :],
                                          in_=wot_d.rearrange("(c p) n -> p c n", p=128)),
                             "b_gw", 16)

                @block.sync
                def _(sy):
                    sy.wait_ge(sems["c_dw"], ph_c.base["c_dw"])
                    for t in range(NT):
                        tb = t % 2
                        sl = "d_dl0" if tb == 0 else "d_dl1"
                        if t >= 2:
                            sy.wait_ge(sems["d_mm"], ph_c.tok("d_mm", t - 1))
                        src = aoT_d[:, t * 128:(t + 1) * 128].rearrange("(c p) j -> p c j", p=128)
                        ph_c.inc(sy.dma_start(out=aotb[:, tb, :, :], in_=src), sl, 16)

                @block.tensor
                def _(te):
                    te.wait_ge(sems["b_gw"], ph_c.tok("b_gw", 16))
                    for t in range(NT):
                        tb = t % 2
                        sl = "d_dl0" if tb == 0 else "d_dl1"
                        te.wait_ge(sems[sl], ph_c.tok(sl, 16 * (t // 2 + 1)))
                        if t >= 2:
                            te.wait_ge(sems["d_xd"], ph_c.tok("d_xd", t - 1))
                        for c in range(NCH):
                            te.matmul(psO[:, tb, 0:512], aotb[:, tb, c, :], wotb[:, c, 0:512],
                                      start=(c == 0), stop=(c == NCH - 1))
                            i = te.matmul(psO[:, tb, 512:1024], aotb[:, tb, c, :],
                                          wotb[:, c, 512:1024],
                                          start=(c == 0), stop=(c == NCH - 1))
                        ph_c.inc(i, "d_mm", 1)

                @block.vector
                def _(v):
                    for t in range(NT):
                        tb = t % 2
                        v.wait_ge(sems["d_mm"], ph_c.tok("d_mm", t + 1))
                        ph_c.inc(v.tensor_tensor(xres[:, t, :], xres[:, t, :],
                                                 psO[:, tb, :], op=ALU.add), "d_xd", 1)

                @block.sync
                def _(sy):
                    if snapshot_x:
                        for t in range(NT):
                            sy.wait_ge(sems["d_xd"], ph_c.tok("d_xd", t + 1))
                            ph_c.inc(sy.dma_start(out=xsnap_d[t * 128:(t + 1) * 128, :],
                                                  in_=xres[:, t, :]), "x_sn", 16)

        return ("d_xd", ph_c.base["d_xd"])


    NKV = 1024
    NQ = 512

    def emit_l0_ffn(tag, pre, xsrc, xwait2=None):
        import math
        n_f = np.float32(S)
        logk_q = np.float32(np.log(np.float32(min(NQ * 1.125, float(S)))))
        logk_kv = np.float32(np.log(np.float32(min(NKV * 1.125, float(S)))))
        log_n = np.float32(np.log(n_f))
        na_init = {0: float(np.float32(log_n - logk_q)), 1: float(np.float32(log_n - logk_kv))}
        logk = {0: float(logk_q), 1: float(logk_kv)}
        kk = {0: float(NQ), 1: float(NKV)}
        RSUB = {0: NQ // 128, 1: NKV // 128}

        ph = Ph(sems, cnt)
        with contextlib.ExitStack() as pstk:
            sv = pstk.enter_context(nc.sbuf_tensor(f"{tag}_sv", [128, 2, NT], F32))
            gIv = pstk.enter_context(nc.sbuf_tensor(f"{tag}_gI", [128, NT], U32))
            idxu = pstk.enter_context(nc.sbuf_tensor(f"{tag}_idxu", [128, 12], U32))

            with (
                nc.sbuf_tensor(f"{tag}_stats", [128, NT * 12], F32) as stats,
                nc.sbuf_tensor(f"{tag}_mv", [128, NT * 2], F32) as mv,
                nc.sbuf_tensor(f"{tag}_r", [128, NT], F32) as rr,
                nc.sbuf_tensor(f"{tag}_nmr", [128, NT], F32) as nmr,
                nc.sbuf_tensor(f"{tag}_cst", [128, 4 * D], F32) as cst,
                nc.sbuf_tensor(f"{tag}_xmf", [128, 2 * D], F32) as xmf,
                nc.sbuf_tensor(f"{tag}_sd", [128, D], F32) as sdump,
                nc.sbuf_tensor(f"{tag}_l2f", [128, D], F32) as l2f,
                nc.sbuf_tensor(f"{tag}_xmb", [128, 2 * D], BF16) as xmb,
                nc.sbuf_tensor(f"{tag}_lnt", [128, 2 * D], BF16) as lnt,
                nc.sbuf_tensor(f"{tag}_zpad", [128, NCH * 64], BF16) as zpad,
                nc.psum_tensor(f"{tag}_ptA", [128, 2 * D], BF16) as ptA,
                nc.Block() as block,
            ):
                emit_ln_stats(block, ph, stats, mv, rr, nmr, xwait=pre)

                @block.sync
                def _(sy):
                    sy.wait_ge(sems[pre[0]], pre[1])
                    ph.inc(sy.dma_start(out=cst[:, 0:D], in_=rtq_b[:]), "s_w", 16)
                    ph.inc(sy.dma_start(out=cst[:, D:2 * D], in_=rtkv_b[:]), "s_w", 16)
                    ph.inc(sy.dma_start(out=cst[:, 2 * D:3 * D], in_=lnw_b[:]), "s_w", 16)
                    ph.inc(sy.dma_start(out=cst[:, 3 * D:4 * D], in_=lnb_b[:]), "s_w", 16)

                if _os.environ.get("L0P1") != "nozpad":
                    @block.gpsimd
                    def _(g_):
                        g_.wait_ge(sems[pre[0]], pre[1])
                        ph.inc(g_.memset(zpad[:], 0.0), "g_c", 1)

                    @block.sync
                    def _(sy):
                        sy.wait_ge(sems["g_c"], ph.tok("g_c", 1))
                        ph.inc(sy.dma_start(out=lnT_d[:, :, 0:64].rearrange("c p j -> p c j"),
                                            in_=zpad.ap().rearrange("p (c j) -> p c j", j=64)),
                               "x_sn", 16)
                        ph.inc(sy.dma_start(
                            out=lnT_d[:, :, S + 64:S + 128].rearrange("c p j -> p c j"),
                            in_=zpad.ap().rearrange("p (c j) -> p c j", j=64)), "x_sn", 16)

                @block.vector
                def _(v):
                    v.wait_ge(sems["s_w"], ph.tok("s_w", 64))
                    for t in range(NT):
                        o = (t % 2) * D
                        if t >= 2:
                            v.wait_ge(sems["p_tt"], ph.tok("p_tt", t - 1))
                        v.tensor_scalar(xmf[:, o:o + D], xres[:, t, :], rr[:, t:t + 1],
                                        nmr[:, t:t + 1], op0=ALU.mult, op1=ALU.add)
                        v.drain()
                        if _os.environ.get("L0P1") != "nos":
                            v.tensor_tensor(sdump[:], xmf[:, o:o + D], cst[:, 0:D], op=ALU.mult)
                            v.drain()
                            v.tensor_reduce(sv[:, 0, t:t + 1], sdump[:],
                                            axis=mybir.AxisListType.X, op=ALU.add)
                            v.drain()
                            v.tensor_tensor(sdump[:], xmf[:, o:o + D], cst[:, D:2 * D],
                                            op=ALU.mult)
                            v.drain()
                            v.tensor_reduce(sv[:, 1, t:t + 1], sdump[:],
                                            axis=mybir.AxisListType.X, op=ALU.add)
                        ph.inc(v.drain(), "p_sv", 1)
                        v.tensor_tensor(l2f[:], xmf[:, o:o + D], cst[:, 2 * D:3 * D], op=ALU.mult)
                        v.drain()
                        ph.inc(v.tensor_tensor(xmb[:, o:o + D], l2f[:], cst[:, 3 * D:4 * D],
                                               op=ALU.add), "p_vx", 1)

                @block.tensor
                def _(te):
                    te.wait_ge(sems["s_aux"], 1)
                    for t in range(NT):
                        o = (t % 2) * D
                        te.wait_ge(sems["p_vx"], ph.tok("p_vx", t + 1))
                        if t >= 2:
                            te.wait_ge(sems["p_se"], ph.tok("p_se", t - 1))
                        for c in range(NCH):
                            i = te.transpose(ptA[:, o + c * 128:o + (c + 1) * 128],
                                             xmb[:, o + c * 128:o + (c + 1) * 128], ident[:])
                        ph.inc(i, "p_tt", 1)

                @block.scalar
                def _(s):
                    for t in range(NT):
                        o = (t % 2) * D
                        s.wait_ge(sems["p_tt"], ph.tok("p_tt", t + 1))
                        if t >= 2:
                            sd = "p_dw0" if t % 2 == 0 else "p_dw1"
                            s.wait_ge(sems[sd], ph.tok(sd, 16 * (t // 2)))
                        ph.inc(s.copy(lnt[:, o:o + D], ptA[:, o:o + D]), "p_se", 1)

                @block.sync
                def _(sy):
                    for t in range(NT):
                        o = (t % 2) * D
                        sy.wait_ge(sems["p_se"], ph.tok("p_se", t + 1))
                        if _os.environ.get("L0P1") == "nodma":
                            dst = xsnap_d[t * 128:(t + 1) * 128, 0:512]
                            srcv = lnt[:, o:o + 1024].rearrange("p (c j) -> p (c j)", j=128)[:, 0:512]
                            ph.inc(sy.dma_start(out=xsnap_d[t * 128:(t + 1) * 128, 0:256]
                                                .bitcast(BF16) if False else
                                                lnT_d[0, :, 64 + t * 128:64 + (t + 1) * 128],
                                                in_=lnt[:, o:o + 128]),
                                   "p_dw0" if t % 2 == 0 else "p_dw1", 16)
                        else:
                            dst = lnT_d[:, :, 64 + t * 128:64 + (t + 1) * 128].rearrange("c p j -> p c j")
                            srcv = lnt[:, o:o + D].rearrange("p (c j) -> p c j", j=128)
                            ph.inc(sy.dma_start(out=dst, in_=srcv),
                                   "p_dw0" if t % 2 == 0 else "p_dw1", 16)

            if _os.environ.get("L0CUT") == "p1":
                return ("p_se", ph.tok("p_se", 0))
            ph2 = Ph(sems, cnt)
            with contextlib.ExitStack() as rstk:
                def sb(name, shape, dt=F32):
                    return rstk.enter_context(nc.sbuf_tensor(f"{tag}_{name}", shape, dt))
                na = sb("na", [128, 2]); ms = sb("ms", [128, 2]); sh = sb("sh", [128, 2])
                red = sb("red", [128, 2]); lred = sb("lred", [128, 2])
                tbuf = sb("tbuf", [128, 2, NT]); ebuf = sb("ebuf", [128, 2, NT])
                stil = sb("stil", [128, 2, NT])
                lo = sb("lo", [128, 2]); hi = sb("hi", [128, 2]); vv = sb("vv", [128, 2])
                cdump = sb("cd", [128, NT]); cntb = sb("cnt", [128, 2])
                cond8 = sb("c8", [128, 2], mybir.dt.uint8)
                ncond8 = sb("nc8", [128, 2], mybir.dt.uint8)
                vstar = sb("vst", [128, 2]); nhi = sb("nhi", [128, 2])
                mhi = sb("mhi", [128, 2, NT]); meq = sb("meq", [128, 2, NT])
                m8 = sb("m8", [128, 2, NT], mybir.dt.uint8)
                neginf = sb("ninf", [128, 1])
                selb = sb("sel", [128, 2, NT]); rank = sb("rank", [128, 2, NT])
                tot = sb("tot", [1, 2, NT]); totc = sb("totc", [1, 2, NT])
                offs = sb("offs", [128, 2, NT])
                tlow = sb("tlow", [128, 128]); onec = sb("onec", [128, 1])
                onesr = sb("onesr", [1, 128]); idf32 = sb("idf32", [128, 128])
                st0 = sb("st0", [1, 16]); tr2 = sb("tr2", [2, 128]); m2 = sb("m2", [2, 1])
                iotaR = sb("ioR", [128, 128]); iotaC = sb("ioC", [128, 1])
                iotaI = sb("ioI", [128, 128], I32)
                mr = sb("mr", [128, NT]); mrs = sb("mrs", [128, NT])
                fval = sb("fval", [128, NT]); fvalI = sb("fvalI", [128, NT], I32)
                cba = sb("cba", [128, NT * 128])
                gf = sb("gf", [128, NT]); idxf = sb("idxf", [128, 12])
                c512 = sb("c512", [128, 1])
                pfx = rstk.enter_context(nc.psum_tensor(f"{tag}_pfx", [128, 2, NT], F32))
                ptot = rstk.enter_context(nc.psum_tensor(f"{tag}_ptot", [128, 2, NT], F32))
                pidx = rstk.enter_context(nc.psum_tensor(f"{tag}_pidx", [128, 16], F32))
                pAR = rstk.enter_context(nc.psum_tensor(f"{tag}_pAR", [128, 512], F32))

                NIT = 49
                NBI = 48

                steps = []
                sc = dict(r_v=cnt.get("r_v", 0), r_s=cnt.get("r_s", 0),
                          r_g=cnt.get("r_g", 0), r_t=cnt.get("r_t", 0))

                def step(eng, waits, emit, inc_sem):
                    sc[inc_sem] += 1
                    steps.append((eng, list(waits), emit, inc_sem, sc[inc_sem]))
                    return sc[inc_sem]

                X = mybir.AxisListType.X

                def E(f):
                    return f

                def em_const(g_):
                    g_.memset(tlow[:], 1.0); g_.memset(onec[:], 1.0)
                    g_.memset(onesr[:], 1.0); g_.memset(idf32[:], 0.0)
                    g_.memset(neginf[:], -3.0e38); g_.memset(c512[:], float(NQ))
                    g_.drain()
                    g_.affine_select(out=idf32[:], in_=idf32[:], compare_op=ALU.not_equal,
                                     fill=1.0, base=0, pattern=[[-1, 128]], channel_multiplier=1)
                    g_.affine_select(out=tlow[:], in_=tlow[:], compare_op=ALU.is_ge, fill=0.0,
                                     base=0, pattern=[[1, 128]], channel_multiplier=-1)
                    g_.iota(iotaI[:], pattern=[[1, 128]], base=1, channel_multiplier=0)
                    g_.drain()
                    g_.tensor_copy(iotaR[:], iotaI[:])
                    g_.drain()
                    g_.iota(iotaI[:, 0:1], pattern=[[0, 1]], base=0, channel_multiplier=1)
                    g_.iota(fvalI[:], pattern=[[128, NT]], base=0, channel_multiplier=0)
                    g_.drain()
                    g_.tensor_copy(iotaC[:], iotaI[:, 0:1])
                    g_.tensor_copy(fval[:], fvalI[:])
                    return g_.drain()
                pdw = [("p_dw0", ph.tok("p_dw0", 16 * (NT // 2))), ("p_dw1", ph.tok("p_dw1", 16 * (NT // 2)))]
                t_const = step("g", [("p_sv", ph.tok("p_sv", NT))] + pdw, em_const, "r_g")

                def em_ms(v):
                    v.tensor_reduce(ms[:], sv[:, :, :], axis=X, op=ALU.max)
                    v.memset(na[:, 0:1], na_init[0])
                    v.memset(na[:, 1:2], na_init[1])
                    return v.drain()
                t_msv = step("v", [("p_sv", ph.tok("p_sv", NT))] + pdw, em_ms, "r_v")
                def em_msT(te):
                    return te.transpose(pAR[0:2, 0:128], ms[:], idf32[:])
                t_mst = step("t", [("r_v", t_msv), ("r_g", t_const)], em_msT, "r_t")
                def em_msE(v):
                    v.tensor_copy(tr2[:], pAR[0:2, 0:128])
                    v.drain()
                    v.tensor_reduce(m2[:], tr2[:], axis=X, op=ALU.max)
                    return v.drain()
                t_mse = step("v", [("r_t", t_mst)], em_msE, "r_v")
                def em_msT2(te):
                    return te.transpose(pAR[0:1, 128:130], m2[:], idf32[0:2, 0:2])
                t_mst2 = step("t", [("r_v", t_mse)], em_msT2, "r_t")
                def em_msE2(v):
                    v.tensor_copy(st0[0:1, 4:6], pAR[0:1, 128:130])
                    return v.drain()
                t_mse2 = step("v", [("r_t", t_mst2)], em_msE2, "r_v")
                def em_msB(te):
                    return te.matmul(pAR[:, 256:258], onesr[:], st0[0:1, 4:6],
                                     start=True, stop=True)
                t_msb = step("t", [("r_v", t_mse2)], em_msB, "r_t")
                def em_msE3(v):
                    v.tensor_copy(ms[:], pAR[:, 256:258])
                    return v.drain()
                t_msg = step("v", [("r_t", t_msb)], em_msE3, "r_v")

                for it in range(NIT):
                    def em_tb(v, it=it):
                        v.tensor_tensor(sh[:], ms[:], na[:], op=ALU.min)
                        v.drain()
                        v.tensor_copy(st0[0:1, 2:4], sh[0:1, :])
                        v.drain()
                        for r in range(2):
                            v.tensor_scalar(tbuf[:, r, :], sv[:, r, :], na[:, r:r + 1],
                                            sh[:, r:r + 1], op0=ALU.min, op1=ALU.subtract)
                        return v.drain()
                    t_tb = step("v", [("r_v", t_msg)] if it == 0 else [], em_tb, "r_v")
                    t_ex = step("s", [("r_v", t_tb)],
                                E(lambda s: s.activation(
                                    out=ebuf.ap().rearrange("p a b -> p (a b)"),
                                    in_=tbuf.ap().rearrange("p a b -> p (a b)"), func=AF.Exp)),
                                "r_s")
                    t_rd = step("v", [("r_s", t_ex)],
                                E(lambda v: v.tensor_reduce(red[:], ebuf[:, :, :], axis=X,
                                                            op=ALU.add)), "r_v")
                    t_ar = step("t", [("r_v", t_rd)],
                                E(lambda te: te.matmul(pAR[0:1, 0:2], onec[:], red[:],
                                                       start=True, stop=True)), "r_t")
                    def em_lnp(s):
                        s.copy(st0[0:1, 0:2], pAR[0:1, 0:2])
                        s.drain()
                        return s.activation(out=st0[0:1, 0:2], in_=st0[0:1, 0:2], func=AF.Ln)
                    t_ln = step("s", [("r_t", t_ar)], em_lnp, "r_s")
                    def em_na0(v):
                        v.scalar_tensor_tensor(out=st0[0:1, 6:7], in0=st0[0:1, 2:3],
                                               scalar=-logk[0], in1=st0[0:1, 0:1],
                                               op0=ALU.add, op1=ALU.add)
                        v.scalar_tensor_tensor(out=st0[0:1, 7:8], in0=st0[0:1, 3:4],
                                               scalar=-logk[1], in1=st0[0:1, 1:2],
                                               op0=ALU.add, op1=ALU.add)
                        return v.drain()
                    t_na0 = step("v", [("r_s", t_ln)], em_na0, "r_v")
                    t_nab = step("t", [("r_v", t_na0)],
                                 E(lambda te: te.matmul(pAR[:, 4:6], onesr[:], st0[0:1, 6:8],
                                                        start=True, stop=True)), "r_t")
                    def em_nae(v):
                        v.tensor_copy(na[:], pAR[:, 4:6])
                        return v.drain()
                    t_msg = step("v", [("r_t", t_nab)], em_nae, "r_v")

                CUT2 = _os.environ.get("L0CUT")
                def em_si(v):
                    for r in range(2):
                        v.tensor_scalar(stil[:, r, :], sv[:, r, :], na[:, r:r + 1], None,
                                        op0=ALU.min)
                    v.drain()
                    v.tensor_scalar(lo[:], ms[:], -1.0, -1.0, op0=ALU.mult, op1=ALU.add)
                    v.tensor_scalar(hi[:], na[:], 1.0, None, op0=ALU.add)
                    return v.drain()
                t_prev = step("v", [], em_si, "r_v")

                for it in range(0 if CUT2 == "scan" else NBI):
                    def em_cnt(v):
                        v.tensor_tensor(vv[:], lo[:], hi[:], op=ALU.add)
                        v.drain()
                        v.tensor_scalar(vv[:], vv[:], 0.5, None, op0=ALU.mult)
                        v.drain()
                        v.tensor_scalar(cdump[:], stil[:, 0, :], vv[:, 0:1], None,
                                        op0=ALU.is_ge, op1=ALU.add, accum_out=cntb[:, 0:1])
                        v.drain()
                        v.tensor_scalar(cdump[:], stil[:, 1, :], vv[:, 1:2], None,
                                        op0=ALU.is_ge, op1=ALU.add, accum_out=cntb[:, 1:2])
                        return v.drain()
                    t_cnt = step("v", [], em_cnt, "r_v")
                    t_ac0 = step("t", [("r_v", t_cnt)],
                                 E(lambda te: te.matmul(pAR[0:1, 8:10], onec[:], cntb[:],
                                                        start=True, stop=True)), "r_t")
                    t_ac1 = step("s", [("r_t", t_ac0)],
                                 E(lambda s: s.copy(st0[0:1, 8:10], pAR[0:1, 8:10])), "r_s")
                    t_ac = step("t", [("r_s", t_ac1)],
                                E(lambda te: te.matmul(pAR[:, 12:14], onesr[:], st0[0:1, 8:10],
                                                       start=True, stop=True)), "r_t")
                    t_ac2 = step("v", [("r_t", t_ac)],
                                 E(lambda v: (v.tensor_copy(cntb[:], pAR[:, 12:14]), v.drain())[1]), "r_v")
                    def em_sel(v):
                        for r in range(2):
                            v.tensor_scalar(cond8[:, r:r + 1], cntb[:, r:r + 1], kk[r], None,
                                            op0=ALU.is_ge)
                            v.tensor_scalar(ncond8[:, r:r + 1], cntb[:, r:r + 1], kk[r], None,
                                            op0=ALU.is_lt)
                        v.drain()
                        v.select(lo[:], cond8[:], vv[:], lo[:], add_drain=True)
                        v.drain()
                        v.select(hi[:], ncond8[:], vv[:], hi[:], add_drain=True)
                        return v.drain()
                    t_prev = step("v", [], em_sel, "r_v")

                def em_vs(v):
                    for r in range(2):
                        v.tensor_scalar(m8[:, r, :], stil[:, r, :], hi[:, r:r + 1], None,
                                        op0=ALU.is_ge)
                    v.drain()
                    v.select(tbuf.ap().rearrange("p a b -> p (a b)"),
                             m8.ap().rearrange("p a b -> p (a b)"),
                             neginf.ap().to_broadcast([128, 2 * NT]),
                             stil.ap().rearrange("p a b -> p (a b)"), add_drain=True)
                    v.drain()
                    return v.tensor_reduce(vstar[:], tbuf[:, :, :], axis=X, op=ALU.max)
                t_vs = step("v", [], em_vs, "r_v")
                def em_vsT(te):
                    return te.transpose(pAR[0:2, 0:128], vstar[:], idf32[:])
                t_vst = step("t", [("r_v", t_vs)], em_vsT, "r_t")
                def em_vsE(v):
                    v.tensor_copy(tr2[:], pAR[0:2, 0:128])
                    v.drain()
                    v.tensor_reduce(m2[:], tr2[:], axis=X, op=ALU.max)
                    return v.drain()
                t_vse = step("v", [("r_t", t_vst)], em_vsE, "r_v")
                def em_vsT2(te):
                    return te.transpose(pAR[0:1, 128:130], m2[:], idf32[0:2, 0:2])
                t_vst2 = step("t", [("r_v", t_vse)], em_vsT2, "r_t")
                def em_vsE2(v):
                    v.tensor_copy(st0[0:1, 10:12], pAR[0:1, 128:130])
                    return v.drain()
                t_vse2 = step("v", [("r_t", t_vst2)], em_vsE2, "r_v")
                def em_vsB(te):
                    return te.matmul(pAR[:, 256:258], onesr[:], st0[0:1, 10:12],
                                     start=True, stop=True)
                t_vsb = step("t", [("r_v", t_vse2)], em_vsB, "r_t")
                def em_vsE3(v):
                    v.tensor_copy(vstar[:], pAR[:, 256:258])
                    return v.drain()
                t_vsa = step("v", [("r_t", t_vsb)], em_vsE3, "r_v")
                def em_mk(v):
                    for r in range(2):
                        v.tensor_scalar(mhi[:, r, :], stil[:, r, :], vstar[:, r:r + 1], None,
                                        op0=ALU.is_gt)
                        v.tensor_scalar(meq[:, r, :], stil[:, r, :], vstar[:, r:r + 1], None,
                                        op0=ALU.is_equal)
                    v.drain()
                    v.tensor_scalar(cdump[:], mhi[:, 0, :], 1.0, None, op0=ALU.mult,
                                    op1=ALU.add, accum_out=nhi[:, 0:1])
                    v.drain()
                    v.tensor_scalar(cdump[:], mhi[:, 1, :], 1.0, None, op0=ALU.mult,
                                    op1=ALU.add, accum_out=nhi[:, 1:2])
                    return v.drain()
                t_mk = step("v", [("r_v", t_vsa)], em_mk, "r_v")
                t_nh0 = step("t", [("r_v", t_mk)],
                             E(lambda te: te.matmul(pAR[0:1, 8:10], onec[:], nhi[:],
                                                    start=True, stop=True)), "r_t")
                t_nh1 = step("s", [("r_t", t_nh0)],
                             E(lambda s: s.copy(st0[0:1, 12:14], pAR[0:1, 8:10])), "r_s")
                t_nh2 = step("t", [("r_s", t_nh1)],
                             E(lambda te: te.matmul(pAR[:, 12:14], onesr[:], st0[0:1, 12:14],
                                                    start=True, stop=True)), "r_t")
                t_nha = step("v", [("r_t", t_nh2)],
                             E(lambda v: (v.tensor_copy(nhi[:], pAR[:, 12:14]), v.drain())[1]), "r_v")
                def em_p1(te):
                    for r in range(2):
                        te.matmul(pfx[:, r, :], tlow[:], meq[:, r, :], start=True, stop=True)
                        i = te.matmul(ptot[0:1, r, :], onec[:], meq[:, r, :], start=True,
                                      stop=True)
                    return i
                t_p1 = step("t", [("r_v", t_mk), ("r_g", t_const)], em_p1, "r_t")
                def em_sc1(v):
                    v.tensor_copy(totc[0:1, :, :], ptot[0:1, :, :])
                    v.drain()
                    for r in range(2):
                        v.tensor_tensor_scan(tot[0:1, r, :], totc[0:1, r, :], totc[0:1, r, :],
                                             0.0, op0=ALU.add, op1=ALU.bypass)
                    v.drain()
                    v.tensor_tensor(tot[0:1, :, :], tot[0:1, :, :], totc[0:1, :, :],
                                    op=ALU.subtract)
                    return v.drain()
                t_sc1 = step("v", [("r_t", t_p1)], em_sc1, "r_v")
                def em_ob(te):
                    return te.matmul(pAR[:, 64:64 + 2 * NT], onesr[:],
                                     tot.ap().rearrange("p a b -> p (a b)"),
                                     start=True, stop=True)
                t_ob1a = step("t", [("r_v", t_sc1)], em_ob, "r_t")
                t_ob1 = step("v", [("r_t", t_ob1a)],
                             E(lambda v: (v.tensor_copy(
                                 offs.ap().rearrange("p a b -> p (a b)"),
                                 pAR[:, 64:64 + 2 * NT]), v.drain())[1]), "r_v")
                def em_sel2(v):
                    v.tensor_tensor(rank[:, :, :], pfx[:, :, :], offs[:, :, :], op=ALU.add)
                    v.drain()
                    for r in range(2):
                        v.tensor_scalar(tbuf[:, r, :], rank[:, r, :], nhi[:, r:r + 1], kk[r],
                                        op0=ALU.add, op1=ALU.is_le)
                    v.drain()
                    v.tensor_tensor(tbuf[:, :, :], tbuf[:, :, :], meq[:, :, :], op=ALU.mult)
                    v.drain()
                    v.tensor_tensor(selb[:, :, :], tbuf[:, :, :], mhi[:, :, :], op=ALU.add)
                    return v.drain()
                t_sel2 = step("v", [("r_v", t_ob1), ("r_v", t_nha)], em_sel2, "r_v")
                def em_p2(te):
                    for r in range(2):
                        te.matmul(pfx[:, r, :], tlow[:], selb[:, r, :], start=True, stop=True)
                        i = te.matmul(ptot[0:1, r, :], onec[:], selb[:, r, :], start=True,
                                      stop=True)
                    return i
                t_p2s = step("t", [("r_v", t_sel2)], em_p2, "r_t")
                t_sc2 = step("v", [("r_t", t_p2s)], em_sc1, "r_v")
                t_ob2a = step("t", [("r_v", t_sc2)], em_ob, "r_t")
                t_ob2 = step("v", [("r_t", t_ob2a)],
                             E(lambda v: (v.tensor_copy(
                                 offs.ap().rearrange("p a b -> p (a b)"),
                                 pAR[:, 64:64 + 2 * NT]), v.drain())[1]), "r_v")
                def em_rank2(v):
                    v.tensor_tensor(rank[:, :, :], pfx[:, :, :], offs[:, :, :], op=ALU.add)
                    return v.drain()
                t_rank2 = step("v", [("r_v", t_ob2)], em_rank2, "r_v")
                def em_g(v):
                    v.tensor_scalar(m8[:, 0, :], selb[:, 0, :], 0.0, None, op0=ALU.is_equal)
                    v.tensor_scalar(gf[:], rank[:, 0, :], -1.0, None, op0=ALU.add)
                    v.drain()
                    v.select(gf[:], m8[:, 0, :], c512.ap().to_broadcast([128, NT]), gf[:],
                             add_drain=True)
                    v.drain()
                    v.tensor_copy(gIv[:], gf[:])
                    return v.drain()
                t_g = step("v", [("r_v", t_rank2)], em_g, "r_v")

                t_cb_prev = t_g
                t_mmslot = {}
                nslot = 0
                for r in range(2):
                    for rs in range(RSUB[r]):
                        col = (0 if r == 0 else RSUB[0]) + rs
                        def em_cb(v, r=r, rs=rs):
                            v.tensor_tensor(mr[:], rank[:, r, :], selb[:, r, :], op=ALU.mult)
                            v.drain()
                            v.tensor_scalar(mrs[:], mr[:], -128.0 * rs, None, op0=ALU.add)
                            v.drain()
                            for f in range(NT):
                                v.tensor_tensor(cba[:, f * 128:(f + 1) * 128],
                                                mrs[:, f:f + 1].to_broadcast([128, 128]),
                                                iotaR[:], op=ALU.is_equal)
                            return v.drain()
                        waits = [("r_v", t_cb_prev)]
                        if nslot >= 1:
                            waits.append(("r_t", t_mmslot[nslot - 1]))
                        t_cb = step("v", waits, em_cb, "r_v")
                        def em_mm(te, col=col):
                            for f in range(NT):
                                te.matmul(pidx[:, col:col + 1],
                                          cba[:, f * 128:(f + 1) * 128], iotaC[:],
                                          start=(f == 0), stop=False,
                                          skip_group_check=True)
                                i = te.matmul(pidx[:, col:col + 1],
                                              cba[:, f * 128:(f + 1) * 128],
                                              fval[:, f:f + 1],
                                              start=False, stop=(f == NT - 1),
                                              skip_group_check=True)
                            return i
                        t_mm = step("t", [("r_v", t_cb)], em_mm, "r_t")
                        t_mmslot[nslot] = t_mm
                        t_cb_prev = t_cb
                        nslot += 1
                def em_cvt(v):
                    v.tensor_copy(idxf[:], pidx[:, 0:12])
                    v.drain()
                    v.tensor_copy(idxu[:], idxf[:])
                    return v.drain()
                t_cvt = step("v", [("r_t", t_mmslot[nslot - 1])], em_cvt, "r_v")

                engines = {"v": "vector", "s": "scalar", "g": "gpsimd", "t": "tensor"}
                for k in ("r_v", "r_s", "r_g", "r_t"):
                    cnt[k] = sc[k]
                with nc.Block() as block:
                    def make_emitter(ekey):
                        def run(eng):
                            for (e, waits, emit, sem_name, val) in steps:
                                if e != ekey:
                                    continue
                                for (wsem, wval) in waits:
                                    eng.wait_ge(sems[wsem], wval)
                                last = emit(eng)
                                last.then_inc(sems[sem_name], 1)
                        return run
                    _eng_used = {e for (e, *_rest) in steps}
                    if "v" in _eng_used:
                        block.vector(make_emitter("v"))
                    if "s" in _eng_used:
                        block.scalar(make_emitter("s"))
                    if "g" in _eng_used:
                        block.gpsimd(make_emitter("g"))
                    if "t" in _eng_used:
                        block.tensor(make_emitter("t"))

                ph3 = Ph(sems, cnt)
                with nc.Block() as block:
                    @block.sync
                    def _(sy):
                        sy.wait_ge(sems["r_v"], t_cvt)
                        ph3.inc(sy.dma_start(out=dbg_na[:], in_=na[:]), "r_dw", 16)
                        ph3.inc(sy.dma_start(out=dbg_sel[:], in_=selb[:, :, :]), "r_dw", 16)
                        ph3.inc(sy.dma_start(out=dbg_idx[:], in_=idxu[:]), "r_dw", 16)
                        ph3.inc(sy.dma_start(out=dbg_g[:], in_=gIv[:]), "r_dw", 16)
                        ph3.inc(sy.dma_start(out=dbg_x[:, 0:2], in_=vstar[:]), "r_dw", 16)
                        ph3.inc(sy.dma_start(out=dbg_x[:, 2:4], in_=lo[:]), "r_dw", 16)
                        ph3.inc(sy.dma_start(out=dbg_x[:, 4:6], in_=hi[:]), "r_dw", 16)
                        ph3.inc(sy.dma_start(out=dbg_x[:, 6:8], in_=nhi[:]), "r_dw", 16)
                        ph3.inc(sy.dma_start(out=dbg_x[:, 8:10], in_=cntb[:]), "r_dw", 16)
                        sy.wait_ge(sems["r_dw"], ph3.tok("r_dw", 144))


            if _os.environ.get("L0CUT") == "p2":
                return ("r_v", cnt.get("r_v", 0) - NT)
            NRQ = NQ // 128
            NRK = NKV // 128
            NCHK = S // 1024

            with contextlib.ExitStack() as hstk:
                def sbh(name, shape, dt=F32):
                    return hstk.enter_context(nc.sbuf_tensor(f"{tag}_{name}", shape, dt))
                xqnT = sbh("xqnT", [128, NCH, NQ], BF16)
                xkvnT = sbh("xkvnT", [128, NCH, NKV], BF16)
                ptH = hstk.enter_context(nc.psum_tensor(f"{tag}_ptH", [128, 1024], BF16))
                ppj = hstk.enter_context(nc.psum_tensor(f"{tag}_ppj", [128, 2, 512], F32))
                psSh = hstk.enter_context(nc.psum_tensor(f"{tag}_psSh", [128, 1, 2, 512], F32))
                psAVh = hstk.enter_context(nc.psum_tensor(f"{tag}_psAVh", [128, 2, 512], F32))

                gstk_in = contextlib.ExitStack()
                def sbg(name, shape, dt=F32):
                    return gstk_in.enter_context(nc.sbuf_tensor(f"{tag}_{name}", shape, dt))
                xgp = sbg("xgp", [128, D])
                gst = sbg("gst", [128, 12]); gmv = sbg("gmv", [128, 2])
                grr = sbg("grr", [128, 1]); gnm = sbg("gnm", [128, 1])
                gxm = sbg("gxm", [128, D]); gss = sbg("gss", [128, 1])
                gcp = sbg("gcp", [128, 1])
                gmb = sbg("gmb", [128, D])
                xqn = sbg("xqn", [128, NRQ, D], BF16)
                xkvn = sbg("xkvn", [128, NRK, D], BF16)

                steps2 = []
                sc2 = {}
                for nm in ["h_v", "h_s", "h_t", "h_dl", "h_dw", "g_g", "g_c", "s_w", "h_w0", "h_w1"]:
                    sc2[nm] = cnt.get(nm, 0)

                def st2(eng, waits, emit, inc_sem, inc_n=1):
                    sc2[inc_sem] += inc_n
                    steps2.append((eng, list(waits), emit, inc_sem, inc_n))
                    return sc2[inc_sem]

                X = mybir.AxisListType.X

                def em_c(sy):
                    return sy.dma_start(out=gmb[:, 0:D], in_=gamma_b[:])
                pdw2 = [("p_dw0", ph.tok("p_dw0", 16 * (NT // 2))), ("p_dw1", ph.tok("p_dw1", 16 * (NT // 2))), ("x_sn", cnt.get("x_sn", 0))]
                t_cw = st2("y", [("r_v", cnt["r_v"])] + pdw2, em_c, "s_w", 16)

                f_prev = cnt.get("f_xd", 0)
                t_null = st2("y", [("f_xd", f_prev)],
                             lambda sy: sy.dma_start(out=heavy_d[512:513, :],
                                                     in_=nullq_b[0:1, :]), "h_dw", 16)

                t_gn = {}
                prevn = None
                for i in range(NRQ + NRK):
                    w = [("r_v", cnt["r_v"])] + pdw2
                    if xwait2 is not None:
                        w.append(xwait2)
                    if prevn is not None:
                        w.append(("h_v", prevn))
                    t_ga = st2("g", w,
                               (lambda i=i: lambda g_: g_.indirect_dma_start(
                                   out=xgp[:], out_offset=None, in_=xsrc,
                                   in_offset=bass.IndirectOffsetOnAxis(
                                       ap=idxu[:, i:i + 1], axis=0)))(), "g_g", 16)
                    def em_n1(v, i=i):
                        v.bn_stats(gst[:, 0:6], xgp[:, 0:512])
                        v.bn_stats(gst[:, 6:12], xgp[:, 512:1024])
                        v.drain()
                        v.bn_aggr(gmv[:], gst[:])
                        v.drain()
                        v.tensor_scalar(grr[:], gmv[:, 1:2], 1e-5, None, op0=ALU.add)
                        return v.drain()
                    t_n1 = st2("v", [("g_g", sc2["g_g"]), ("s_w", t_cw)], em_n1, "h_v")
                    def em_s1(s):
                        s.activation(out=grr[:], in_=grr[:], func=AF.Ln)
                        s.drain()
                        return s.activation(out=grr[:], in_=grr[:], func=AF.Exp, scale=-0.5)
                    t_s1 = st2("s", [("h_v", t_n1)], em_s1, "h_s")
                    def em_n2(v):
                        v.scalar_tensor_tensor(out=gnm[:], in0=gmv[:, 0:1], scalar=-1.0,
                                               in1=grr[:], op0=ALU.mult, op1=ALU.mult)
                        v.drain()
                        v.tensor_scalar(gxm[:], xgp[:], grr[:], gnm[:], op0=ALU.mult, op1=ALU.add)
                        v.drain()
                        v.tensor_tensor(xgp[:], gxm[:], gxm[:], op=ALU.mult)
                        v.drain()
                        v.tensor_reduce(gss[:], xgp[:], axis=mybir.AxisListType.X, op=ALU.add)
                        return v.drain()
                    t_n2 = st2("v", [("h_s", t_s1)], em_n2, "h_v")
                    def em_s2(s):
                        s.activation(out=gss[:], in_=gss[:], func=AF.Ln)
                        s.drain()
                        return s.activation(out=gss[:], in_=gss[:], func=AF.Exp, scale=0.5)
                    t_s2 = st2("s", [("h_v", t_n2)], em_s2, "h_s")
                    def em_n3(v, i=i):
                        v.tensor_scalar(gss[:], gss[:], 1e-12, None, op0=ALU.max)
                        v.drain()
                        v.reciprocal(gcp[:], gss[:])
                        v.drain()
                        v.tensor_scalar(gxm[:], gxm[:], gcp[:], 32.0, op0=ALU.mult, op1=ALU.mult)
                        v.drain()
                        dst = xqn[:, i, :] if i < NRQ else xkvn[:, i - NRQ, :]
                        v.tensor_tensor(dst, gxm[:], gmb[:, 0:D], op=ALU.mult)
                        return v.drain()
                    prevn = st2("v", [("h_s", t_s2)], em_n3, "h_v")
                    t_gn[i] = prevn

                t_tr = {}
                for i in range(NRQ + NRK):
                    def em_tr(te, i=i):
                        srcb = xqn[:, i, :] if i < NRQ else xkvn[:, i - NRQ, :]
                        for c in range(NCH):
                            ii = te.transpose(ptH[:, c * 128:(c + 1) * 128],
                                              srcb[:, c * 128:(c + 1) * 128], ident[:])
                        return ii
                    w = [("h_v", t_gn[i]), ("s_aux", 1)]
                    if i > 0:
                        w.append(("h_s", t_tr[i - 1]))
                    t_t = st2("t", w, em_tr, "h_t")
                    def em_te(s, i=i):
                        pv = ptH.ap().rearrange("p (c j) -> p c j", j=128)
                        if i < NRQ:
                            return s.copy(xqnT[:, :, i * 128:(i + 1) * 128], pv)
                        j = i - NRQ
                        return s.copy(xkvnT[:, :, j * 128:(j + 1) * 128], pv)
                    t_tr[i] = st2("s", [("h_t", t_t)], em_te, "h_s")

                steps2_p1 = steps2
                with nc.Block() as block:
                    def mk2a(ekey):
                        def run(eng):
                            for (e, waits, emit, sem_name, inc_n) in steps2_p1:
                                if e != ekey:
                                    continue
                                for (wsem, wval) in waits:
                                    eng.wait_ge(sems[wsem], wval)
                                emit(eng).then_inc(sems[sem_name], inc_n)
                        return run
                    _eng_used = {e for (e, *_rest) in steps2_p1}
                    if "v" in _eng_used:
                        block.vector(mk2a("v"))
                    if "s" in _eng_used:
                        block.scalar(mk2a("s"))
                    if "g" in _eng_used:
                        block.gpsimd(mk2a("g"))
                    if "t" in _eng_used:
                        block.tensor(mk2a("t"))
                    if "y" in _eng_used:
                        block.sync(mk2a("y"))
                gstk_in.close()
                kvfm = sbh("kvfm", [128, 4, NKV], BF16)
                vstg = sbh("vstg", [128, NKV], BF16)
                vtmh = sbh("vtmh", [128, NRK, 8, 65], BF16)
                wppb = sbh("wppb", [128, 2, NCH, 128], BF16)
                nksb = sbh("nksb", [128, 8])
                nksbf = sbh("nksbf", [128, 8], BF16)
                nvsb = sbh("nvsb", [1, 8 * 65], BF16)
                qfmh = sbh("qfmh", [128, NRQ, NQ], BF16)
                ensb = sbh("ensb", [1, 2, 512], BF16)
                ath = sbh("ath", [128, 2, 512], BF16)
                rsh = sbh("rsh", [128, 8])
                aoh = sbh("aoh", [128, 4, 512], BF16)
                aohT = sbh("aohT", [128, 4, 512], BF16)
                whob = sbh("whob", [128, 4, D], BF16)
                hvo = sbh("hvo", [128, 512])
                steps2 = []
                t_who = st2("g", [("h_s", sc2["h_s"])],
                            lambda g_: g_.dma_start(out=whob[:, :, :],
                                                    in_=whoT.rearrange("(c p) n -> p c n",
                                                                       p=128)), "h_dl", 16)
                def em_c2(sy):
                    return sy.dma_start(out=nksb[:], in_=nullk_dup[:])
                t_cw2 = st2("y", [("h_s", sc2["h_s"])], em_c2, "s_w", 16)
                t_cw3 = st2("g", [("h_s", sc2["h_s"])],
                            lambda g_: g_.dma_start(out=nvsb[0:1, :],
                                                    in_=nullv1.rearrange("h n -> (h n)")
                                                    .rearrange("x -> () x")), "g_g", 16)
                t_ones = st2("g", [("h_s", sc2["h_s"])],
                             lambda g_: g_.memset(vtmh[:, :, :, 64:65], 1.0), "g_c", 1)
                t_nkb = st2("v", [("s_w", sc2["s_w"])],
                            lambda v: v.tensor_copy(nksbf[:], nksb[:]), "h_v")

                t_ev = {}
                units = [("q", e, 0) for e in range(4)] + \
                        [("kv", e, h) for e in range(8) for h in range(2)]
                t_mm_u = {}
                prev_vt = None
                for ui, (kind, e, h) in enumerate(units):
                    wsrc = whqT if kind == "q" else whkvT
                    def em_dl(sy, wsrc=wsrc, e=e, ui=ui):
                        return sy.dma_start(out=wppb[:, ui % 2, :, :],
                                            in_=wsrc[:, e * 128:(e + 1) * 128]
                                            .rearrange("(c p) n -> p c n", p=128))
                    w = [("r_v", cnt["r_v"])] + pdw2
                    if ui >= 2:
                        w.append(("h_t", t_mm_u[ui - 2]))
                    sem_dl = "h_w0" if ui % 2 == 0 else "h_w1"
                    t_dl = st2("g", w, em_dl, sem_dl, 16)
                    def em_mm(te, kind=kind, e=e, h=h, ui=ui):
                        srcT = xqnT if kind == "q" else xkvnT
                        w0 = h * 512
                        for c in range(NCH):
                            ii = te.matmul(ppj[:, ui % 2, :], wppb[:, ui % 2, c, :],
                                           srcT[:, c, w0:w0 + 512],
                                           start=(c == 0), stop=(c == NCH - 1))
                        return ii
                    w = [(sem_dl, sc2[sem_dl]), ("h_s", t_tr[NRQ + NRK - 1])]
                    if ui >= 2:
                        w.append(("h_s", t_ev[ui - 2]))
                    t_mm = st2("t", w, em_mm, "h_t")
                    t_mm_u[ui] = t_mm
                    def em_ev(s, kind=kind, e=e, h=h, ui=ui):
                        if kind == "q":
                            return s.copy(qfmh[:, e, :], ppj[:, ui % 2, :])
                        if e < 4:
                            return s.copy(kvfm[:, e, h * 512:(h + 1) * 512], ppj[:, ui % 2, :])
                        return s.copy(vstg[:, h * 512:(h + 1) * 512], ppj[:, ui % 2, :])
                    w = [("h_t", t_mm)]
                    if kind == "kv" and e >= 5 and h == 0 and prev_vt is not None:
                        w.append(("h_s", prev_vt))
                    t_ev[ui] = st2("s", w, em_ev, "h_s")
                    if kind == "kv" and e >= 4 and h == 1:
                        j = e - 4
                        for r8 in range(8):
                            def em_vt(te, r8=r8):
                                return te.transpose(ptH[:, 0:128],
                                                    vstg[:, r8 * 128:(r8 + 1) * 128], ident[:])
                            w = [("h_s", t_ev[ui])]
                            if prev_vt is not None:
                                w.append(("h_s", prev_vt))
                            t_vt = st2("t", w, em_vt, "h_t")
                            def em_ve(s, j=j, r8=r8):
                                pv = ptH[:, 0:128].rearrange("p (a b) -> p a b", b=64)
                                return s.copy(vtmh[:, r8, 2 * j:2 * j + 2, 0:64], pv)
                            prev_vt = st2("s", [("h_t", t_vt), ("g_c", t_ones)], em_ve, "h_s")
                t_kv_done = t_ev[len(units) - 1]

                t_pair_done = {}
                for pj in range(4):
                    hA, hB = 2 * pj, 2 * pj + 1
                    def em_ns(te, pj=pj):
                        te.matmul(psSh[0:1, 0, 0, :], nksbf[0:64, 2 * pj:2 * pj + 1],
                                  qfmh[0:64, pj, :], start=True, stop=True)
                        return te.matmul(psSh[0:1, 0, 1, :], nksbf[64:128, 2 * pj + 1:2 * pj + 2],
                                         qfmh[64:128, pj, :], start=True, stop=True,
                                         tile_position=(64, 0))
                    w = [("h_s", t_kv_done), ("s_w", t_cw), ("h_v", t_nkb), ("g_g", t_cw3)]
                    if pj > 0:
                        w.append(("h_s", t_pair_done[pj - 1]))
                    if prev_vt is not None:
                        w.append(("h_s", prev_vt))
                    t_ns = st2("t", w, em_ns, "h_t")
                    def em_nse(s, pj=pj):
                        s.activation(out=ensb[0:1, 0, :], in_=psSh[0:1, 0, 0, :], func=AF.Exp,
                                     scale=0.125)
                        return s.activation(out=ensb[0:1, 1, :], in_=psSh[0:1, 0, 1, :],
                                            func=AF.Exp, scale=0.125)
                    t_nse = st2("s", [("h_t", t_ns)], em_nse, "h_s")
                    def em_nav(te, pj=pj):
                        for a in range(2):
                            for j in range(4):
                                ii = te.matmul(psAVh[:, a, j * 65:(j + 1) * 65],
                                               ensb[0:1, a, j * 128:(j + 1) * 128],
                                               nvsb[0:1, (2 * pj + a) * 65:(2 * pj + a + 1) * 65],
                                               start=(j == 0), stop=False,
                                               skip_group_check=True)
                        return ii
                    t_nav = st2("t", [("h_s", t_nse)], em_nav, "h_t")
                    t_ex_k = {}
                    t_av_prev = t_nav
                    for kc in range(NRK):
                        def em_sc(te, pj=pj, kc=kc):
                            te.matmul(psSh[:, 0, 0, :], kvfm[0:64, pj, kc * 128:(kc + 1) * 128],
                                      qfmh[0:64, pj, :], start=True, stop=True)
                            return te.matmul(psSh[:, 0, 1, :],
                                             kvfm[64:128, pj, kc * 128:(kc + 1) * 128],
                                             qfmh[64:128, pj, :], start=True, stop=True,
                                             tile_position=(64, 0))
                        w = []
                        if kc == 0:
                            w.append(("h_s", t_nse))
                        if kc >= 1:
                            w.append(("h_s", t_ex_k[kc - 1]))
                        t_sc = st2("t", w, em_sc, "h_t")
                        def em_ex(s, pj=pj, kc=kc):
                            s.activation(out=ath[:, 0, :], in_=psSh[:, 0, 0, :], func=AF.Exp,
                                         scale=0.125)
                            return s.activation(out=ath[:, 1, :], in_=psSh[:, 0, 1, :],
                                                func=AF.Exp, scale=0.125)
                        w = [("h_t", t_sc)]
                        if kc >= 1:
                            w.append(("h_t", t_av_prev))
                        t_ex = st2("s", w, em_ex, "h_s")
                        t_ex_k[kc] = t_ex
                        def em_av(te, pj=pj, kc=kc):
                            for a in range(2):
                                for j in range(4):
                                    ii = te.matmul(psAVh[:, a, j * 65:(j + 1) * 65],
                                                   ath[:, a, j * 128:(j + 1) * 128],
                                                   vtmh[:, kc, 2 * pj + a, :],
                                                   start=False, stop=(kc == NRK - 1 and j == 3),
                                                   skip_group_check=True)
                            return ii
                        t_av_prev = st2("t", [("h_s", t_ex)], em_av, "h_t")
                    def em_rc(v, pj=pj):
                        return v.reciprocal(
                            rsh.ap().rearrange("p (a j) -> p a j", j=4),
                            psAVh[:, :, 0:260].rearrange("p a (j n) -> p a j n", n=65)
                            [:, :, :, 64:65])
                    t_rc = st2("v", [("h_t", t_av_prev)], em_rc, "h_v")
                    def em_ae(s, pj=pj):
                        for a in range(2):
                            for j in range(4):
                                ii = s.activation(out=aoh[:, j, (2 * pj + a) * 64:
                                                           (2 * pj + a + 1) * 64],
                                                  in_=psAVh[:, a, j * 65:j * 65 + 64],
                                                  func=AF.Identity,
                                                  scale=rsh[:, a * 4 + j:a * 4 + j + 1])
                        return ii
                    t_pair_done[pj] = st2("s", [("h_v", t_rc)], em_ae, "h_s")

                t_hdw = sc2["h_dw"]
                prev_o = None
                for j in range(4):
                    def em_aot(te, j=j):
                        for c in range(4):
                            ii = te.transpose(ptH[:, c * 128:(c + 1) * 128],
                                              aoh[:, j, c * 128:(c + 1) * 128], ident[:])
                        return ii
                    w = [("h_s", t_pair_done[3])]
                    if prev_o is not None:
                        w.append(("h_v", prev_o))
                    t_at = st2("t", w, em_aot, "h_t")
                    def em_ate(s, j=j):
                        pv = ptH[:, 0:512].rearrange("p (c j2) -> p c j2", j2=128)
                        return s.copy(aohT[:, :, j * 128:(j + 1) * 128], pv)
                    t_ae2 = st2("s", [("h_t", t_at)], em_ate, "h_s")
                    def em_op(te, j=j):
                        for half in range(2):
                            for c in range(4):
                                ii = te.matmul(ppj[:, half, :], aohT[:, c, j * 128:(j + 1) * 128],
                                               whob[:, c, half * 512:(half + 1) * 512],
                                               start=(c == 0), stop=(c == 3))
                        return ii
                    t_op = st2("t", [("h_s", t_ae2), ("h_dl", t_who)], em_op, "h_t")
                    for half in range(2):
                        def em_oe(v, j=j, half=half):
                            v.tensor_copy(hvo[:], ppj[:, half, :])
                            return v.drain()
                        t_oe = st2("v", [("h_t", t_op),
                                         ("h_dw", t_null + 32 * j + 16 * half)], em_oe, "h_v")
                        prev_o = t_oe
                        def em_od(sy, j=j, half=half):
                            return sy.dma_start(out=heavy_d[j * 128:(j + 1) * 128,
                                                            half * 512:(half + 1) * 512],
                                                in_=hvo[:])
                        t_od = st2("y", [("h_v", t_oe), ("f_xd", f_prev)], em_od, "h_dw", 16)
                HEAVY_DONE = sc2["h_dw"]

                with nc.Block() as block:
                    def mk2(ekey):
                        def run(eng):
                            for (e, waits, emit, sem_name, inc_n) in steps2:
                                if e != ekey:
                                    continue
                                for (wsem, wval) in waits:
                                    eng.wait_ge(sems[wsem], wval)
                                last = emit(eng)
                                last.then_inc(sems[sem_name], inc_n)
                        return run
                    _eng_used = {e for (e, *_rest) in steps2}
                    if "v" in _eng_used:
                        block.vector(mk2("v"))
                    if "s" in _eng_used:
                        block.scalar(mk2("s"))
                    if "g" in _eng_used:
                        block.gpsimd(mk2("g"))
                    if "t" in _eng_used:
                        block.tensor(mk2("t"))
                    if "y" in _eng_used:
                        block.sync(mk2("y"))
                for k in sc2:
                    cnt[k] = sc2[k]

            if _os.environ.get("L0CUT") == "heavy":
                return ("h_v", cnt.get("h_v", 0) - NT)
            cnt_pad = ("x_sn", cnt.get("x_sn", 0))
            phl = Ph(sems, cnt)
            with contextlib.ExitStack() as lstk:
                def sbl(name, shape, dt=BF16):
                    return lstk.enter_context(nc.sbuf_tensor(f"{tag}_{name}", shape, dt))
                wlb = sbl("wlb", [128, NCH, 512])
                lnb = sbl("lnbuf", [128, 2, NCH, 384])
                qfl = sbl("qfl", [128, 4, 1024])
                kfl = sbl("kfl", [128, 4, 1152])
                vtl = sbl("vtl", [128, 9, 8, 65])
                atl = sbl("atl", [128, 2, 2, 2, 128])
                rsl = sbl("rsl", [128, 8], F32)
                aol = sbl("aol", [128, 4, 128])
                aolT = sbl("aolT", [128, 4, 128])
                wlob = sbl("wlob", [128, 4, 1024])
                lmsk = sbl("lmsk", [128, 4 * 128])
                rowf = sbl("rowf", [128, 2], F32)
                ptL = lstk.enter_context(nc.psum_tensor(f"{tag}_ptL", [128, 1024], BF16))
                ppL = lstk.enter_context(nc.psum_tensor(f"{tag}_ppL", [128, 2, 512], F32))
                psSl = lstk.enter_context(nc.psum_tensor(f"{tag}_psSl", [128, 2, 2, 256], F32))
                psAVl = lstk.enter_context(nc.psum_tensor(f"{tag}_psAVl", [128, 2, 256], F32))
                psOl = lstk.enter_context(nc.psum_tensor(f"{tag}_psOl", [128, 2, 512], F32))

                steps3 = []
                sc3 = {nm: cnt.get(nm, 0) for nm in
                       ["l_dl0", "l_dl1", "l_gw", "l_mm", "l_qe", "l_sc", "l_ex", "l_mk",
                        "l_av", "l_rc", "l_ae", "l_tr", "l_xd", "h_v", "g_c"]}

                def st3(eng, waits, emit, inc_sem, inc_n=1):
                    sc3[inc_sem] += inc_n
                    steps3.append((eng, list(waits), emit, inc_sem, inc_n))
                    return sc3[inc_sem]

                def em_lm(g_):
                    g_.memset(lmsk[:], 1.0)
                    g_.memset(rowf[:], 0.0)
                    g_.drain()
                    g_.affine_select(out=lmsk[:, 0:128], in_=lmsk[:, 0:128],
                                     compare_op=ALU.is_ge, fill=0.0, base=63,
                                     pattern=[[-1, 128]], channel_multiplier=0)
                    g_.affine_select(out=lmsk[:, 128:256], in_=lmsk[:, 128:256],
                                     compare_op=ALU.is_ge, fill=0.0, base=-64,
                                     pattern=[[1, 128]], channel_multiplier=0)
                    g_.affine_select(out=lmsk[:, 256:384], in_=lmsk[:, 256:384],
                                     compare_op=ALU.is_ge, fill=0.0, base=-64,
                                     pattern=[[0, 128]], channel_multiplier=1)
                    g_.affine_select(out=lmsk[:, 384:512], in_=lmsk[:, 384:512],
                                     compare_op=ALU.is_ge, fill=0.0, base=63,
                                     pattern=[[0, 128]], channel_multiplier=-1)
                    g_.drain()
                    g_.memset(rowf[:, 0:1], 1.0)
                    g_.memset(rowf[:, 1:2], 1.0)
                    g_.drain()
                    g_.affine_select(out=rowf[:, 0:1], in_=rowf[:, 0:1], compare_op=ALU.is_ge,
                                     fill=0.0, base=-64, pattern=[[0, 1]], channel_multiplier=1)
                    g_.affine_select(out=rowf[:, 1:2], in_=rowf[:, 1:2], compare_op=ALU.is_ge,
                                     fill=0.0, base=63, pattern=[[0, 1]], channel_multiplier=-1)
                    return g_.drain()
                t_lm = None
                if _os.environ.get("L0L4") != "nolm" and _os.environ.get("L0L2") != "nomask":
                    t_lm = st3("g", [("h_v", cnt["h_v"]), ("h_dw", HEAVY_DONE)], em_lm, "g_c", 1)
                def em_lm2(v):
                    v.tensor_scalar(lmsk[:, 0:128], lmsk[:, 0:128], rowf[:, 0:1], None,
                                    op0=ALU.max)
                    v.drain()
                    v.tensor_scalar(lmsk[:, 128:256], lmsk[:, 128:256], rowf[:, 1:2], None,
                                    op0=ALU.max)
                    return v.drain()
                t_lm2 = None
                if t_lm is not None:
                    t_lm2 = st3("v", [("g_c", t_lm)], em_lm2, "l_mk")

                def em_wlo(sy):
                    return sy.dma_start(out=wlob[:, :, :],
                                        in_=wloT.rearrange("(c p) n -> p c n", p=128))
                t_wlo = None
                if _os.environ.get("L0L4") not in ("nowlo", "nolm"):
                    t_wlo = st3("g", [("h_v", cnt["h_v"]), ("h_dw", HEAVY_DONE)], em_wlo,
                                "l_gw", 16)

                t_xadd = {}
                for ch in range(0 if _os.environ.get("L0L2") == "noprod" else NCHK):
                    QSEC = [(0, 384), (384, 384), (768, 256)]
                    KSEC = [(0, 384), (384, 384), (768, 384)]
                    def em_wq_(g_, ch=ch):
                        return g_.dma_start(out=wlb[:, :, :],
                                            in_=wlqkvT[:, 0:512]
                                            .rearrange("(c p) n -> p c n", p=128))
                    w = [("l_gw", sc3["l_gw"]), ("h_dw", HEAVY_DONE)]
                    if ch > 0:
                        w.append(("l_mm", sc3["l_mm"]))
                    t_wq = st3("g", w, em_wq_, "l_gw", 16)
                    prods = [("q", e, sec) for e in range(4) for sec in range(3)]
                    prods += [("k", e, sec) for e in range(4) for sec in range(3)]
                    t_pev = {}
                    t_pmm = {}
                    t_wpart = {"q": t_wq}
                    for ui, (kind, e, sec) in enumerate(prods):
                        if kind == "q":
                            s0, ncols = QSEC[sec]
                            tok0 = 64 + ch * 1024 + s0
                        else:
                            s0, ncols = KSEC[sec]
                            tok0 = ch * 1024 + s0
                        if kind == "k" and e == 0 and sec == 0:
                            def em_wk_(g_, ch=ch):
                                return g_.dma_start(out=wlb[:, :, :],
                                                    in_=wlqkvT[:, 512:1024]
                                                    .rearrange("(c p) n -> p c n", p=128))
                            t_wpart["k"] = st3("g", [("l_mm", sc3["l_mm"])], em_wk_, "l_gw", 16)
                        def em_pl(sy, tok0=tok0, ncols=ncols, ui=ui):
                            return sy.dma_start(out=lnb[:, ui % 2, :, 0:ncols],
                                                in_=lnT_d[:, :, tok0:tok0 + ncols]
                                                .rearrange("c p j -> p c j"))
                        w = [("p_dw0", ph.tok("p_dw0", 16 * (NT // 2))),
                             ("p_dw1", ph.tok("p_dw1", 16 * (NT // 2))),
                             ("h_dw", HEAVY_DONE), cnt_pad]
                        if ui >= 2:
                            w.append(("l_mm", t_pmm[ui - 2]))
                        else:
                            w.append(("l_mm", sc3["l_mm"]))
                        sem_dl = "l_dl0" if ui % 2 == 0 else "l_dl1"
                        t_pl = st3("y", w, em_pl, sem_dl, 16)
                        def em_pm(te, kind=kind, e=e, ui=ui, ncols=ncols):
                            for c in range(NCH):
                                ii = te.matmul(ppL[:, ui % 2, 0:ncols],
                                               wlb[:, c, e * 128:(e + 1) * 128],
                                               lnb[:, ui % 2, c, 0:ncols],
                                               start=(c == 0), stop=(c == NCH - 1))
                            return ii
                        w = [(sem_dl, sc3[sem_dl]), ("l_gw", t_wpart[kind])]
                        if _os.environ.get("L0L3") == "serial" and ui >= 1:
                            w.append(("l_qe", t_pev[ui - 1]))
                        elif ui >= 2:
                            w.append(("l_qe", t_pev[ui - 2]))
                        else:
                            w.append(("l_qe", sc3["l_qe"]))
                        t_pm_cur = st3("t", w, em_pm, "l_mm")
                        t_pmm[ui] = t_pm_cur
                        def em_pe(s, kind=kind, e=e, sec=sec, ui=ui, ncols=ncols):
                            if kind == "q":
                                s0 = QSEC[sec][0]
                                return s.copy(qfl[:, e, s0:s0 + ncols], ppL[:, ui % 2, 0:ncols])
                            s0 = KSEC[sec][0]
                            return s.copy(kfl[:, e, s0:s0 + ncols], ppL[:, ui % 2, 0:ncols])
                        t_pev[ui] = st3("s", [("l_mm", t_pm_cur)], em_pe, "l_qe")
                    def em_wv(g_, ch=ch):
                        return g_.dma_start(out=wlb[:, :, 0:512],
                                            in_=wlqkvT[:, 1024:1536]
                                            .rearrange("(c p) n -> p c n", p=128))
                    t_wv = st3("g", [("l_mm", t_pmm[len(prods) - 1])], em_wv, "l_gw", 16)
                    prev_ve = None
                    for e in range(0 if _os.environ.get("L0L2") == "nov" else 4):
                        for sec in range(3):
                            tok0 = ch * 1024 + KSEC[sec][0]
                            def em_vm(te, e=e, sec=sec):
                                ui2 = e * 3 + sec
                                for c in range(NCH):
                                    ii = te.matmul(ppL[:, ui2 % 2, 0:384],
                                                   wlb[:, c, e * 128:(e + 1) * 128],
                                                   lnb[:, ui2 % 2, c, 0:384],
                                                   start=(c == 0), stop=(c == NCH - 1))
                                return ii
                            ui = e * 3 + sec
                            def em_vl(sy, tok0=tok0, ui=ui):
                                return sy.dma_start(out=lnb[:, ui % 2, :, 0:384],
                                                    in_=lnT_d[:, :, tok0:tok0 + 384]
                                                    .rearrange("c p j -> p c j"))
                            sem_dl = "l_dl0" if ui % 2 == 0 else "l_dl1"
                            w = [("l_gw", sc3["l_gw"])]
                            if ui >= 2:
                                w.append(("l_mm", t_vmm[ui - 2]))
                            else:
                                w.append(("l_mm", sc3["l_mm"]))
                            t_vl = st3("y", w, em_vl, sem_dl, 16)
                            w = [(sem_dl, sc3[sem_dl])]
                            if ui >= 2:
                                w.append(("l_qe", t_vev[ui - 2]))
                            else:
                                w.append(("l_qe", sc3["l_qe"]))
                            t_vm = st3("t", w, em_vm, "l_mm")
                            if ui == 0:
                                t_vmm = {}
                                t_vev = {}
                            t_vmm[ui] = t_vm
                            def em_vev(s, e=e, sec=sec):
                                ui2 = e * 3 + sec
                                return s.copy(atl.ap().rearrange("p a b c d -> p (a b c d)")
                                              [:, 0:384], ppL[:, ui2 % 2, 0:384])
                            t_ve = st3("s", [("l_mm", t_vm)], em_vev, "l_qe")
                            t_vev[ui] = t_ve
                            for blk in range(3):
                                r8 = sec * 3 + blk
                                def em_vt(te, blk=blk):
                                    return te.transpose(ptL[:, 0:128],
                                                        atl.ap().rearrange(
                                                            "p a b c d -> p (a b c d)")
                                                        [:, blk * 128:(blk + 1) * 128], ident[:])
                                w = [("l_qe", t_ve)]
                                if prev_ve is not None:
                                    w.append(("l_qe", prev_ve))
                                t_vt = st3("t", w, em_vt, "l_tr")
                                def em_vte(s, e=e, r8=r8):
                                    pv = ptL[:, 0:128].rearrange("p (a b) -> p a b", b=64)
                                    return s.copy(vtl[:, r8, 2 * e:2 * e + 2, 0:64], pv)
                                wv = [("l_tr", t_vt)]
                                if t_lm is not None:
                                    wv.append(("g_c", t_lm))
                                prev_ve = st3("s", wv, em_vte, "l_qe")
                    t_prod_done = prev_ve if prev_ve is not None else t_pev[len(prods) - 1]
                    t_vo = st3("g", [("l_qe", t_prod_done)],
                               (lambda g_: g_.memset(vtl[:, :, :, 64:65], 1.0)), "g_c", 1)

                    if ch == 0:
                        t_lex = {}
                        t_lav = {}
                    prev_x = None
                    for qt in range(0 if _os.environ.get("L0L") == "prod" else 8):
                        gt = ch * 8 + qt
                        first = (gt == 0)
                        lastt = (gt == NT - 1)
                        STAGE = int(_os.environ.get("L0L5", "6"))
                        for pj in range(4):
                            ui = qt * 4 + pj
                            pp = ui % 2
                            def em_lsc(te, qt=qt, pj=pj, pp=pp):
                                _v6 = _os.environ.get("L0L6", "")
                                ii = None
                                for kc in range(2):
                                    if "k1" in _v6 and kc == 1:
                                        continue
                                    k0 = qt * 128 + kc * 128
                                    if "b" not in _v6:
                                        ii = te.matmul(psSl[:, 0, pp, kc * 128:(kc + 1) * 128],
                                                  kfl[0:64, pj, k0:k0 + 128],
                                                  qfl[0:64, pj, qt * 128:(qt + 1) * 128],
                                                  start=True, stop=True)
                                    if "a" not in _v6:
                                        ii = te.matmul(psSl[:, 1, pp, kc * 128:(kc + 1) * 128],
                                                   kfl[64:128, pj, k0:k0 + 128],
                                                   qfl[64:128, pj, qt * 128:(qt + 1) * 128],
                                                   start=True, stop=True,
                                                   tile_position=(64, 0))
                                return ii
                            w = [("l_qe", t_prod_done), ("g_c", t_vo)]
                            if ui >= 2 and STAGE >= 2:
                                w.append(("l_ex", t_lex[ui - 2]))
                            t_sc_ = st3("t", w, em_lsc, "l_sc")
                            if STAGE < 2:
                                continue
                            def em_lex(s, pp=pp):
                                s.activation(out=atl[:, pp, 0, :, :]
                                             .rearrange("p a b -> p (a b)"),
                                             in_=psSl[:, 0, pp, 0:256], func=AF.Exp, scale=0.125)
                                return s.activation(out=atl[:, pp, 1, :, :]
                                                    .rearrange("p a b -> p (a b)"),
                                                    in_=psSl[:, 1, pp, 0:256], func=AF.Exp,
                                                    scale=0.125)
                            w = [("l_sc", t_sc_)]
                            if ui >= 2 and STAGE >= 4 and (ui - 2) in t_lav:
                                w.append(("l_av", t_lav[ui - 2]))
                            t_ex_ = st3("s", w, em_lex, "l_ex")
                            t_lex[ui] = t_ex_
                            if STAGE < 3:
                                continue
                            def em_lmk(v, pp=pp, first=first, lastt=lastt):
                                m0 = lmsk[:, 256:384] if first else lmsk[:, 0:128]
                                m1 = lmsk[:, 384:512] if lastt else lmsk[:, 128:256]
                                for a in range(2):
                                    v.tensor_tensor(atl[:, pp, a, 0, :], atl[:, pp, a, 0, :],
                                                    m0, op=ALU.mult)
                                    v.tensor_tensor(atl[:, pp, a, 1, :], atl[:, pp, a, 1, :],
                                                    m1, op=ALU.mult)
                                return v.drain()
                            t_mk_ = st3("v", [("l_ex", t_ex_), ("l_mk", t_lm2)], em_lmk, "l_mk")
                            if STAGE < 4:
                                continue
                            def em_lav(te, qt=qt, pj=pj, pp=pp):
                                for a in range(2):
                                    for kc in range(2):
                                        r8 = qt + kc
                                        ii = te.matmul(psAVl[:, a, 0:65],
                                                       atl[:, pp, a, kc, :],
                                                       vtl[:, r8, 2 * pj + a, :],
                                                       start=(kc == 0), stop=(kc == 1),
                                                       skip_group_check=True)
                                return ii
                            t_av_ = st3("t", [("l_mk", t_mk_)], em_lav, "l_av")
                            t_lav[ui] = t_av_
                            if STAGE < 5:
                                continue
                            def em_lrc(v, pj=pj):
                                return v.reciprocal(rsl[:, 2 * pj:2 * pj + 2],
                                                    psAVl[:, :, 64:65])
                            t_rc_ = st3("v", [("l_av", t_av_)], em_lrc, "l_rc")
                            def em_lae(s, pj=pj):
                                for a in range(2):
                                    ii = s.activation(out=aol[:, 0, 0:64][:, 0:64] if False
                                                      else aol.ap().rearrange(
                                                          "p a b -> p (a b)")
                                                      [:, (2 * pj + a) * 64:
                                                       (2 * pj + a + 1) * 64],
                                                      in_=psAVl[:, a, 0:64],
                                                      func=AF.Identity,
                                                      scale=rsl[:, 2 * pj + a:2 * pj + a + 1])
                                return ii
                            t_ae_ = st3("s", [("l_rc", t_rc_)], em_lae, "l_ae")
                        if STAGE < 6:
                            continue
                        def em_lat(te, qt=qt):
                            for c in range(4):
                                ii = te.transpose(ptL[:, c * 128:(c + 1) * 128],
                                                  aol.ap().rearrange("p a b -> p (a b)")
                                                  [:, c * 128:(c + 1) * 128], ident[:])
                            return ii
                        w = [("l_ae", t_ae_)]
                        if prev_x is not None:
                            w.append(("l_xd", prev_x))
                        t_lat_ = st3("t", w, em_lat, "l_tr")
                        def em_late(s):
                            return s.copy(aolT.ap().rearrange("p a b -> p (a b)"),
                                          ptL[:, 0:512])
                        t_late_ = st3("s", [("l_tr", t_lat_)], em_late, "l_qe")
                        def em_lop(te, qt=qt):
                            for half in range(2):
                                for c in range(4):
                                    ii = te.matmul(psOl[:, half, :],
                                                   aolT[:, c, :],
                                                   wlob[:, c, half * 512:(half + 1) * 512],
                                                   start=(c == 0), stop=(c == 3))
                            return ii
                        t_lop_ = st3("t", [("l_qe", t_late_), ("l_gw", t_wlo)], em_lop, "l_mm")
                        def em_lxd(v, gt=gt):
                            v.scalar_tensor_tensor(out=xres[:, gt, 0:512],
                                                   in0=psOl[:, 0, :], scalar=0.5,
                                                   in1=xres[:, gt, 0:512],
                                                   op0=ALU.mult, op1=ALU.add)
                            v.drain()
                            v.scalar_tensor_tensor(out=xres[:, gt, 512:1024],
                                                   in0=psOl[:, 1, :], scalar=0.5,
                                                   in1=xres[:, gt, 512:1024],
                                                   op0=ALU.mult, op1=ALU.add)
                            return v.drain()
                        prev_x = st3("v", [("l_mm", t_lop_)], em_lxd, "l_xd")
                        t_xadd[gt] = prev_x

                with nc.Block() as block:
                    def mk3(ekey):
                        def run(eng):
                            for (e, waits, emit, sem_name, inc_n) in steps3:
                                if e != ekey:
                                    continue
                                for (wsem, wval) in waits:
                                    eng.wait_ge(sems[wsem], wval)
                                last = emit(eng)
                                last.then_inc(sems[sem_name], inc_n)
                        return run
                    _eng_used = {e for (e, *_rest) in steps3}
                    if "v" in _eng_used:
                        block.vector(mk3("v"))
                    if "s" in _eng_used:
                        block.scalar(mk3("s"))
                    if "g" in _eng_used:
                        block.gpsimd(mk3("g"))
                    if "t" in _eng_used:
                        block.tensor(mk3("t"))
                    if "y" in _eng_used:
                        block.sync(mk3("y"))
                for k in sc3:
                    cnt[k] = sc3[k]

            if _os.environ.get("L0CUT") == "light":
                return ("l_xd", cnt.get("l_xd", 0) - NT)
            ph4 = Ph(sems, cnt)
            with (
                nc.sbuf_tensor(f"{tag}_hb", [128, 2, D], F32) as hb,
                nc.Block() as block,
            ):
                @block.gpsimd
                def _(g_):
                    g_.wait_ge(sems["h_dw"], HEAVY_DONE)
                    for t in range(NT):
                        if t >= 2:
                            g_.wait_ge(sems["f_xd"], ph4.tok("f_xd", t - 1))
                        ph4.inc(g_.indirect_dma_start(
                            out=hb[:, t % 2, :], out_offset=None, in_=heavy_d,
                            in_offset=bass.IndirectOffsetOnAxis(ap=gIv[:, t:t + 1], axis=0)),
                            "f_g0" if t % 2 == 0 else "f_g1", 16)

                @block.vector
                def _(v):
                    for t in range(NT):
                        sfg = "f_g0" if t % 2 == 0 else "f_g1"
                        v.wait_ge(sems[sfg], ph4.tok(sfg, 16 * (t // 2 + 1)))
                        v.wait_ge(sems["l_xd"], t_xadd[t])
                        v.scalar_tensor_tensor(out=xres[:, t, :], in0=hb[:, t % 2, :],
                                               scalar=0.5, in1=xres[:, t, :],
                                               op0=ALU.mult, op1=ALU.add)
                        ph4.inc(v.drain(), "f_xd", 1)

        return ("f_xd", ph4.base["f_xd"])


    pre = X_READY

    def chain(res):
        return (res[0], res[1] + NT)

    if "s1" in want:
        pre = chain(emit_l0_ffn("s1", pre, x_in))
    if "s2" in want:
        pre = chain(emit_dense_attn("s2", pre, wqkv_l0, wot_l0, rope=False, snapshot_x=True))
        snap_tok = ("x_sn", cnt.get("x_sn", 0))
    if "s3" in want:
        pre = chain(emit_l0_ffn("s3", pre, xsnap_d, xwait2=snap_tok))
    if "s4" in want:
        pre = chain(emit_l1_ffn("s4", pre))
    if "s5" in want:
        pre = chain(emit_dense_attn("s5", pre, wqkv_l1, wot_l1, rope=True, snapshot_x=False))
    if "s6" in want:
        pre = chain(emit_l1_ffn("s6", pre))

    ph_o = Ph(sems, cnt)
    last_sem, last_top = pre
    with nc.Block() as block:
        @block.sync
        def _(sync):
            for t in range(NT):
                if last_sem == "s_in":
                    sync.wait_ge(sems["s_in"], last_top)
                else:
                    sync.wait_ge(sems[last_sem], max(0, last_top - NT + t + 1))
                ph_o.inc(sync.dma_start(out=y_out[t * 128:(t + 1) * 128, :],
                                        in_=xres[:, t, :]), "s_out", 16)
            sync.wait_ge(sems["s_out"], ph_o.tok("s_out", 16 * NT))

    stack.close()
    return nc


def _rope_tables(S):
    theta = 1.0 / (10000.0 ** (np.arange(0, 64, 2, dtype=np.float64) / 64.0))
    m = np.arange(S, dtype=np.float64)
    ang = np.outer(m, theta)
    ang = np.concatenate([ang, ang], axis=1)
    return np.cos(ang).astype(np.float32), np.sin(ang).astype(np.float32)


def _pack_qkv(WQ, WK, WV):
    out = np.empty((8, D, 384), dtype=np.float32)
    for g in range(8):
        sl = slice(g * 128, (g + 1) * 128)
        out[g, :, 0:128] = WQ.T[:, sl]
        out[g, :, 128:256] = WK.T[:, sl]
        out[g, :, 256:384] = WV.T[:, sl]
    return out


def make_in_maps(inputs, S=4096):
    a = {k: np.asarray(v) for k, v in inputs.items()}
    x = np.ascontiguousarray(a["L0_attn_in_features"][:, :S, :], dtype=np.float32)
    w1qT = np.ascontiguousarray(a["L1_ffn_weight_q"].T)
    w1s = np.ascontiguousarray(np.broadcast_to(
        a["L1_ffn_weight_scale"].astype(np.float32)[None, :], (128, D)))
    w1b = np.ascontiguousarray(np.broadcast_to(
        a["L1_ffn_bias"].astype(np.float32)[None, :], (128, D)))
    wqkv0 = _pack_qkv(a["L0_attn_q_proj_weight"], a["L0_attn_k_proj_weight"],
                      a["L0_attn_v_proj_weight"])
    wot0 = np.ascontiguousarray(a["L0_attn_o_proj_weight"].T.astype(np.float32))
    wqkv1 = _pack_qkv(a["L1_attn_W_Q"], a["L1_attn_W_K"], a["L1_attn_W_V"])
    wot1 = np.ascontiguousarray(a["L1_attn_W_O"].T.astype(np.float32))
    rc, rsn = _rope_tables(S)

    def bc(v):
        return np.ascontiguousarray(np.broadcast_to(np.asarray(v, dtype=np.float32)[None, :],
                                                    (128, D)))
    wkv = a["L0_ffn_Wkv"]
    kidx = []
    vidx = []
    for h in range(8):
        kidx.extend(range(h * 128, h * 128 + 64))
        vidx.extend(range(h * 128 + 64, h * 128 + 128))
    perm = []
    for j in range(4):
        perm.extend(kidx[2 * j * 64:(2 * j + 2) * 64])
    for j in range(4):
        perm.extend(vidx[2 * j * 64:(2 * j + 2) * 64])
    whkvT = np.ascontiguousarray(wkv[perm, :].T.astype(np.float32))
    nullkv = np.asarray(a["L0_ffn_null_kv"], dtype=np.float32)
    nullk_fm = np.ascontiguousarray(nullkv[0].T)
    nullv1 = np.concatenate([nullkv[1], np.ones((8, 1), np.float32)], axis=1)
    com = {"w1qT": w1qT, "w1scale": w1s, "w1bias": w1b,
           "wqkv_l0": wqkv0, "wot_l0": wot0, "wqkv_l1": wqkv1, "wot_l1": wot1,
           "ropec": rc, "ropes": rsn,
           "rtq_b": bc(a["L0_ffn_rt_q"][0]), "rtkv_b": bc(a["L0_ffn_rt_kv"][0]),
           "lnw_b": bc(a["L0_ffn_ln_w"]), "lnb_b": bc(a["L0_ffn_ln_b"]),
           "gamma_b": bc(a["L0_ffn_gamma"]), "nullq_b": bc(a["L0_ffn_null_q"]),
           "wlqkvT": np.ascontiguousarray(a["L0_ffn_Wqkv"].T.astype(np.float32)),
           "wloT": np.ascontiguousarray(a["L0_ffn_Wout_l"].T.astype(np.float32)),
           "whqT": np.ascontiguousarray(a["L0_ffn_Wq"].T.astype(np.float32)),
           "whkvT": whkvT,
           "whoT": np.ascontiguousarray(a["L0_ffn_Wout_h"].T.astype(np.float32)),
           "nullk_dup": np.ascontiguousarray(np.vstack([nullk_fm, nullk_fm])), "nullv1": nullv1}
    return [dict(com, x_in=x[b]) for b in range(8)]


def kernel(**inputs):
    nc = build_program(S=4096, want=("s1", "s2", "s3", "s4", "s5", "s6"))
    in_maps = make_in_maps(inputs, S=4096)
    res = run_bass_kernel_spmd(nc, in_maps, core_ids=list(range(8)))
    out = np.stack([res.results[b]["y_out"] for b in range(8)], axis=0)
    return out.astype(np.float32)


# version 16
# speedup vs baseline: 1.2157x; 1.2157x over previous
import contextlib
import os as _os
import numpy as np
import concourse.bass as bass
import concourse.mybir as mybir
from concourse.bass_utils import run_bass_kernel_spmd
import bass_rust
from concourse import library_config

RED = bass_rust.ReduceOp

F32 = mybir.dt.float32
BF16 = mybir.dt.bfloat16
I8 = mybir.dt.int8
U32 = mybir.dt.uint32
I32 = mybir.dt.int32
ALU = mybir.AluOpType
AF = mybir.ActivationFunctionType

D = 1024
NCH = D // 128
MAGIC = 12582912.0


class Ph:

    def __init__(self, sems, cnt):
        self.sems = sems
        self.cnt = cnt
        self.base = {k: cnt.get(k, 0) for k in sems}

    def inc(self, inst, name, n=1):
        inst.then_inc(self.sems[name], n)
        self.cnt[name] = self.cnt.get(name, 0) + n
        return self.cnt[name]

    def tok(self, name, k):
        return self.base[name] + k


def build_program(S=4096, want=("s4", "s5")):
    NT = S // 128
    NQS = S // 512
    nc = bass.Bass("TRN2", target_bir_lowering=False, debug=False)

    x_in = nc.dram_tensor("x_in", [S, D], F32, kind="ExternalInput").ap()
    y_out = nc.dram_tensor("y_out", [S, D], F32, kind="ExternalOutput").ap()
    w1qT = nc.dram_tensor("w1qT", [D, D], I8, kind="ExternalInput").ap()
    w1scale = nc.dram_tensor("w1scale", [128, D], F32, kind="ExternalInput").ap()
    w1bias = nc.dram_tensor("w1bias", [128, D], F32, kind="ExternalInput").ap()
    wqkv_l0 = nc.dram_tensor("wqkv_l0", [8, D, 384], F32, kind="ExternalInput").ap()
    wot_l0 = nc.dram_tensor("wot_l0", [D, D], F32, kind="ExternalInput").ap()
    wqkv_l1 = nc.dram_tensor("wqkv_l1", [8, D, 384], F32, kind="ExternalInput").ap()
    wot_l1 = nc.dram_tensor("wot_l1", [D, D], F32, kind="ExternalInput").ap()
    ropec = nc.dram_tensor("ropec", [S, 64], F32, kind="ExternalInput").ap()
    ropes = nc.dram_tensor("ropes", [S, 64], F32, kind="ExternalInput").ap()
    lnT_d = nc.dram_tensor("lnT_d", [NCH, 128, S + 128], BF16, kind="Internal").ap()
    aoT_d = nc.dram_tensor("aoT_d", [D, S], BF16, kind="Internal").ap()
    xsnap_d = nc.dram_tensor("xsnap_d", [S, D], F32, kind="Internal").ap()
    rtq_b = nc.dram_tensor("rtq_b", [128, D], F32, kind="ExternalInput").ap()
    rtkv_b = nc.dram_tensor("rtkv_b", [128, D], F32, kind="ExternalInput").ap()
    lnw_b = nc.dram_tensor("lnw_b", [128, D], F32, kind="ExternalInput").ap()
    lnb_b = nc.dram_tensor("lnb_b", [128, D], F32, kind="ExternalInput").ap()
    gamma_b = nc.dram_tensor("gamma_b", [128, D], F32, kind="ExternalInput").ap()
    nullq_b = nc.dram_tensor("nullq_b", [128, D], F32, kind="ExternalInput").ap()
    wlqkvT = nc.dram_tensor("wlqkvT", [D, 1536], F32, kind="ExternalInput").ap()
    wloT = nc.dram_tensor("wloT", [512, D], F32, kind="ExternalInput").ap()
    whqT = nc.dram_tensor("whqT", [D, 512], F32, kind="ExternalInput").ap()
    whkvT = nc.dram_tensor("whkvT", [D, D], F32, kind="ExternalInput").ap()
    whoT = nc.dram_tensor("whoT", [512, D], F32, kind="ExternalInput").ap()
    nullk_dup = nc.dram_tensor("nullk_dup", [128, 8], F32, kind="ExternalInput").ap()
    nullv1 = nc.dram_tensor("nullv1", [8, 65], F32, kind="ExternalInput").ap()
    heavy_d = nc.dram_tensor("heavy_d", [513, D], F32, kind="Internal").ap()

    stack = contextlib.ExitStack()
    SEM_NAMES = ["s_in", "s_w", "s_aux", "s_vq", "s_tr", "s_te", "s_mm",
                 "s_me", "s_xd", "s_g", "s_out", "s_v1", "s_s1",
                 "a_vx", "a_tt", "a_se", "a_dw0", "a_dw1", "b_dl0", "b_dl1",
                 "b_gw", "b_mm", "b_qe", "b_vr", "b_tq", "b_fe", "b_rp0", "b_rp1",
                 "c_sc", "c_ex", "c_mk", "c_av", "c_rc", "c_ae", "c_tr", "c_te",
                 "c_dw", "d_dl0", "d_dl1", "d_mm", "d_xd", "x_sn",
                 "p_vx", "p_sv", "p_tt", "p_se", "p_dw0", "p_dw1", "p_cw",
                 "r_v", "r_s", "r_g", "r_t", "r_dw", "g_g", "h_v", "h_s", "h_t",
                 "h_dl", "h_dw", "g_c", "h_w0", "h_w1", "l_dl0", "l_dl1", "l_gw", "l_mm", "l_qe", "l_sc",
                 "l_ex", "l_mk", "l_av", "l_rc", "l_ae", "l_tr", "l_xd", "f_g0", "f_g1", "f_xd"]
    sems = {n: stack.enter_context(nc.semaphore(n)) for n in SEM_NAMES}
    cnt = {}

    xres = stack.enter_context(nc.sbuf_tensor("xres", [128, NT, D], F32))
    ident = stack.enter_context(nc.sbuf_tensor("ident", [128, 128], BF16))
    trimask = stack.enter_context(nc.sbuf_tensor("trimask", [128, 128], BF16))

    ph0 = Ph(sems, cnt)
    with nc.Block() as block:
        @block.sync
        def _(sync):
            for t in range(NT):
                ph0.inc(sync.dma_start(out=xres[:, t, :], in_=x_in[t * 128:(t + 1) * 128, :]),
                        "s_in", 16)

        @block.gpsimd
        def _(g):
            g.memset(ident[:], 0.0)
            g.memset(trimask[:], 1.0)
            g.drain()
            g.affine_select(out=ident[:], in_=ident[:], compare_op=ALU.not_equal,
                            fill=1.0, base=0, pattern=[[-1, 128]], channel_multiplier=1)
            ph0.inc(g.affine_select(out=trimask[:], in_=trimask[:], compare_op=ALU.is_ge,
                                    fill=0.0, base=0, pattern=[[1, 128]], channel_multiplier=-1),
                    "s_aux", 1)
    X_READY = ("s_in", 16 * NT)

    def emit_ln_stats(block, ph, stats, mv, rr, nmr, xwait):
        @block.vector
        def _(v):
            v.wait_ge(sems[xwait[0]], xwait[1])
            for t in range(NT):
                v.bn_stats(stats[:, 12 * t:12 * t + 6], xres[:, t, 0:512])
                v.bn_stats(stats[:, 12 * t + 6:12 * t + 12], xres[:, t, 512:1024])
            v.drain()
            for t in range(NT):
                v.bn_aggr(mv[:, 2 * t:2 * (t + 1)], stats[:, 12 * t:12 * (t + 1)])
            v.drain()
            mvr = mv.ap().rearrange("p (t two) -> p t two", two=2)
            ph.inc(v.tensor_scalar(rr[:], mvr[:, :, 1:2], 1e-5, None, op0=ALU.add), "s_v1", 1)

        @block.scalar
        def _(s):
            s.wait_ge(sems["s_v1"], ph.tok("s_v1", 1))
            s.activation(out=rr[:], in_=rr[:], func=AF.Ln)
            s.drain()
            ph.inc(s.activation(out=rr[:], in_=rr[:], func=AF.Exp, scale=-0.5), "s_s1", 1)

        @block.vector
        def _(v):
            v.wait_ge(sems["s_s1"], ph.tok("s_s1", 1))
            mvr = mv.ap().rearrange("p (t two) -> p t two", two=2)
            v.scalar_tensor_tensor(out=nmr[:], in0=mvr[:, :, 0:1], scalar=-1.0,
                                   in1=rr[:], op0=ALU.mult, op1=ALU.mult)
            ph.inc(v.drain(), "s_v1", 1)

    def emit_l1_ffn(tag, pre):
        ph = Ph(sems, cnt)
        with (
            nc.sbuf_tensor(f"{tag}_bc", [128, 2 * D], F32) as bcast,
            nc.sbuf_tensor(f"{tag}_wq", [128, NCH * D], BF16) as wq,
            nc.sbuf_tensor(f"{tag}_stats", [128, NT * 12], F32) as stats,
            nc.sbuf_tensor(f"{tag}_mv", [128, NT * 2], F32) as mv,
            nc.sbuf_tensor(f"{tag}_r", [128, NT], F32) as rr,
            nc.sbuf_tensor(f"{tag}_nmr", [128, NT], F32) as nmr,
            nc.sbuf_tensor(f"{tag}_xs", [128, NT], F32) as xs,
            nc.sbuf_tensor(f"{tag}_rxs", [128, NT], F32) as rxs,
            nc.sbuf_tensor(f"{tag}_xm", [128, 2 * D], F32) as xm,
            nc.sbuf_tensor(f"{tag}_xqb", [128, 2 * D], BF16) as xqb,
            nc.sbuf_tensor(f"{tag}_xqT", [128, 2 * D], BF16) as xqT,
            nc.sbuf_tensor(f"{tag}_t2", [128, 2 * D], F32) as t2,
            nc.psum_tensor(f"{tag}_ptr", [128, 2 * D], BF16) as ptr,
            nc.psum_tensor(f"{tag}_pmm", [128, 2 * D], F32) as pmm,
            nc.Block() as block,
        ):
            @block.gpsimd
            def _(g):
                for c in range(NCH):
                    ph.inc(g.dma_start(out=wq[:, c * D:(c + 1) * D],
                                       in_=w1qT[c * 128:(c + 1) * 128, :]), "s_g", 16)

            @block.sync
            def _(sy):
                sy.wait_ge(sems[pre[0]], pre[1])
                ph.inc(sy.dma_start(out=bcast[:, 0:D], in_=w1scale[:]), "s_w", 16)
                ph.inc(sy.dma_start(out=bcast[:, D:2 * D], in_=w1bias[:]), "s_w", 16)

            emit_ln_stats(block, ph, stats, mv, rr, nmr, xwait=pre)

            @block.vector
            def _(v):
                v.wait_ge(sems["s_w"], ph.tok("s_w", 32))

                def dequant(j):
                    oj = (j % 2) * D
                    v.wait_ge(sems["s_mm"], ph.tok("s_mm", j + 1))
                    ph.inc(v.tensor_scalar(t2[:, oj:oj + D], pmm[:, oj:oj + D], xs[:, j:j + 1],
                                           0.5, op0=ALU.mult, op1=ALU.mult), "s_me", 1)
                    v.drain()
                    v.tensor_tensor(t2[:, oj:oj + D], t2[:, oj:oj + D], bcast[:, 0:D], op=ALU.mult)
                    v.drain()
                    v.tensor_tensor(xres[:, j, :], xres[:, j, :], t2[:, oj:oj + D], op=ALU.add)
                    v.drain()
                    ph.inc(v.scalar_tensor_tensor(out=xres[:, j, :], in0=bcast[:, D:2 * D],
                                                  scalar=0.5, in1=xres[:, j, :],
                                                  op0=ALU.mult, op1=ALU.add), "s_xd", 1)

                for t in range(NT + 2):
                    if t >= 2:
                        dequant(t - 2)
                    if t >= NT:
                        continue
                    o = (t % 2) * D
                    if t >= 2:
                        v.wait_ge(sems["s_tr"], ph.tok("s_tr", t - 1))
                    v.tensor_scalar(xm[:, o:o + D], xres[:, t, :], rr[:, t:t + 1],
                                    nmr[:, t:t + 1], op0=ALU.mult, op1=ALU.add)
                    v.drain()
                    v.reduce_max(xs[:, t:t + 1], xm[:, o:o + D], axis=mybir.AxisListType.X,
                                 apply_absolute_value=True)
                    v.drain()
                    v.tensor_scalar(xs[:, t:t + 1], xs[:, t:t + 1], 1e-6, 1.0 / 127.0,
                                    op0=ALU.max, op1=ALU.mult)
                    v.drain()
                    v.reciprocal(rxs[:, t:t + 1], xs[:, t:t + 1])
                    v.drain()
                    v.tensor_scalar(xm[:, o:o + D], xm[:, o:o + D], rxs[:, t:t + 1], MAGIC,
                                    op0=ALU.mult, op1=ALU.add)
                    v.drain()
                    ph.inc(v.tensor_scalar(xqb[:, o:o + D], xm[:, o:o + D], -MAGIC, None,
                                           op0=ALU.add), "s_vq", 1)

            @block.tensor
            def _(te):
                te.wait_ge(sems["s_aux"], 1)
                te.wait_ge(sems["s_g"], ph.tok("s_g", 16 * NCH))
                for t in range(NT):
                    o = (t % 2) * D
                    te.wait_ge(sems["s_vq"], ph.tok("s_vq", t + 1))
                    if t >= 2:
                        te.wait_ge(sems["s_te"], ph.tok("s_te", t - 1))
                    for c in range(NCH):
                        i = te.transpose(ptr[:, o + c * 128:o + (c + 1) * 128],
                                         xqb[:, o + c * 128:o + (c + 1) * 128], ident[:])
                    ph.inc(i, "s_tr", 1)
                    te.wait_ge(sems["s_te"], ph.tok("s_te", t + 1))
                    if t >= 2:
                        te.wait_ge(sems["s_me"], ph.tok("s_me", t - 1))
                    for c in range(NCH):
                        for h in range(2):
                            i = te.matmul(pmm[:, o + h * 512:o + (h + 1) * 512],
                                          xqT[:, o + c * 128:o + (c + 1) * 128],
                                          wq[:, c * D + h * 512:c * D + (h + 1) * 512],
                                          start=(c == 0), stop=(c == NCH - 1))
                    ph.inc(i, "s_mm", 1)

            @block.scalar
            def _(s):
                for t in range(NT):
                    o = (t % 2) * D
                    s.wait_ge(sems["s_tr"], ph.tok("s_tr", t + 1))
                    ph.inc(s.copy(xqT[:, o:o + D], ptr[:, o:o + D]), "s_te", 1)
        return ("s_xd", ph.base["s_xd"])

    def emit_dense_attn(tag, pre, wqkv_d, wot_d, rope, snapshot_x):
        ph = Ph(sems, cnt)
        with (
            nc.sbuf_tensor(f"{tag}_stats", [128, NT * 12], F32) as stats,
            nc.sbuf_tensor(f"{tag}_mv", [128, NT * 2], F32) as mv,
            nc.sbuf_tensor(f"{tag}_r", [128, NT], F32) as rr,
            nc.sbuf_tensor(f"{tag}_nmr", [128, NT], F32) as nmr,
        ):
            with (
                nc.sbuf_tensor(f"{tag}_xmb", [128, 2 * D], BF16) as xmb,
                nc.sbuf_tensor(f"{tag}_lnt", [128, 2 * D], BF16) as lnt,
                nc.psum_tensor(f"{tag}_ptA", [128, 2 * D], BF16) as ptA,
                nc.Block() as block,
            ):
                emit_ln_stats(block, ph, stats, mv, rr, nmr, xwait=pre)

                @block.vector
                def _(v):
                    for t in range(NT):
                        o = (t % 2) * D
                        if t >= 2:
                            v.wait_ge(sems["a_tt"], ph.tok("a_tt", t - 1))
                        ph.inc(v.tensor_scalar(xmb[:, o:o + D], xres[:, t, :], rr[:, t:t + 1],
                                               nmr[:, t:t + 1], op0=ALU.mult, op1=ALU.add),
                               "a_vx", 1)

                @block.tensor
                def _(te):
                    te.wait_ge(sems["s_aux"], 1)
                    for t in range(NT):
                        o = (t % 2) * D
                        te.wait_ge(sems["a_vx"], ph.tok("a_vx", t + 1))
                        if t >= 2:
                            te.wait_ge(sems["a_se"], ph.tok("a_se", t - 1))
                        for c in range(NCH):
                            i = te.transpose(ptA[:, o + c * 128:o + (c + 1) * 128],
                                             xmb[:, o + c * 128:o + (c + 1) * 128], ident[:])
                        ph.inc(i, "a_tt", 1)

                @block.scalar
                def _(s):
                    for t in range(NT):
                        o = (t % 2) * D
                        s.wait_ge(sems["a_tt"], ph.tok("a_tt", t + 1))
                        if t >= 2:
                            sd = "a_dw0" if t % 2 == 0 else "a_dw1"
                            s.wait_ge(sems[sd], ph.tok(sd, 16 * (t // 2)))
                        ph.inc(s.copy(lnt[:, o:o + D], ptA[:, o:o + D]), "a_se", 1)

                @block.sync
                def _(sy):
                    for t in range(NT):
                        o = (t % 2) * D
                        sy.wait_ge(sems["a_se"], ph.tok("a_se", t + 1))
                        dst = lnT_d[:, :, 64 + t * 128:64 + (t + 1) * 128].rearrange("c p j -> p c j")
                        srcv = lnt[:, o:o + D].rearrange("p (c j) -> p c j", j=128)
                        ph.inc(sy.dma_start(out=dst, in_=srcv),
                               "a_dw0" if t % 2 == 0 else "a_dw1", 16)

            with contextlib.ExitStack() as bstk:
                qfm = bstk.enter_context(nc.sbuf_tensor(f"{tag}_qfm", [128, S], BF16))
                kfm = bstk.enter_context(nc.sbuf_tensor(f"{tag}_kfm", [128, S], BF16))
                vt = bstk.enter_context(nc.sbuf_tensor(f"{tag}_vt", [128, NT, 2, 65], BF16))
                lntb = bstk.enter_context(nc.sbuf_tensor(f"{tag}_lntb", [128, 2, NCH, 128], BF16))
                wsb = bstk.enter_context(nc.sbuf_tensor(f"{tag}_wsb", [128, NCH, 384], BF16))
                qk32 = bstk.enter_context(nc.sbuf_tensor(f"{tag}_qk32", [128, 2, 256], F32))
                rv1 = bstk.enter_context(nc.sbuf_tensor(f"{tag}_rv1", [128, 256], F32))
                rv2 = bstk.enter_context(nc.sbuf_tensor(f"{tag}_rv2", [128, 256], F32))
                qkb = bstk.enter_context(nc.sbuf_tensor(f"{tag}_qkb", [128, 2, 256], BF16))
                rp = bstk.enter_context(nc.sbuf_tensor(f"{tag}_rp", [128, 2, 2, 64], F32))
                at = bstk.enter_context(nc.sbuf_tensor(f"{tag}_at", [128, 2, 2, 512], BF16))
                rs = bstk.enter_context(nc.sbuf_tensor(f"{tag}_rs", [128, 8], F32))
                aoq = bstk.enter_context(nc.sbuf_tensor(f"{tag}_aoq", [128, 4, 128], BF16))
                aot = bstk.enter_context(nc.sbuf_tensor(f"{tag}_aot", [128, 512], BF16))
                ptB = bstk.enter_context(nc.psum_tensor(f"{tag}_ptB", [128, 2, 1024], BF16))
                psS = bstk.enter_context(nc.psum_tensor(f"{tag}_psS", [128, 2, 2, 512], F32))
                psAV = bstk.enter_context(nc.psum_tensor(f"{tag}_psAV", [128, 2, 512], F32))
                ph_i = Ph(sems, cnt)
                with nc.Block() as block:
                    @block.gpsimd
                    def _(g_):
                        g_.wait_ge(sems["a_dw0"], ph_i.base["a_dw0"])
                        g_.wait_ge(sems["a_dw1"], ph_i.base["a_dw1"])
                        ph_i.inc(g_.memset(vt[:, :, :, 64:65], 1.0), "s_aux", 1)
                ones_tok = cnt["s_aux"]

                NKC = [4 * q + 4 for q in range(NQS)]
                for g in range(8):
                    ph_g = Ph(sems, cnt)
                    with nc.Block() as block:
                        @block.gpsimd
                        def _(g_):
                            g_.wait_ge(sems["b_mm"], ph_g.base["b_mm"])
                            ph_g.inc(g_.dma_start(out=wsb[:, :, :],
                                                  in_=wqkv_d[g].rearrange("(c p) n -> p c n", p=128)),
                                     "b_gw", 16)

                        @block.sync
                        def _(sy):
                            if g == 0:
                                sy.wait_ge(sems["a_dw0"], ph.tok("a_dw0", 16 * (NT // 2)))
                                sy.wait_ge(sems["a_dw1"], ph.tok("a_dw1", 16 * (NT // 2)))
                            for t in range(NT):
                                tb = t % 2
                                sl = "b_dl0" if tb == 0 else "b_dl1"
                                sr = "b_rp0" if tb == 0 else "b_rp1"
                                if t >= 2:
                                    sy.wait_ge(sems["b_mm"], ph_g.tok("b_mm", t - 1))
                                    if rope:
                                        sy.wait_ge(sems["b_vr"], ph_g.tok("b_vr", t - 1))
                                src = lnT_d[:, :, 64 + t * 128:64 + (t + 1) * 128].rearrange("c p j -> p c j")
                                ph_g.inc(sy.dma_start(out=lntb[:, tb, :, :], in_=src), sl, 16)
                                if rope:
                                    ph_g.inc(sy.dma_start(out=rp[:, tb, 0, :],
                                                          in_=ropec[t * 128:(t + 1) * 128, :]),
                                             sr, 16)
                                    ph_g.inc(sy.dma_start(out=rp[:, tb, 1, :],
                                                          in_=ropes[t * 128:(t + 1) * 128, :]),
                                             sr, 16)

                        @block.tensor
                        def _(te):
                            te.wait_ge(sems["b_gw"], ph_g.tok("b_gw", 16))
                            te.wait_ge(sems["s_aux"], ones_tok)
                            for t in range(NT):
                                tb = t % 2
                                sl = "b_dl0" if tb == 0 else "b_dl1"
                                te.wait_ge(sems[sl], ph_g.tok(sl, 16 * (t // 2 + 1)))
                                if t >= 2:
                                    te.wait_ge(sems["b_qe"], ph_g.tok("b_qe", 2 * (t - 1)))
                                for c in range(NCH):
                                    i = te.matmul(psS[:, tb, 0, 0:384], lntb[:, tb, c, :],
                                                  wsb[:, c, :], start=(c == 0), stop=(c == NCH - 1))
                                ph_g.inc(i, "b_mm", 1)
                                if rope:
                                    te.wait_ge(sems["b_vr"], ph_g.tok("b_vr", t + 1))
                                else:
                                    te.wait_ge(sems["b_qe"], ph_g.tok("b_qe", 2 * t + 1))
                                if t >= 2:
                                    te.wait_ge(sems["b_fe"], ph_g.tok("b_fe", t - 1))
                                i = te.transpose(ptB[:, tb, 0:128], qkb[:, tb, 0:128], ident[:])
                                i = te.transpose(ptB[:, tb, 128:256], qkb[:, tb, 128:256], ident[:])
                                ph_g.inc(i, "b_tq", 1)

                        @block.scalar
                        def _(s):
                            if g == 0:
                                s.wait_ge(sems["a_dw0"], ph_g.base["a_dw0"])
                                s.wait_ge(sems["a_dw1"], ph_g.base["a_dw1"])
                            if g > 0:
                                s.wait_ge(sems["c_av"], ph_g.base["c_av"])
                            for t in range(NT):
                                tb = t % 2
                                s.wait_ge(sems["b_mm"], ph_g.tok("b_mm", t + 1))
                                if rope:
                                    if t >= 2:
                                        s.wait_ge(sems["b_vr"], ph_g.tok("b_vr", t - 1))
                                    i = s.copy(qk32[:, tb, :], psS[:, tb, 0, 0:256])
                                else:
                                    if t >= 2:
                                        s.wait_ge(sems["b_tq"], ph_g.tok("b_tq", t - 1))
                                    i = s.copy(qkb[:, tb, :], psS[:, tb, 0, 0:256])
                                ph_g.inc(i, "b_qe", 1)
                                ph_g.inc(s.copy(vt[:, t, :, 0:64], psS[:, tb, 0, 256:384]),
                                         "b_qe", 1)
                                s.wait_ge(sems["b_tq"], ph_g.tok("b_tq", t + 1))
                                i = s.copy(qfm[:, t * 128:(t + 1) * 128], ptB[:, tb, 0:128])
                                i = s.copy(kfm[:, t * 128:(t + 1) * 128], ptB[:, tb, 128:256])
                                ph_g.inc(i, "b_fe", 1)

                        if rope:
                            @block.vector
                            def _(v):
                                for t in range(NT):
                                    tb = t % 2
                                    v.wait_ge(sems["b_qe"], ph_g.tok("b_qe", 2 * t + 1))
                                    sr = "b_rp0" if tb == 0 else "b_rp1"
                                    v.wait_ge(sems[sr], ph_g.tok(sr, 32 * (t // 2 + 1)))
                                    if t >= 2:
                                        v.wait_ge(sems["b_tq"], ph_g.tok("b_tq", t - 1))
                                    qk3 = qk32[:, tb, :].rearrange("p (a n) -> p a n", n=64)
                                    cosb = rp[:, tb, 0:1, :].to_broadcast([128, 4, 64])
                                    sinb = rp[:, tb, 1:2, :].to_broadcast([128, 4, 64])
                                    v.tensor_tensor(rv1.ap().rearrange("p (a n) -> p a n", n=64),
                                                    qk3, cosb, op=ALU.mult)
                                    v.tensor_tensor(rv2.ap().rearrange("p (a n) -> p a n", n=64),
                                                    qk3, sinb, op=ALU.mult)
                                    v.drain()
                                    r1 = rv1.ap().rearrange("p (a h n) -> p a h n", h=2, n=32)
                                    r2 = rv2.ap().rearrange("p (a h n) -> p a h n", h=2, n=32)
                                    ob = qkb[:, tb, :].rearrange("p (a h n) -> p a h n", h=2, n=32)
                                    v.tensor_tensor(ob[:, :, 0:1, :], r1[:, :, 0:1, :],
                                                    r2[:, :, 1:2, :], op=ALU.subtract)
                                    i = v.tensor_tensor(ob[:, :, 1:2, :], r1[:, :, 1:2, :],
                                                        r2[:, :, 0:1, :], op=ALU.add)
                                    ph_g.inc(i, "b_vr", 1)

                    ph_a = Ph(sems, cnt)
                    n_mask = {}
                    cm = 0
                    idx0 = 0
                    av_at_qsec = {}
                    for q in range(NQS):
                        for kc in range(NKC[q]):
                            if kc >= 4 * q:
                                cm += 2
                            n_mask[idx0] = cm
                            idx0 += 1
                        av_at_qsec[q] = idx0

                    units = []
                    for q in range(NQS):
                        for kc in range(NKC[q]):
                            units.append((q, kc))
                    with nc.Block() as block:
                        @block.tensor
                        def _(te):
                            te.wait_ge(sems["b_fe"], ph_g.tok("b_fe", NT))
                            def emit_scores(idx):
                                q, kc = units[idx]
                                pp = idx % 2
                                j0 = max(0, kc - 4 * q)
                                qo = j0 * 128
                                if idx >= 2:
                                    te.wait_ge(sems["c_ex"], ph_a.tok("c_ex", 2 * (idx - 1)))
                                i = te.matmul(psS[:, pp, 0, qo:512],
                                              kfm[0:64, kc * 128:(kc + 1) * 128],
                                              qfm[0:64, q * 512 + qo:(q + 1) * 512],
                                              start=True, stop=True)
                                i = te.matmul(psS[:, pp, 1, qo:512],
                                              kfm[64:128, kc * 128:(kc + 1) * 128],
                                              qfm[64:128, q * 512 + qo:(q + 1) * 512],
                                              start=True, stop=True, tile_position=(64, 0))
                                ph_a.inc(i, "c_sc", 2)
                            def emit_av(idx):
                                q, kc = units[idx]
                                pp = idx % 2
                                j0 = max(0, kc - 4 * q)
                                te.wait_ge(sems["c_ex"], ph_a.tok("c_ex", 2 * (idx + 1)))
                                if kc >= 4 * q:
                                    te.wait_ge(sems["c_mk"], ph_a.tok("c_mk", n_mask[idx]))
                                for h in range(2):
                                    for j in range(j0, 4):
                                        i = te.matmul(psAV[:, h, j * 65:(j + 1) * 65],
                                                      at[:, pp, h, j * 128:(j + 1) * 128],
                                                      vt[:, kc, h, :],
                                                      start=(kc == 0 and j == 0),
                                                      stop=(kc == NKC[q] - 1 and j == 3),
                                                      skip_group_check=True)
                                ph_a.inc(i, "c_av", 1)
                                if kc == NKC[q] - 1:
                                    te.wait_ge(sems["c_ae"], ph_a.tok("c_ae", q + 1))
                                    for j in range(4):
                                        i = te.transpose(ptB[:, 1, j * 128:(j + 1) * 128],
                                                         aoq[:, j, :], ident[:])
                                    ph_a.inc(i, "c_tr", 1)
                            for u in range(len(units)):
                                emit_scores(u)
                                if u >= 1:
                                    emit_av(u - 1)
                            emit_av(len(units) - 1)

                        @block.scalar
                        def _(s):
                            idx = 0
                            for q in range(NQS):
                                for kc in range(NKC[q]):
                                    pp = idx % 2
                                    j0 = max(0, kc - 4 * q)
                                    qo = j0 * 128
                                    s.wait_ge(sems["c_sc"], ph_a.tok("c_sc", 2 * (idx + 1)))
                                    s.wait_ge(sems["c_av"], ph_a.tok("c_av", max(0, idx - 1)))
                                    i = s.activation(out=at[:, pp, 0, qo:512],
                                                     in_=psS[:, pp, 0, qo:512],
                                                     func=AF.Exp, scale=0.125)
                                    i = s.activation(out=at[:, pp, 1, qo:512],
                                                     in_=psS[:, pp, 1, qo:512],
                                                     func=AF.Exp, scale=0.125)
                                    ph_a.inc(i, "c_ex", 2)
                                    idx += 1
                                s.wait_ge(sems["c_rc"], ph_a.tok("c_rc", q + 1))
                                if q >= 1:
                                    s.wait_ge(sems["c_tr"], ph_a.tok("c_tr", q))
                                for h in range(2):
                                    for j in range(4):
                                        i = s.activation(out=aoq[:, j, h * 64:(h + 1) * 64],
                                                         in_=psAV[:, h, j * 65:j * 65 + 64],
                                                         func=AF.Identity,
                                                         scale=rs[:, h * 4 + j:h * 4 + j + 1])
                                ph_a.inc(i, "c_ae", 1)
                                s.wait_ge(sems["c_tr"], ph_a.tok("c_tr", q + 1))
                                if ph_a.tok("c_dw", 16 * q) > 0:
                                    s.wait_ge(sems["c_dw"], ph_a.tok("c_dw", 16 * q))
                                ph_a.inc(s.copy(aot[:], ptB[:, 1, 0:512]), "c_te", 1)

                        @block.vector
                        def _(v):
                            idx = 0
                            for q in range(NQS):
                                for kc in range(NKC[q]):
                                    pp = idx % 2
                                    if kc >= 4 * q:
                                        j = kc - 4 * q
                                        v.wait_ge(sems["c_ex"], ph_a.tok("c_ex", 2 * (idx + 1)))
                                        v.tensor_tensor(at[:, pp, 0, j * 128:(j + 1) * 128],
                                                        at[:, pp, 0, j * 128:(j + 1) * 128],
                                                        trimask[:], op=ALU.mult)
                                        i = v.tensor_tensor(at[:, pp, 1, j * 128:(j + 1) * 128],
                                                            at[:, pp, 1, j * 128:(j + 1) * 128],
                                                            trimask[:], op=ALU.mult)
                                        ph_a.inc(i, "c_mk", 2)
                                    idx += 1
                                v.wait_ge(sems["c_av"], ph_a.tok("c_av", av_at_qsec[q]))
                                if q >= 1:
                                    v.wait_ge(sems["c_ae"], ph_a.tok("c_ae", q))
                                i = v.reciprocal(
                                    rs.ap().rearrange("p (h j) -> p h j", j=4),
                                    psAV[:, :, 0:260].rearrange("p h (j n) -> p h j n", n=65)[:, :, :, 64:65])
                                ph_a.inc(i, "c_rc", 1)

                        @block.sync
                        def _(sy):
                            for q in range(NQS):
                                sy.wait_ge(sems["c_te"], ph_a.tok("c_te", q + 1))
                                ph_a.inc(sy.dma_start(out=aoT_d[g * 128:(g + 1) * 128,
                                                               q * 512:(q + 1) * 512],
                                                      in_=aot[:]), "c_dw", 16)

            ph_c = Ph(sems, cnt)
            with (
                nc.sbuf_tensor(f"{tag}_wot", [128, NCH, D], BF16) as wotb,
                nc.sbuf_tensor(f"{tag}_aotb", [128, 2, NCH, 128], BF16) as aotb,
                nc.psum_tensor(f"{tag}_psO", [128, 2, D], F32) as psO,
                nc.Block() as block,
            ):
                @block.gpsimd
                def _(g_):
                    g_.wait_ge(sems["c_av"], ph_c.base["c_av"])
                    ph_c.inc(g_.dma_start(out=wotb[:, :, :],
                                          in_=wot_d.rearrange("(c p) n -> p c n", p=128)),
                             "b_gw", 16)

                @block.sync
                def _(sy):
                    sy.wait_ge(sems["c_dw"], ph_c.base["c_dw"])
                    for t in range(NT):
                        tb = t % 2
                        sl = "d_dl0" if tb == 0 else "d_dl1"
                        if t >= 2:
                            sy.wait_ge(sems["d_mm"], ph_c.tok("d_mm", t - 1))
                        src = aoT_d[:, t * 128:(t + 1) * 128].rearrange("(c p) j -> p c j", p=128)
                        ph_c.inc(sy.dma_start(out=aotb[:, tb, :, :], in_=src), sl, 16)

                @block.tensor
                def _(te):
                    te.wait_ge(sems["b_gw"], ph_c.tok("b_gw", 16))
                    for t in range(NT):
                        tb = t % 2
                        sl = "d_dl0" if tb == 0 else "d_dl1"
                        te.wait_ge(sems[sl], ph_c.tok(sl, 16 * (t // 2 + 1)))
                        if t >= 2:
                            te.wait_ge(sems["d_xd"], ph_c.tok("d_xd", t - 1))
                        for c in range(NCH):
                            te.matmul(psO[:, tb, 0:512], aotb[:, tb, c, :], wotb[:, c, 0:512],
                                      start=(c == 0), stop=(c == NCH - 1))
                            i = te.matmul(psO[:, tb, 512:1024], aotb[:, tb, c, :],
                                          wotb[:, c, 512:1024],
                                          start=(c == 0), stop=(c == NCH - 1))
                        ph_c.inc(i, "d_mm", 1)

                @block.vector
                def _(v):
                    for t in range(NT):
                        tb = t % 2
                        v.wait_ge(sems["d_mm"], ph_c.tok("d_mm", t + 1))
                        ph_c.inc(v.tensor_tensor(xres[:, t, :], xres[:, t, :],
                                                 psO[:, tb, :], op=ALU.add), "d_xd", 1)

                @block.sync
                def _(sy):
                    if snapshot_x:
                        for t in range(NT):
                            sy.wait_ge(sems["d_xd"], ph_c.tok("d_xd", t + 1))
                            ph_c.inc(sy.dma_start(out=xsnap_d[t * 128:(t + 1) * 128, :],
                                                  in_=xres[:, t, :]), "x_sn", 16)

        return ("d_xd", ph_c.base["d_xd"])


    NKV = 1024
    NQ = 512

    def emit_l0_ffn(tag, pre, xsrc, xwait2=None):
        import math
        n_f = np.float32(S)
        logk_q = np.float32(np.log(np.float32(min(NQ * 1.125, float(S)))))
        logk_kv = np.float32(np.log(np.float32(min(NKV * 1.125, float(S)))))
        log_n = np.float32(np.log(n_f))
        na_init = {0: float(np.float32(log_n - logk_q)), 1: float(np.float32(log_n - logk_kv))}
        logk = {0: float(logk_q), 1: float(logk_kv)}
        kk = {0: float(NQ), 1: float(NKV)}
        RSUB = {0: NQ // 128, 1: NKV // 128}

        ph = Ph(sems, cnt)
        with contextlib.ExitStack() as pstk:
            sv = pstk.enter_context(nc.sbuf_tensor(f"{tag}_sv", [128, 2, NT], F32))
            gIv = pstk.enter_context(nc.sbuf_tensor(f"{tag}_gI", [128, NT], U32))
            idxu = pstk.enter_context(nc.sbuf_tensor(f"{tag}_idxu", [128, 12], U32))

            with (
                nc.sbuf_tensor(f"{tag}_stats", [128, NT * 12], F32) as stats,
                nc.sbuf_tensor(f"{tag}_mv", [128, NT * 2], F32) as mv,
                nc.sbuf_tensor(f"{tag}_r", [128, NT], F32) as rr,
                nc.sbuf_tensor(f"{tag}_nmr", [128, NT], F32) as nmr,
                nc.sbuf_tensor(f"{tag}_cst", [128, 4 * D], F32) as cst,
                nc.sbuf_tensor(f"{tag}_xmf", [128, 2 * D], F32) as xmf,
                nc.sbuf_tensor(f"{tag}_sd", [128, D], F32) as sdump,
                nc.sbuf_tensor(f"{tag}_l2f", [128, D], F32) as l2f,
                nc.sbuf_tensor(f"{tag}_xmb", [128, 2 * D], BF16) as xmb,
                nc.sbuf_tensor(f"{tag}_lnt", [128, 2 * D], BF16) as lnt,
                nc.sbuf_tensor(f"{tag}_zpad", [128, NCH * 64], BF16) as zpad,
                nc.psum_tensor(f"{tag}_ptA", [128, 2 * D], BF16) as ptA,
                nc.Block() as block,
            ):
                emit_ln_stats(block, ph, stats, mv, rr, nmr, xwait=pre)

                @block.sync
                def _(sy):
                    sy.wait_ge(sems[pre[0]], pre[1])
                    ph.inc(sy.dma_start(out=cst[:, 0:D], in_=rtq_b[:]), "s_w", 16)
                    ph.inc(sy.dma_start(out=cst[:, D:2 * D], in_=rtkv_b[:]), "s_w", 16)
                    ph.inc(sy.dma_start(out=cst[:, 2 * D:3 * D], in_=lnw_b[:]), "s_w", 16)
                    ph.inc(sy.dma_start(out=cst[:, 3 * D:4 * D], in_=lnb_b[:]), "s_w", 16)

                if True:
                    @block.gpsimd
                    def _(g_):
                        g_.wait_ge(sems[pre[0]], pre[1])
                        ph.inc(g_.memset(zpad[:], 0.0), "g_c", 1)

                    @block.sync
                    def _(sy):
                        sy.wait_ge(sems["g_c"], ph.tok("g_c", 1))
                        ph.inc(sy.dma_start(out=lnT_d[:, :, 0:64].rearrange("c p j -> p c j"),
                                            in_=zpad.ap().rearrange("p (c j) -> p c j", j=64)),
                               "x_sn", 16)
                        ph.inc(sy.dma_start(
                            out=lnT_d[:, :, S + 64:S + 128].rearrange("c p j -> p c j"),
                            in_=zpad.ap().rearrange("p (c j) -> p c j", j=64)), "x_sn", 16)

                @block.vector
                def _(v):
                    v.wait_ge(sems["s_w"], ph.tok("s_w", 64))
                    for t in range(NT):
                        o = (t % 2) * D
                        if t >= 2:
                            v.wait_ge(sems["p_tt"], ph.tok("p_tt", t - 1))
                        v.tensor_scalar(xmf[:, o:o + D], xres[:, t, :], rr[:, t:t + 1],
                                        nmr[:, t:t + 1], op0=ALU.mult, op1=ALU.add)
                        v.drain()
                        if True:
                            v.tensor_tensor(sdump[:], xmf[:, o:o + D], cst[:, 0:D], op=ALU.mult)
                            v.drain()
                            v.tensor_reduce(sv[:, 0, t:t + 1], sdump[:],
                                            axis=mybir.AxisListType.X, op=ALU.add)
                            v.drain()
                            v.tensor_tensor(sdump[:], xmf[:, o:o + D], cst[:, D:2 * D],
                                            op=ALU.mult)
                            v.drain()
                            v.tensor_reduce(sv[:, 1, t:t + 1], sdump[:],
                                            axis=mybir.AxisListType.X, op=ALU.add)
                        ph.inc(v.drain(), "p_sv", 1)
                        v.tensor_tensor(l2f[:], xmf[:, o:o + D], cst[:, 2 * D:3 * D], op=ALU.mult)
                        v.drain()
                        ph.inc(v.tensor_tensor(xmb[:, o:o + D], l2f[:], cst[:, 3 * D:4 * D],
                                               op=ALU.add), "p_vx", 1)

                @block.tensor
                def _(te):
                    te.wait_ge(sems["s_aux"], 1)
                    for t in range(NT):
                        o = (t % 2) * D
                        te.wait_ge(sems["p_vx"], ph.tok("p_vx", t + 1))
                        if t >= 2:
                            te.wait_ge(sems["p_se"], ph.tok("p_se", t - 1))
                        for c in range(NCH):
                            i = te.transpose(ptA[:, o + c * 128:o + (c + 1) * 128],
                                             xmb[:, o + c * 128:o + (c + 1) * 128], ident[:])
                        ph.inc(i, "p_tt", 1)

                @block.scalar
                def _(s):
                    for t in range(NT):
                        o = (t % 2) * D
                        s.wait_ge(sems["p_tt"], ph.tok("p_tt", t + 1))
                        if t >= 2:
                            sd = "p_dw0" if t % 2 == 0 else "p_dw1"
                            s.wait_ge(sems[sd], ph.tok(sd, 16 * (t // 2)))
                        ph.inc(s.copy(lnt[:, o:o + D], ptA[:, o:o + D]), "p_se", 1)

                @block.sync
                def _(sy):
                    for t in range(NT):
                        o = (t % 2) * D
                        sy.wait_ge(sems["p_se"], ph.tok("p_se", t + 1))
                        if False:
                            dst = xsnap_d[t * 128:(t + 1) * 128, 0:512]
                            srcv = lnt[:, o:o + 1024].rearrange("p (c j) -> p (c j)", j=128)[:, 0:512]
                            ph.inc(sy.dma_start(out=xsnap_d[t * 128:(t + 1) * 128, 0:256]
                                                .bitcast(BF16) if False else
                                                lnT_d[0, :, 64 + t * 128:64 + (t + 1) * 128],
                                                in_=lnt[:, o:o + 128]),
                                   "p_dw0" if t % 2 == 0 else "p_dw1", 16)
                        else:
                            dst = lnT_d[:, :, 64 + t * 128:64 + (t + 1) * 128].rearrange("c p j -> p c j")
                            srcv = lnt[:, o:o + D].rearrange("p (c j) -> p c j", j=128)
                            ph.inc(sy.dma_start(out=dst, in_=srcv),
                                   "p_dw0" if t % 2 == 0 else "p_dw1", 16)

            ph2 = Ph(sems, cnt)
            with contextlib.ExitStack() as rstk:
                def sb(name, shape, dt=F32):
                    return rstk.enter_context(nc.sbuf_tensor(f"{tag}_{name}", shape, dt))
                na = sb("na", [128, 2]); ms = sb("ms", [128, 2]); sh = sb("sh", [128, 2])
                red = sb("red", [128, 2]); lred = sb("lred", [128, 2])
                tbuf = sb("tbuf", [128, 2, NT]); ebuf = sb("ebuf", [128, 2, NT])
                stil = sb("stil", [128, 2, NT])
                lo = sb("lo", [128, 2]); hi = sb("hi", [128, 2]); vv = sb("vv", [128, 2])
                cdump = sb("cd", [128, NT]); cntb = sb("cnt", [128, 2])
                cond8 = sb("c8", [128, 2], mybir.dt.uint8)
                ncond8 = sb("nc8", [128, 2], mybir.dt.uint8)
                vstar = sb("vst", [128, 2]); nhi = sb("nhi", [128, 2])
                mhi = sb("mhi", [128, 2, NT]); meq = sb("meq", [128, 2, NT])
                m8 = sb("m8", [128, 2, NT], mybir.dt.uint8)
                neginf = sb("ninf", [128, 1])
                selb = sb("sel", [128, 2, NT]); rank = sb("rank", [128, 2, NT])
                tot = sb("tot", [1, 2, NT]); totc = sb("totc", [1, 2, NT])
                offs = sb("offs", [128, 2, NT])
                tlow = sb("tlow", [128, 128]); onec = sb("onec", [128, 1])
                onesr = sb("onesr", [1, 128]); idf32 = sb("idf32", [128, 128])
                st0 = sb("st0", [1, 16]); tr2 = sb("tr2", [2, 128]); m2 = sb("m2", [2, 1])
                iotaR = sb("ioR", [128, 128]); iotaC = sb("ioC", [128, 1])
                iotaI = sb("ioI", [128, 128], I32)
                mr = sb("mr", [128, NT]); mrs = sb("mrs", [128, NT])
                fval = sb("fval", [128, NT]); fvalI = sb("fvalI", [128, NT], I32)
                cba = sb("cba", [128, NT * 128])
                gf = sb("gf", [128, NT]); idxf = sb("idxf", [128, 12])
                c512 = sb("c512", [128, 1])
                pfx = rstk.enter_context(nc.psum_tensor(f"{tag}_pfx", [128, 2, NT], F32))
                ptot = rstk.enter_context(nc.psum_tensor(f"{tag}_ptot", [128, 2, NT], F32))
                pidx = rstk.enter_context(nc.psum_tensor(f"{tag}_pidx", [128, 16], F32))
                pAR = rstk.enter_context(nc.psum_tensor(f"{tag}_pAR", [128, 512], F32))

                NIT = 49
                NBI = 48

                steps = []
                sc = dict(r_v=cnt.get("r_v", 0), r_s=cnt.get("r_s", 0),
                          r_g=cnt.get("r_g", 0), r_t=cnt.get("r_t", 0))

                def step(eng, waits, emit, inc_sem):
                    sc[inc_sem] += 1
                    steps.append((eng, list(waits), emit, inc_sem, sc[inc_sem]))
                    return sc[inc_sem]

                X = mybir.AxisListType.X

                def E(f):
                    return f

                def em_const(g_):
                    g_.memset(tlow[:], 1.0); g_.memset(onec[:], 1.0)
                    g_.memset(onesr[:], 1.0); g_.memset(idf32[:], 0.0)
                    g_.memset(neginf[:], -3.0e38); g_.memset(c512[:], float(NQ))
                    g_.drain()
                    g_.affine_select(out=idf32[:], in_=idf32[:], compare_op=ALU.not_equal,
                                     fill=1.0, base=0, pattern=[[-1, 128]], channel_multiplier=1)
                    g_.affine_select(out=tlow[:], in_=tlow[:], compare_op=ALU.is_ge, fill=0.0,
                                     base=0, pattern=[[1, 128]], channel_multiplier=-1)
                    g_.iota(iotaI[:], pattern=[[1, 128]], base=1, channel_multiplier=0)
                    g_.drain()
                    g_.tensor_copy(iotaR[:], iotaI[:])
                    g_.drain()
                    g_.iota(iotaI[:, 0:1], pattern=[[0, 1]], base=0, channel_multiplier=1)
                    g_.iota(fvalI[:], pattern=[[128, NT]], base=0, channel_multiplier=0)
                    g_.drain()
                    g_.tensor_copy(iotaC[:], iotaI[:, 0:1])
                    g_.tensor_copy(fval[:], fvalI[:])
                    return g_.drain()
                pdw = [("p_dw0", ph.tok("p_dw0", 16 * (NT // 2))), ("p_dw1", ph.tok("p_dw1", 16 * (NT // 2)))]
                t_const = step("g", [("p_sv", ph.tok("p_sv", NT))] + pdw, em_const, "r_g")

                def em_ms(v):
                    v.tensor_reduce(ms[:], sv[:, :, :], axis=X, op=ALU.max)
                    v.memset(na[:, 0:1], na_init[0])
                    v.memset(na[:, 1:2], na_init[1])
                    return v.drain()
                t_msv = step("v", [("p_sv", ph.tok("p_sv", NT))] + pdw, em_ms, "r_v")
                def em_msT(te):
                    return te.transpose(pAR[0:2, 0:128], ms[:], idf32[:])
                t_mst = step("t", [("r_v", t_msv), ("r_g", t_const)], em_msT, "r_t")
                def em_msE(v):
                    v.tensor_copy(tr2[:], pAR[0:2, 0:128])
                    v.drain()
                    v.tensor_reduce(m2[:], tr2[:], axis=X, op=ALU.max)
                    return v.drain()
                t_mse = step("v", [("r_t", t_mst)], em_msE, "r_v")
                def em_msT2(te):
                    return te.transpose(pAR[0:1, 128:130], m2[:], idf32[0:2, 0:2])
                t_mst2 = step("t", [("r_v", t_mse)], em_msT2, "r_t")
                def em_msE2(v):
                    v.tensor_copy(st0[0:1, 4:6], pAR[0:1, 128:130])
                    return v.drain()
                t_mse2 = step("v", [("r_t", t_mst2)], em_msE2, "r_v")
                def em_msB(te):
                    return te.matmul(pAR[:, 256:258], onesr[:], st0[0:1, 4:6],
                                     start=True, stop=True)
                t_msb = step("t", [("r_v", t_mse2)], em_msB, "r_t")
                def em_msE3(v):
                    v.tensor_copy(ms[:], pAR[:, 256:258])
                    return v.drain()
                t_msg = step("v", [("r_t", t_msb)], em_msE3, "r_v")

                for it in range(NIT):
                    def em_tb(v, it=it):
                        v.tensor_tensor(sh[:], ms[:], na[:], op=ALU.min)
                        v.drain()
                        v.tensor_copy(st0[0:1, 2:4], sh[0:1, :])
                        v.drain()
                        for r in range(2):
                            v.tensor_scalar(tbuf[:, r, :], sv[:, r, :], na[:, r:r + 1],
                                            sh[:, r:r + 1], op0=ALU.min, op1=ALU.subtract)
                        return v.drain()
                    t_tb = step("v", [("r_v", t_msg)] if it == 0 else [], em_tb, "r_v")
                    t_ex = step("s", [("r_v", t_tb)],
                                E(lambda s: s.activation(
                                    out=ebuf.ap().rearrange("p a b -> p (a b)"),
                                    in_=tbuf.ap().rearrange("p a b -> p (a b)"), func=AF.Exp)),
                                "r_s")
                    t_rd = step("v", [("r_s", t_ex)],
                                E(lambda v: v.tensor_reduce(red[:], ebuf[:, :, :], axis=X,
                                                            op=ALU.add)), "r_v")
                    t_ar = step("t", [("r_v", t_rd)],
                                E(lambda te: te.matmul(pAR[0:1, 0:2], onec[:], red[:],
                                                       start=True, stop=True)), "r_t")
                    def em_lnp(s):
                        s.copy(st0[0:1, 0:2], pAR[0:1, 0:2])
                        s.drain()
                        return s.activation(out=st0[0:1, 0:2], in_=st0[0:1, 0:2], func=AF.Ln)
                    t_ln = step("s", [("r_t", t_ar)], em_lnp, "r_s")
                    def em_na0(v):
                        v.scalar_tensor_tensor(out=st0[0:1, 6:7], in0=st0[0:1, 2:3],
                                               scalar=-logk[0], in1=st0[0:1, 0:1],
                                               op0=ALU.add, op1=ALU.add)
                        v.scalar_tensor_tensor(out=st0[0:1, 7:8], in0=st0[0:1, 3:4],
                                               scalar=-logk[1], in1=st0[0:1, 1:2],
                                               op0=ALU.add, op1=ALU.add)
                        return v.drain()
                    t_na0 = step("v", [("r_s", t_ln)], em_na0, "r_v")
                    t_nab = step("t", [("r_v", t_na0)],
                                 E(lambda te: te.matmul(pAR[:, 4:6], onesr[:], st0[0:1, 6:8],
                                                        start=True, stop=True)), "r_t")
                    def em_nae(v):
                        v.tensor_copy(na[:], pAR[:, 4:6])
                        return v.drain()
                    t_msg = step("v", [("r_t", t_nab)], em_nae, "r_v")

                CUT2 = None
                def em_si(v):
                    for r in range(2):
                        v.tensor_scalar(stil[:, r, :], sv[:, r, :], na[:, r:r + 1], None,
                                        op0=ALU.min)
                    v.drain()
                    v.tensor_scalar(lo[:], ms[:], -1.0, -1.0, op0=ALU.mult, op1=ALU.add)
                    v.tensor_scalar(hi[:], na[:], 1.0, None, op0=ALU.add)
                    return v.drain()
                t_prev = step("v", [], em_si, "r_v")

                for it in range(0 if CUT2 == "scan" else NBI):
                    def em_cnt(v):
                        v.tensor_tensor(vv[:], lo[:], hi[:], op=ALU.add)
                        v.drain()
                        v.tensor_scalar(vv[:], vv[:], 0.5, None, op0=ALU.mult)
                        v.drain()
                        v.tensor_scalar(cdump[:], stil[:, 0, :], vv[:, 0:1], None,
                                        op0=ALU.is_ge, op1=ALU.add, accum_out=cntb[:, 0:1])
                        v.drain()
                        v.tensor_scalar(cdump[:], stil[:, 1, :], vv[:, 1:2], None,
                                        op0=ALU.is_ge, op1=ALU.add, accum_out=cntb[:, 1:2])
                        return v.drain()
                    t_cnt = step("v", [], em_cnt, "r_v")
                    t_ac0 = step("t", [("r_v", t_cnt)],
                                 E(lambda te: te.matmul(pAR[0:1, 8:10], onec[:], cntb[:],
                                                        start=True, stop=True)), "r_t")
                    t_ac1 = step("s", [("r_t", t_ac0)],
                                 E(lambda s: s.copy(st0[0:1, 8:10], pAR[0:1, 8:10])), "r_s")
                    t_ac = step("t", [("r_s", t_ac1)],
                                E(lambda te: te.matmul(pAR[:, 12:14], onesr[:], st0[0:1, 8:10],
                                                       start=True, stop=True)), "r_t")
                    t_ac2 = step("v", [("r_t", t_ac)],
                                 E(lambda v: (v.tensor_copy(cntb[:], pAR[:, 12:14]), v.drain())[1]), "r_v")
                    def em_sel(v):
                        for r in range(2):
                            v.tensor_scalar(cond8[:, r:r + 1], cntb[:, r:r + 1], kk[r], None,
                                            op0=ALU.is_ge)
                            v.tensor_scalar(ncond8[:, r:r + 1], cntb[:, r:r + 1], kk[r], None,
                                            op0=ALU.is_lt)
                        v.drain()
                        v.select(lo[:], cond8[:], vv[:], lo[:], add_drain=True)
                        v.drain()
                        v.select(hi[:], ncond8[:], vv[:], hi[:], add_drain=True)
                        return v.drain()
                    t_prev = step("v", [], em_sel, "r_v")

                def em_vs(v):
                    for r in range(2):
                        v.tensor_scalar(m8[:, r, :], stil[:, r, :], hi[:, r:r + 1], None,
                                        op0=ALU.is_ge)
                    v.drain()
                    v.select(tbuf.ap().rearrange("p a b -> p (a b)"),
                             m8.ap().rearrange("p a b -> p (a b)"),
                             neginf.ap().to_broadcast([128, 2 * NT]),
                             stil.ap().rearrange("p a b -> p (a b)"), add_drain=True)
                    v.drain()
                    return v.tensor_reduce(vstar[:], tbuf[:, :, :], axis=X, op=ALU.max)
                t_vs = step("v", [], em_vs, "r_v")
                def em_vsT(te):
                    return te.transpose(pAR[0:2, 0:128], vstar[:], idf32[:])
                t_vst = step("t", [("r_v", t_vs)], em_vsT, "r_t")
                def em_vsE(v):
                    v.tensor_copy(tr2[:], pAR[0:2, 0:128])
                    v.drain()
                    v.tensor_reduce(m2[:], tr2[:], axis=X, op=ALU.max)
                    return v.drain()
                t_vse = step("v", [("r_t", t_vst)], em_vsE, "r_v")
                def em_vsT2(te):
                    return te.transpose(pAR[0:1, 128:130], m2[:], idf32[0:2, 0:2])
                t_vst2 = step("t", [("r_v", t_vse)], em_vsT2, "r_t")
                def em_vsE2(v):
                    v.tensor_copy(st0[0:1, 10:12], pAR[0:1, 128:130])
                    return v.drain()
                t_vse2 = step("v", [("r_t", t_vst2)], em_vsE2, "r_v")
                def em_vsB(te):
                    return te.matmul(pAR[:, 256:258], onesr[:], st0[0:1, 10:12],
                                     start=True, stop=True)
                t_vsb = step("t", [("r_v", t_vse2)], em_vsB, "r_t")
                def em_vsE3(v):
                    v.tensor_copy(vstar[:], pAR[:, 256:258])
                    return v.drain()
                t_vsa = step("v", [("r_t", t_vsb)], em_vsE3, "r_v")
                def em_mk(v):
                    for r in range(2):
                        v.tensor_scalar(mhi[:, r, :], stil[:, r, :], vstar[:, r:r + 1], None,
                                        op0=ALU.is_gt)
                        v.tensor_scalar(meq[:, r, :], stil[:, r, :], vstar[:, r:r + 1], None,
                                        op0=ALU.is_equal)
                    v.drain()
                    v.tensor_scalar(cdump[:], mhi[:, 0, :], 1.0, None, op0=ALU.mult,
                                    op1=ALU.add, accum_out=nhi[:, 0:1])
                    v.drain()
                    v.tensor_scalar(cdump[:], mhi[:, 1, :], 1.0, None, op0=ALU.mult,
                                    op1=ALU.add, accum_out=nhi[:, 1:2])
                    return v.drain()
                t_mk = step("v", [("r_v", t_vsa)], em_mk, "r_v")
                t_nh0 = step("t", [("r_v", t_mk)],
                             E(lambda te: te.matmul(pAR[0:1, 8:10], onec[:], nhi[:],
                                                    start=True, stop=True)), "r_t")
                t_nh1 = step("s", [("r_t", t_nh0)],
                             E(lambda s: s.copy(st0[0:1, 12:14], pAR[0:1, 8:10])), "r_s")
                t_nh2 = step("t", [("r_s", t_nh1)],
                             E(lambda te: te.matmul(pAR[:, 12:14], onesr[:], st0[0:1, 12:14],
                                                    start=True, stop=True)), "r_t")
                t_nha = step("v", [("r_t", t_nh2)],
                             E(lambda v: (v.tensor_copy(nhi[:], pAR[:, 12:14]), v.drain())[1]), "r_v")
                def em_p1(te):
                    for r in range(2):
                        te.matmul(pfx[:, r, :], tlow[:], meq[:, r, :], start=True, stop=True)
                        i = te.matmul(ptot[0:1, r, :], onec[:], meq[:, r, :], start=True,
                                      stop=True)
                    return i
                t_p1 = step("t", [("r_v", t_mk), ("r_g", t_const)], em_p1, "r_t")
                def em_sc1(v):
                    v.tensor_copy(totc[0:1, :, :], ptot[0:1, :, :])
                    v.drain()
                    for r in range(2):
                        v.tensor_tensor_scan(tot[0:1, r, :], totc[0:1, r, :], totc[0:1, r, :],
                                             0.0, op0=ALU.add, op1=ALU.bypass)
                    v.drain()
                    v.tensor_tensor(tot[0:1, :, :], tot[0:1, :, :], totc[0:1, :, :],
                                    op=ALU.subtract)
                    return v.drain()
                t_sc1 = step("v", [("r_t", t_p1)], em_sc1, "r_v")
                def em_ob(te):
                    return te.matmul(pAR[:, 64:64 + 2 * NT], onesr[:],
                                     tot.ap().rearrange("p a b -> p (a b)"),
                                     start=True, stop=True)
                t_ob1a = step("t", [("r_v", t_sc1)], em_ob, "r_t")
                t_ob1 = step("v", [("r_t", t_ob1a)],
                             E(lambda v: (v.tensor_copy(
                                 offs.ap().rearrange("p a b -> p (a b)"),
                                 pAR[:, 64:64 + 2 * NT]), v.drain())[1]), "r_v")
                def em_sel2(v):
                    v.tensor_tensor(rank[:, :, :], pfx[:, :, :], offs[:, :, :], op=ALU.add)
                    v.drain()
                    for r in range(2):
                        v.tensor_scalar(tbuf[:, r, :], rank[:, r, :], nhi[:, r:r + 1], kk[r],
                                        op0=ALU.add, op1=ALU.is_le)
                    v.drain()
                    v.tensor_tensor(tbuf[:, :, :], tbuf[:, :, :], meq[:, :, :], op=ALU.mult)
                    v.drain()
                    v.tensor_tensor(selb[:, :, :], tbuf[:, :, :], mhi[:, :, :], op=ALU.add)
                    return v.drain()
                t_sel2 = step("v", [("r_v", t_ob1), ("r_v", t_nha)], em_sel2, "r_v")
                def em_p2(te):
                    for r in range(2):
                        te.matmul(pfx[:, r, :], tlow[:], selb[:, r, :], start=True, stop=True)
                        i = te.matmul(ptot[0:1, r, :], onec[:], selb[:, r, :], start=True,
                                      stop=True)
                    return i
                t_p2s = step("t", [("r_v", t_sel2)], em_p2, "r_t")
                t_sc2 = step("v", [("r_t", t_p2s)], em_sc1, "r_v")
                t_ob2a = step("t", [("r_v", t_sc2)], em_ob, "r_t")
                t_ob2 = step("v", [("r_t", t_ob2a)],
                             E(lambda v: (v.tensor_copy(
                                 offs.ap().rearrange("p a b -> p (a b)"),
                                 pAR[:, 64:64 + 2 * NT]), v.drain())[1]), "r_v")
                def em_rank2(v):
                    v.tensor_tensor(rank[:, :, :], pfx[:, :, :], offs[:, :, :], op=ALU.add)
                    return v.drain()
                t_rank2 = step("v", [("r_v", t_ob2)], em_rank2, "r_v")
                def em_g(v):
                    v.tensor_scalar(m8[:, 0, :], selb[:, 0, :], 0.0, None, op0=ALU.is_equal)
                    v.tensor_scalar(gf[:], rank[:, 0, :], -1.0, None, op0=ALU.add)
                    v.drain()
                    v.select(gf[:], m8[:, 0, :], c512.ap().to_broadcast([128, NT]), gf[:],
                             add_drain=True)
                    v.drain()
                    v.tensor_copy(gIv[:], gf[:])
                    return v.drain()
                t_g = step("v", [("r_v", t_rank2)], em_g, "r_v")

                t_cb_prev = t_g
                t_mmslot = {}
                nslot = 0
                for r in range(2):
                    for rs in range(RSUB[r]):
                        col = (0 if r == 0 else RSUB[0]) + rs
                        def em_cb(v, r=r, rs=rs):
                            v.tensor_tensor(mr[:], rank[:, r, :], selb[:, r, :], op=ALU.mult)
                            v.drain()
                            v.tensor_scalar(mrs[:], mr[:], -128.0 * rs, None, op0=ALU.add)
                            v.drain()
                            for f in range(NT):
                                v.tensor_tensor(cba[:, f * 128:(f + 1) * 128],
                                                mrs[:, f:f + 1].to_broadcast([128, 128]),
                                                iotaR[:], op=ALU.is_equal)
                            return v.drain()
                        waits = [("r_v", t_cb_prev)]
                        if nslot >= 1:
                            waits.append(("r_t", t_mmslot[nslot - 1]))
                        t_cb = step("v", waits, em_cb, "r_v")
                        def em_mm(te, col=col):
                            for f in range(NT):
                                te.matmul(pidx[:, col:col + 1],
                                          cba[:, f * 128:(f + 1) * 128], iotaC[:],
                                          start=(f == 0), stop=False,
                                          skip_group_check=True)
                                i = te.matmul(pidx[:, col:col + 1],
                                              cba[:, f * 128:(f + 1) * 128],
                                              fval[:, f:f + 1],
                                              start=False, stop=(f == NT - 1),
                                              skip_group_check=True)
                            return i
                        t_mm = step("t", [("r_v", t_cb)], em_mm, "r_t")
                        t_mmslot[nslot] = t_mm
                        t_cb_prev = t_cb
                        nslot += 1
                def em_cvt(v):
                    v.tensor_copy(idxf[:], pidx[:, 0:12])
                    v.drain()
                    v.tensor_copy(idxu[:], idxf[:])
                    return v.drain()
                t_cvt = step("v", [("r_t", t_mmslot[nslot - 1])], em_cvt, "r_v")

                engines = {"v": "vector", "s": "scalar", "g": "gpsimd", "t": "tensor"}
                for k in ("r_v", "r_s", "r_g", "r_t"):
                    cnt[k] = sc[k]
                with nc.Block() as block:
                    def make_emitter(ekey):
                        def run(eng):
                            for (e, waits, emit, sem_name, val) in steps:
                                if e != ekey:
                                    continue
                                for (wsem, wval) in waits:
                                    eng.wait_ge(sems[wsem], wval)
                                last = emit(eng)
                                last.then_inc(sems[sem_name], 1)
                        return run
                    _eng_used = {e for (e, *_rest) in steps}
                    if "v" in _eng_used:
                        block.vector(make_emitter("v"))
                    if "s" in _eng_used:
                        block.scalar(make_emitter("s"))
                    if "g" in _eng_used:
                        block.gpsimd(make_emitter("g"))
                    if "t" in _eng_used:
                        block.tensor(make_emitter("t"))



            NRQ = NQ // 128
            NRK = NKV // 128
            NCHK = S // 1024

            with contextlib.ExitStack() as hstk:
                def sbh(name, shape, dt=F32):
                    return hstk.enter_context(nc.sbuf_tensor(f"{tag}_{name}", shape, dt))
                xqnT = sbh("xqnT", [128, NCH, NQ], BF16)
                xkvnT = sbh("xkvnT", [128, NCH, NKV], BF16)
                ptH = hstk.enter_context(nc.psum_tensor(f"{tag}_ptH", [128, 1024], BF16))
                ppj = hstk.enter_context(nc.psum_tensor(f"{tag}_ppj", [128, 2, 512], F32))
                psSh = hstk.enter_context(nc.psum_tensor(f"{tag}_psSh", [128, 1, 2, 512], F32))
                psAVh = hstk.enter_context(nc.psum_tensor(f"{tag}_psAVh", [128, 2, 512], F32))

                gstk_in = contextlib.ExitStack()
                def sbg(name, shape, dt=F32):
                    return gstk_in.enter_context(nc.sbuf_tensor(f"{tag}_{name}", shape, dt))
                xgp = sbg("xgp", [128, D])
                gst = sbg("gst", [128, 12]); gmv = sbg("gmv", [128, 2])
                grr = sbg("grr", [128, 1]); gnm = sbg("gnm", [128, 1])
                gxm = sbg("gxm", [128, D]); gss = sbg("gss", [128, 1])
                gcp = sbg("gcp", [128, 1])
                gmb = sbg("gmb", [128, D])
                xqn = sbg("xqn", [128, NRQ, D], BF16)
                xkvn = sbg("xkvn", [128, NRK, D], BF16)

                steps2 = []
                sc2 = {}
                for nm in ["h_v", "h_s", "h_t", "h_dl", "h_dw", "g_g", "g_c", "s_w", "h_w0", "h_w1"]:
                    sc2[nm] = cnt.get(nm, 0)

                def st2(eng, waits, emit, inc_sem, inc_n=1):
                    sc2[inc_sem] += inc_n
                    steps2.append((eng, list(waits), emit, inc_sem, inc_n))
                    return sc2[inc_sem]

                X = mybir.AxisListType.X

                def em_c(sy):
                    return sy.dma_start(out=gmb[:, 0:D], in_=gamma_b[:])
                pdw2 = [("p_dw0", ph.tok("p_dw0", 16 * (NT // 2))), ("p_dw1", ph.tok("p_dw1", 16 * (NT // 2))), ("x_sn", cnt.get("x_sn", 0))]
                t_cw = st2("y", [("r_v", cnt["r_v"])] + pdw2, em_c, "s_w", 16)

                f_prev = cnt.get("f_xd", 0)
                t_null = st2("y", [("f_xd", f_prev)],
                             lambda sy: sy.dma_start(out=heavy_d[512:513, :],
                                                     in_=nullq_b[0:1, :]), "h_dw", 16)

                t_gn = {}
                prevn = None
                for i in range(NRQ + NRK):
                    w = [("r_v", cnt["r_v"])] + pdw2
                    if xwait2 is not None:
                        w.append(xwait2)
                    if prevn is not None:
                        w.append(("h_v", prevn))
                    t_ga = st2("g", w,
                               (lambda i=i: lambda g_: g_.indirect_dma_start(
                                   out=xgp[:], out_offset=None, in_=xsrc,
                                   in_offset=bass.IndirectOffsetOnAxis(
                                       ap=idxu[:, i:i + 1], axis=0)))(), "g_g", 16)
                    def em_n1(v, i=i):
                        v.bn_stats(gst[:, 0:6], xgp[:, 0:512])
                        v.bn_stats(gst[:, 6:12], xgp[:, 512:1024])
                        v.drain()
                        v.bn_aggr(gmv[:], gst[:])
                        v.drain()
                        v.tensor_scalar(grr[:], gmv[:, 1:2], 1e-5, None, op0=ALU.add)
                        return v.drain()
                    t_n1 = st2("v", [("g_g", sc2["g_g"]), ("s_w", t_cw)], em_n1, "h_v")
                    def em_s1(s):
                        s.activation(out=grr[:], in_=grr[:], func=AF.Ln)
                        s.drain()
                        return s.activation(out=grr[:], in_=grr[:], func=AF.Exp, scale=-0.5)
                    t_s1 = st2("s", [("h_v", t_n1)], em_s1, "h_s")
                    def em_n2(v):
                        v.scalar_tensor_tensor(out=gnm[:], in0=gmv[:, 0:1], scalar=-1.0,
                                               in1=grr[:], op0=ALU.mult, op1=ALU.mult)
                        v.drain()
                        v.tensor_scalar(gxm[:], xgp[:], grr[:], gnm[:], op0=ALU.mult, op1=ALU.add)
                        v.drain()
                        v.tensor_tensor(xgp[:], gxm[:], gxm[:], op=ALU.mult)
                        v.drain()
                        v.tensor_reduce(gss[:], xgp[:], axis=mybir.AxisListType.X, op=ALU.add)
                        return v.drain()
                    t_n2 = st2("v", [("h_s", t_s1)], em_n2, "h_v")
                    def em_s2(s):
                        s.activation(out=gss[:], in_=gss[:], func=AF.Ln)
                        s.drain()
                        return s.activation(out=gss[:], in_=gss[:], func=AF.Exp, scale=0.5)
                    t_s2 = st2("s", [("h_v", t_n2)], em_s2, "h_s")
                    def em_n3(v, i=i):
                        v.tensor_scalar(gss[:], gss[:], 1e-12, None, op0=ALU.max)
                        v.drain()
                        v.reciprocal(gcp[:], gss[:])
                        v.drain()
                        v.tensor_scalar(gxm[:], gxm[:], gcp[:], 32.0, op0=ALU.mult, op1=ALU.mult)
                        v.drain()
                        dst = xqn[:, i, :] if i < NRQ else xkvn[:, i - NRQ, :]
                        v.tensor_tensor(dst, gxm[:], gmb[:, 0:D], op=ALU.mult)
                        return v.drain()
                    prevn = st2("v", [("h_s", t_s2)], em_n3, "h_v")
                    t_gn[i] = prevn

                t_tr = {}
                for i in range(NRQ + NRK):
                    def em_tr(te, i=i):
                        srcb = xqn[:, i, :] if i < NRQ else xkvn[:, i - NRQ, :]
                        for c in range(NCH):
                            ii = te.transpose(ptH[:, c * 128:(c + 1) * 128],
                                              srcb[:, c * 128:(c + 1) * 128], ident[:])
                        return ii
                    w = [("h_v", t_gn[i]), ("s_aux", 1)]
                    if i > 0:
                        w.append(("h_s", t_tr[i - 1]))
                    t_t = st2("t", w, em_tr, "h_t")
                    def em_te(s, i=i):
                        pv = ptH.ap().rearrange("p (c j) -> p c j", j=128)
                        if i < NRQ:
                            return s.copy(xqnT[:, :, i * 128:(i + 1) * 128], pv)
                        j = i - NRQ
                        return s.copy(xkvnT[:, :, j * 128:(j + 1) * 128], pv)
                    t_tr[i] = st2("s", [("h_t", t_t)], em_te, "h_s")

                steps2_p1 = steps2
                with nc.Block() as block:
                    def mk2a(ekey):
                        def run(eng):
                            for (e, waits, emit, sem_name, inc_n) in steps2_p1:
                                if e != ekey:
                                    continue
                                for (wsem, wval) in waits:
                                    eng.wait_ge(sems[wsem], wval)
                                emit(eng).then_inc(sems[sem_name], inc_n)
                        return run
                    _eng_used = {e for (e, *_rest) in steps2_p1}
                    if "v" in _eng_used:
                        block.vector(mk2a("v"))
                    if "s" in _eng_used:
                        block.scalar(mk2a("s"))
                    if "g" in _eng_used:
                        block.gpsimd(mk2a("g"))
                    if "t" in _eng_used:
                        block.tensor(mk2a("t"))
                    if "y" in _eng_used:
                        block.sync(mk2a("y"))
                gstk_in.close()
                kvfm = sbh("kvfm", [128, 4, NKV], BF16)
                vstg = sbh("vstg", [128, NKV], BF16)
                vtmh = sbh("vtmh", [128, NRK, 8, 65], BF16)
                wppb = sbh("wppb", [128, 2, NCH, 128], BF16)
                nksb = sbh("nksb", [128, 8])
                nksbf = sbh("nksbf", [128, 8], BF16)
                nvsb = sbh("nvsb", [1, 8 * 65], BF16)
                qfmh = sbh("qfmh", [128, NRQ, NQ], BF16)
                ensb = sbh("ensb", [1, 2, 512], BF16)
                ath = sbh("ath", [128, 2, 512], BF16)
                rsh = sbh("rsh", [128, 8])
                aoh = sbh("aoh", [128, 4, 512], BF16)
                aohT = sbh("aohT", [128, 4, 512], BF16)
                whob = sbh("whob", [128, 4, D], BF16)
                hvo = sbh("hvo", [128, 512])
                steps2 = []
                t_who = st2("g", [("h_s", sc2["h_s"])],
                            lambda g_: g_.dma_start(out=whob[:, :, :],
                                                    in_=whoT.rearrange("(c p) n -> p c n",
                                                                       p=128)), "h_dl", 16)
                def em_c2(sy):
                    return sy.dma_start(out=nksb[:], in_=nullk_dup[:])
                t_cw2 = st2("y", [("h_s", sc2["h_s"])], em_c2, "s_w", 16)
                t_cw3 = st2("g", [("h_s", sc2["h_s"])],
                            lambda g_: g_.dma_start(out=nvsb[0:1, :],
                                                    in_=nullv1.rearrange("h n -> (h n)")
                                                    .rearrange("x -> () x")), "g_g", 16)
                t_ones = st2("g", [("h_s", sc2["h_s"])],
                             lambda g_: g_.memset(vtmh[:, :, :, 64:65], 1.0), "g_c", 1)
                t_nkb = st2("v", [("s_w", sc2["s_w"])],
                            lambda v: v.tensor_copy(nksbf[:], nksb[:]), "h_v")

                t_ev = {}
                units = [("q", e, 0) for e in range(4)] + \
                        [("kv", e, h) for e in range(8) for h in range(2)]
                t_mm_u = {}
                prev_vt = None
                for ui, (kind, e, h) in enumerate(units):
                    wsrc = whqT if kind == "q" else whkvT
                    def em_dl(sy, wsrc=wsrc, e=e, ui=ui):
                        return sy.dma_start(out=wppb[:, ui % 2, :, :],
                                            in_=wsrc[:, e * 128:(e + 1) * 128]
                                            .rearrange("(c p) n -> p c n", p=128))
                    w = [("r_v", cnt["r_v"])] + pdw2
                    if ui >= 2:
                        w.append(("h_t", t_mm_u[ui - 2]))
                    sem_dl = "h_w0" if ui % 2 == 0 else "h_w1"
                    t_dl = st2("g", w, em_dl, sem_dl, 16)
                    def em_mm(te, kind=kind, e=e, h=h, ui=ui):
                        srcT = xqnT if kind == "q" else xkvnT
                        w0 = h * 512
                        for c in range(NCH):
                            ii = te.matmul(ppj[:, ui % 2, :], wppb[:, ui % 2, c, :],
                                           srcT[:, c, w0:w0 + 512],
                                           start=(c == 0), stop=(c == NCH - 1))
                        return ii
                    w = [(sem_dl, sc2[sem_dl]), ("h_s", t_tr[NRQ + NRK - 1])]
                    if ui >= 2:
                        w.append(("h_s", t_ev[ui - 2]))
                    t_mm = st2("t", w, em_mm, "h_t")
                    t_mm_u[ui] = t_mm
                    def em_ev(s, kind=kind, e=e, h=h, ui=ui):
                        if kind == "q":
                            return s.copy(qfmh[:, e, :], ppj[:, ui % 2, :])
                        if e < 4:
                            return s.copy(kvfm[:, e, h * 512:(h + 1) * 512], ppj[:, ui % 2, :])
                        return s.copy(vstg[:, h * 512:(h + 1) * 512], ppj[:, ui % 2, :])
                    w = [("h_t", t_mm)]
                    if kind == "kv" and e >= 5 and h == 0 and prev_vt is not None:
                        w.append(("h_s", prev_vt))
                    t_ev[ui] = st2("s", w, em_ev, "h_s")
                    if kind == "kv" and e >= 4 and h == 1:
                        j = e - 4
                        for r8 in range(8):
                            def em_vt(te, r8=r8):
                                return te.transpose(ptH[:, 0:128],
                                                    vstg[:, r8 * 128:(r8 + 1) * 128], ident[:])
                            w = [("h_s", t_ev[ui])]
                            if prev_vt is not None:
                                w.append(("h_s", prev_vt))
                            t_vt = st2("t", w, em_vt, "h_t")
                            def em_ve(s, j=j, r8=r8):
                                pv = ptH[:, 0:128].rearrange("p (a b) -> p a b", b=64)
                                return s.copy(vtmh[:, r8, 2 * j:2 * j + 2, 0:64], pv)
                            prev_vt = st2("s", [("h_t", t_vt), ("g_c", t_ones)], em_ve, "h_s")
                t_kv_done = t_ev[len(units) - 1]

                t_pair_done = {}
                for pj in range(4):
                    hA, hB = 2 * pj, 2 * pj + 1
                    def em_ns(te, pj=pj):
                        te.matmul(psSh[0:1, 0, 0, :], nksbf[0:64, 2 * pj:2 * pj + 1],
                                  qfmh[0:64, pj, :], start=True, stop=True)
                        return te.matmul(psSh[0:1, 0, 1, :], nksbf[64:128, 2 * pj + 1:2 * pj + 2],
                                         qfmh[64:128, pj, :], start=True, stop=True,
                                         tile_position=(64, 0))
                    w = [("h_s", t_kv_done), ("s_w", t_cw), ("h_v", t_nkb), ("g_g", t_cw3)]
                    if pj > 0:
                        w.append(("h_s", t_pair_done[pj - 1]))
                    if prev_vt is not None:
                        w.append(("h_s", prev_vt))
                    t_ns = st2("t", w, em_ns, "h_t")
                    def em_nse(s, pj=pj):
                        s.activation(out=ensb[0:1, 0, :], in_=psSh[0:1, 0, 0, :], func=AF.Exp,
                                     scale=0.125)
                        return s.activation(out=ensb[0:1, 1, :], in_=psSh[0:1, 0, 1, :],
                                            func=AF.Exp, scale=0.125)
                    t_nse = st2("s", [("h_t", t_ns)], em_nse, "h_s")
                    def em_nav(te, pj=pj):
                        for a in range(2):
                            for j in range(4):
                                ii = te.matmul(psAVh[:, a, j * 65:(j + 1) * 65],
                                               ensb[0:1, a, j * 128:(j + 1) * 128],
                                               nvsb[0:1, (2 * pj + a) * 65:(2 * pj + a + 1) * 65],
                                               start=(j == 0), stop=False,
                                               skip_group_check=True)
                        return ii
                    t_nav = st2("t", [("h_s", t_nse)], em_nav, "h_t")
                    t_ex_k = {}
                    t_av_prev = t_nav
                    for kc in range(NRK):
                        def em_sc(te, pj=pj, kc=kc):
                            te.matmul(psSh[:, 0, 0, :], kvfm[0:64, pj, kc * 128:(kc + 1) * 128],
                                      qfmh[0:64, pj, :], start=True, stop=True)
                            return te.matmul(psSh[:, 0, 1, :],
                                             kvfm[64:128, pj, kc * 128:(kc + 1) * 128],
                                             qfmh[64:128, pj, :], start=True, stop=True,
                                             tile_position=(64, 0))
                        w = []
                        if kc == 0:
                            w.append(("h_s", t_nse))
                        if kc >= 1:
                            w.append(("h_s", t_ex_k[kc - 1]))
                        t_sc = st2("t", w, em_sc, "h_t")
                        def em_ex(s, pj=pj, kc=kc):
                            s.activation(out=ath[:, 0, :], in_=psSh[:, 0, 0, :], func=AF.Exp,
                                         scale=0.125)
                            return s.activation(out=ath[:, 1, :], in_=psSh[:, 0, 1, :],
                                                func=AF.Exp, scale=0.125)
                        w = [("h_t", t_sc)]
                        if kc >= 1:
                            w.append(("h_t", t_av_prev))
                        t_ex = st2("s", w, em_ex, "h_s")
                        t_ex_k[kc] = t_ex
                        def em_av(te, pj=pj, kc=kc):
                            for a in range(2):
                                for j in range(4):
                                    ii = te.matmul(psAVh[:, a, j * 65:(j + 1) * 65],
                                                   ath[:, a, j * 128:(j + 1) * 128],
                                                   vtmh[:, kc, 2 * pj + a, :],
                                                   start=False, stop=(kc == NRK - 1 and j == 3),
                                                   skip_group_check=True)
                            return ii
                        t_av_prev = st2("t", [("h_s", t_ex)], em_av, "h_t")
                    def em_rc(v, pj=pj):
                        return v.reciprocal(
                            rsh.ap().rearrange("p (a j) -> p a j", j=4),
                            psAVh[:, :, 0:260].rearrange("p a (j n) -> p a j n", n=65)
                            [:, :, :, 64:65])
                    t_rc = st2("v", [("h_t", t_av_prev)], em_rc, "h_v")
                    def em_ae(s, pj=pj):
                        for a in range(2):
                            for j in range(4):
                                ii = s.activation(out=aoh[:, j, (2 * pj + a) * 64:
                                                           (2 * pj + a + 1) * 64],
                                                  in_=psAVh[:, a, j * 65:j * 65 + 64],
                                                  func=AF.Identity,
                                                  scale=rsh[:, a * 4 + j:a * 4 + j + 1])
                        return ii
                    t_pair_done[pj] = st2("s", [("h_v", t_rc)], em_ae, "h_s")

                t_hdw = sc2["h_dw"]
                prev_o = None
                for j in range(4):
                    def em_aot(te, j=j):
                        for c in range(4):
                            ii = te.transpose(ptH[:, c * 128:(c + 1) * 128],
                                              aoh[:, j, c * 128:(c + 1) * 128], ident[:])
                        return ii
                    w = [("h_s", t_pair_done[3])]
                    if prev_o is not None:
                        w.append(("h_v", prev_o))
                    t_at = st2("t", w, em_aot, "h_t")
                    def em_ate(s, j=j):
                        pv = ptH[:, 0:512].rearrange("p (c j2) -> p c j2", j2=128)
                        return s.copy(aohT[:, :, j * 128:(j + 1) * 128], pv)
                    t_ae2 = st2("s", [("h_t", t_at)], em_ate, "h_s")
                    def em_op(te, j=j):
                        for half in range(2):
                            for c in range(4):
                                ii = te.matmul(ppj[:, half, :], aohT[:, c, j * 128:(j + 1) * 128],
                                               whob[:, c, half * 512:(half + 1) * 512],
                                               start=(c == 0), stop=(c == 3))
                        return ii
                    t_op = st2("t", [("h_s", t_ae2), ("h_dl", t_who)], em_op, "h_t")
                    for half in range(2):
                        def em_oe(v, j=j, half=half):
                            v.tensor_copy(hvo[:], ppj[:, half, :])
                            return v.drain()
                        t_oe = st2("v", [("h_t", t_op),
                                         ("h_dw", t_null + 32 * j + 16 * half)], em_oe, "h_v")
                        prev_o = t_oe
                        def em_od(sy, j=j, half=half):
                            return sy.dma_start(out=heavy_d[j * 128:(j + 1) * 128,
                                                            half * 512:(half + 1) * 512],
                                                in_=hvo[:])
                        t_od = st2("y", [("h_v", t_oe), ("f_xd", f_prev)], em_od, "h_dw", 16)
                HEAVY_DONE = sc2["h_dw"]

                with nc.Block() as block:
                    def mk2(ekey):
                        def run(eng):
                            for (e, waits, emit, sem_name, inc_n) in steps2:
                                if e != ekey:
                                    continue
                                for (wsem, wval) in waits:
                                    eng.wait_ge(sems[wsem], wval)
                                last = emit(eng)
                                last.then_inc(sems[sem_name], inc_n)
                        return run
                    _eng_used = {e for (e, *_rest) in steps2}
                    if "v" in _eng_used:
                        block.vector(mk2("v"))
                    if "s" in _eng_used:
                        block.scalar(mk2("s"))
                    if "g" in _eng_used:
                        block.gpsimd(mk2("g"))
                    if "t" in _eng_used:
                        block.tensor(mk2("t"))
                    if "y" in _eng_used:
                        block.sync(mk2("y"))
                for k in sc2:
                    cnt[k] = sc2[k]

            cnt_pad = ("x_sn", cnt.get("x_sn", 0))
            phl = Ph(sems, cnt)
            with contextlib.ExitStack() as lstk:
                def sbl(name, shape, dt=BF16):
                    return lstk.enter_context(nc.sbuf_tensor(f"{tag}_{name}", shape, dt))
                wlb = sbl("wlb", [128, NCH, 512])
                lnb = sbl("lnbuf", [128, 2, NCH, 384])
                qfl = sbl("qfl", [128, 4, 1024])
                kfl = sbl("kfl", [128, 4, 1152])
                vtl = sbl("vtl", [128, 9, 8, 65])
                atl = sbl("atl", [128, 2, 2, 2, 128])
                rsl = sbl("rsl", [128, 8], F32)
                aol = sbl("aol", [128, 4, 128])
                aolT = sbl("aolT", [128, 4, 128])
                wlob = sbl("wlob", [128, 4, 1024])
                lmsk = sbl("lmsk", [128, 4 * 128])
                rowf = sbl("rowf", [128, 2], F32)
                ptL = lstk.enter_context(nc.psum_tensor(f"{tag}_ptL", [128, 1024], BF16))
                ppL = lstk.enter_context(nc.psum_tensor(f"{tag}_ppL", [128, 2, 512], F32))
                psSl = lstk.enter_context(nc.psum_tensor(f"{tag}_psSl", [128, 2, 2, 256], F32))
                psAVl = lstk.enter_context(nc.psum_tensor(f"{tag}_psAVl", [128, 2, 256], F32))
                psOl = lstk.enter_context(nc.psum_tensor(f"{tag}_psOl", [128, 2, 512], F32))

                steps3 = []
                sc3 = {nm: cnt.get(nm, 0) for nm in
                       ["l_dl0", "l_dl1", "l_gw", "l_mm", "l_qe", "l_sc", "l_ex", "l_mk",
                        "l_av", "l_rc", "l_ae", "l_tr", "l_xd", "h_v", "g_c"]}

                def st3(eng, waits, emit, inc_sem, inc_n=1):
                    sc3[inc_sem] += inc_n
                    steps3.append((eng, list(waits), emit, inc_sem, inc_n))
                    return sc3[inc_sem]

                def em_lm(g_):
                    g_.memset(lmsk[:], 1.0)
                    g_.memset(rowf[:], 0.0)
                    g_.drain()
                    g_.affine_select(out=lmsk[:, 0:128], in_=lmsk[:, 0:128],
                                     compare_op=ALU.is_ge, fill=0.0, base=63,
                                     pattern=[[-1, 128]], channel_multiplier=0)
                    g_.affine_select(out=lmsk[:, 128:256], in_=lmsk[:, 128:256],
                                     compare_op=ALU.is_ge, fill=0.0, base=-64,
                                     pattern=[[1, 128]], channel_multiplier=0)
                    g_.affine_select(out=lmsk[:, 256:384], in_=lmsk[:, 256:384],
                                     compare_op=ALU.is_ge, fill=0.0, base=-64,
                                     pattern=[[0, 128]], channel_multiplier=1)
                    g_.affine_select(out=lmsk[:, 384:512], in_=lmsk[:, 384:512],
                                     compare_op=ALU.is_ge, fill=0.0, base=63,
                                     pattern=[[0, 128]], channel_multiplier=-1)
                    g_.drain()
                    g_.memset(rowf[:, 0:1], 1.0)
                    g_.memset(rowf[:, 1:2], 1.0)
                    g_.drain()
                    g_.affine_select(out=rowf[:, 0:1], in_=rowf[:, 0:1], compare_op=ALU.is_ge,
                                     fill=0.0, base=-64, pattern=[[0, 1]], channel_multiplier=1)
                    g_.affine_select(out=rowf[:, 1:2], in_=rowf[:, 1:2], compare_op=ALU.is_ge,
                                     fill=0.0, base=63, pattern=[[0, 1]], channel_multiplier=-1)
                    return g_.drain()
                t_lm = None
                if True:
                    t_lm = st3("g", [("h_v", cnt["h_v"]), ("h_dw", HEAVY_DONE)], em_lm, "g_c", 1)
                def em_lm2(v):
                    v.tensor_scalar(lmsk[:, 0:128], lmsk[:, 0:128], rowf[:, 0:1], None,
                                    op0=ALU.max)
                    v.drain()
                    v.tensor_scalar(lmsk[:, 128:256], lmsk[:, 128:256], rowf[:, 1:2], None,
                                    op0=ALU.max)
                    return v.drain()
                t_lm2 = None
                if t_lm is not None:
                    t_lm2 = st3("v", [("g_c", t_lm)], em_lm2, "l_mk")

                def em_wlo(sy):
                    return sy.dma_start(out=wlob[:, :, :],
                                        in_=wloT.rearrange("(c p) n -> p c n", p=128))
                t_wlo = None
                if True:
                    t_wlo = st3("g", [("h_v", cnt["h_v"]), ("h_dw", HEAVY_DONE)], em_wlo,
                                "l_gw", 16)

                t_xadd = {}
                for ch in range(NCHK):
                    QSEC = [(0, 384), (384, 384), (768, 256)]
                    KSEC = [(0, 384), (384, 384), (768, 384)]
                    def em_wq_(g_, ch=ch):
                        return g_.dma_start(out=wlb[:, :, :],
                                            in_=wlqkvT[:, 0:512]
                                            .rearrange("(c p) n -> p c n", p=128))
                    w = [("l_gw", sc3["l_gw"]), ("h_dw", HEAVY_DONE)]
                    if ch > 0:
                        w.append(("l_mm", sc3["l_mm"]))
                    t_wq = st3("g", w, em_wq_, "l_gw", 16)
                    prods = [("q", e, sec) for e in range(4) for sec in range(3)]
                    prods += [("k", e, sec) for e in range(4) for sec in range(3)]
                    t_pev = {}
                    t_pmm = {}
                    t_wpart = {"q": t_wq}
                    for ui, (kind, e, sec) in enumerate(prods):
                        if kind == "q":
                            s0, ncols = QSEC[sec]
                            tok0 = 64 + ch * 1024 + s0
                        else:
                            s0, ncols = KSEC[sec]
                            tok0 = ch * 1024 + s0
                        if kind == "k" and e == 0 and sec == 0:
                            def em_wk_(g_, ch=ch):
                                return g_.dma_start(out=wlb[:, :, :],
                                                    in_=wlqkvT[:, 512:1024]
                                                    .rearrange("(c p) n -> p c n", p=128))
                            t_wpart["k"] = st3("g", [("l_mm", sc3["l_mm"])], em_wk_, "l_gw", 16)
                        def em_pl(sy, tok0=tok0, ncols=ncols, ui=ui):
                            return sy.dma_start(out=lnb[:, ui % 2, :, 0:ncols],
                                                in_=lnT_d[:, :, tok0:tok0 + ncols]
                                                .rearrange("c p j -> p c j"))
                        w = [("p_dw0", ph.tok("p_dw0", 16 * (NT // 2))),
                             ("p_dw1", ph.tok("p_dw1", 16 * (NT // 2))),
                             ("h_dw", HEAVY_DONE), cnt_pad]
                        if ui >= 2:
                            w.append(("l_mm", t_pmm[ui - 2]))
                        else:
                            w.append(("l_mm", sc3["l_mm"]))
                        sem_dl = "l_dl0" if ui % 2 == 0 else "l_dl1"
                        t_pl = st3("y", w, em_pl, sem_dl, 16)
                        def em_pm(te, kind=kind, e=e, ui=ui, ncols=ncols):
                            for c in range(NCH):
                                ii = te.matmul(ppL[:, ui % 2, 0:ncols],
                                               wlb[:, c, e * 128:(e + 1) * 128],
                                               lnb[:, ui % 2, c, 0:ncols],
                                               start=(c == 0), stop=(c == NCH - 1))
                            return ii
                        w = [(sem_dl, sc3[sem_dl]), ("l_gw", t_wpart[kind])]
                        if ui >= 2:
                            w.append(("l_qe", t_pev[ui - 2]))
                        else:
                            w.append(("l_qe", sc3["l_qe"]))
                        t_pm_cur = st3("t", w, em_pm, "l_mm")
                        t_pmm[ui] = t_pm_cur
                        def em_pe(s, kind=kind, e=e, sec=sec, ui=ui, ncols=ncols):
                            if kind == "q":
                                s0 = QSEC[sec][0]
                                return s.copy(qfl[:, e, s0:s0 + ncols], ppL[:, ui % 2, 0:ncols])
                            s0 = KSEC[sec][0]
                            return s.copy(kfl[:, e, s0:s0 + ncols], ppL[:, ui % 2, 0:ncols])
                        t_pev[ui] = st3("s", [("l_mm", t_pm_cur)], em_pe, "l_qe")
                    def em_wv(g_, ch=ch):
                        return g_.dma_start(out=wlb[:, :, 0:512],
                                            in_=wlqkvT[:, 1024:1536]
                                            .rearrange("(c p) n -> p c n", p=128))
                    t_wv = st3("g", [("l_mm", t_pmm[len(prods) - 1])], em_wv, "l_gw", 16)
                    prev_ve = None
                    for e in range(4):
                        for sec in range(3):
                            tok0 = ch * 1024 + KSEC[sec][0]
                            def em_vm(te, e=e, sec=sec):
                                ui2 = e * 3 + sec
                                for c in range(NCH):
                                    ii = te.matmul(ppL[:, ui2 % 2, 0:384],
                                                   wlb[:, c, e * 128:(e + 1) * 128],
                                                   lnb[:, ui2 % 2, c, 0:384],
                                                   start=(c == 0), stop=(c == NCH - 1))
                                return ii
                            ui = e * 3 + sec
                            def em_vl(sy, tok0=tok0, ui=ui):
                                return sy.dma_start(out=lnb[:, ui % 2, :, 0:384],
                                                    in_=lnT_d[:, :, tok0:tok0 + 384]
                                                    .rearrange("c p j -> p c j"))
                            sem_dl = "l_dl0" if ui % 2 == 0 else "l_dl1"
                            w = [("l_gw", sc3["l_gw"])]
                            if ui >= 2:
                                w.append(("l_mm", t_vmm[ui - 2]))
                            else:
                                w.append(("l_mm", sc3["l_mm"]))
                            t_vl = st3("y", w, em_vl, sem_dl, 16)
                            w = [(sem_dl, sc3[sem_dl])]
                            if ui >= 2:
                                w.append(("l_qe", t_vev[ui - 2]))
                            else:
                                w.append(("l_qe", sc3["l_qe"]))
                            t_vm = st3("t", w, em_vm, "l_mm")
                            if ui == 0:
                                t_vmm = {}
                                t_vev = {}
                            t_vmm[ui] = t_vm
                            def em_vev(s, e=e, sec=sec):
                                ui2 = e * 3 + sec
                                return s.copy(atl.ap().rearrange("p a b c d -> p (a b c d)")
                                              [:, 0:384], ppL[:, ui2 % 2, 0:384])
                            t_ve = st3("s", [("l_mm", t_vm)], em_vev, "l_qe")
                            t_vev[ui] = t_ve
                            for blk in range(3):
                                r8 = sec * 3 + blk
                                def em_vt(te, blk=blk):
                                    return te.transpose(ptL[:, 0:128],
                                                        atl.ap().rearrange(
                                                            "p a b c d -> p (a b c d)")
                                                        [:, blk * 128:(blk + 1) * 128], ident[:])
                                w = [("l_qe", t_ve)]
                                if prev_ve is not None:
                                    w.append(("l_qe", prev_ve))
                                t_vt = st3("t", w, em_vt, "l_tr")
                                def em_vte(s, e=e, r8=r8):
                                    pv = ptL[:, 0:128].rearrange("p (a b) -> p a b", b=64)
                                    return s.copy(vtl[:, r8, 2 * e:2 * e + 2, 0:64], pv)
                                wv = [("l_tr", t_vt)]
                                if t_lm is not None:
                                    wv.append(("g_c", t_lm))
                                prev_ve = st3("s", wv, em_vte, "l_qe")
                    t_prod_done = prev_ve if prev_ve is not None else t_pev[len(prods) - 1]
                    t_vo = st3("g", [("l_qe", t_prod_done)],
                               (lambda g_: g_.memset(vtl[:, :, :, 64:65], 1.0)), "g_c", 1)

                    if ch == 0:
                        t_lex = {}
                        t_lav = {}
                    prev_x = None
                    for qt in range(8):
                        gt = ch * 8 + qt
                        first = (gt == 0)
                        lastt = (gt == NT - 1)
                        for pj in range(4):
                            ui = qt * 4 + pj
                            pp = ui % 2
                            def em_lsc(te, qt=qt, pj=pj, pp=pp):
                                for kc in range(2):
                                    k0 = qt * 128 + kc * 128
                                    ii = te.matmul(psSl[:, 0, pp, kc * 128:(kc + 1) * 128],
                                                  kfl[0:64, pj, k0:k0 + 128],
                                                  qfl[0:64, pj, qt * 128:(qt + 1) * 128],
                                                  start=True, stop=True)
                                    ii = te.matmul(psSl[:, 1, pp, kc * 128:(kc + 1) * 128],
                                                   kfl[64:128, pj, k0:k0 + 128],
                                                   qfl[64:128, pj, qt * 128:(qt + 1) * 128],
                                                   start=True, stop=True,
                                                   tile_position=(64, 0))
                                return ii
                            w = [("l_qe", t_prod_done), ("g_c", t_vo)]
                            if ui >= 2:
                                w.append(("l_ex", t_lex[ui - 2]))
                            t_sc_ = st3("t", w, em_lsc, "l_sc")
                            def em_lex(s, pp=pp):
                                s.activation(out=atl[:, pp, 0, :, :]
                                             .rearrange("p a b -> p (a b)"),
                                             in_=psSl[:, 0, pp, 0:256], func=AF.Exp, scale=0.125)
                                return s.activation(out=atl[:, pp, 1, :, :]
                                                    .rearrange("p a b -> p (a b)"),
                                                    in_=psSl[:, 1, pp, 0:256], func=AF.Exp,
                                                    scale=0.125)
                            w = [("l_sc", t_sc_)]
                            if ui >= 2:
                                w.append(("l_av", t_lav[ui - 2]))
                            t_ex_ = st3("s", w, em_lex, "l_ex")
                            t_lex[ui] = t_ex_
                            def em_lmk(v, pp=pp, first=first, lastt=lastt):
                                m0 = lmsk[:, 256:384] if first else lmsk[:, 0:128]
                                m1 = lmsk[:, 384:512] if lastt else lmsk[:, 128:256]
                                for a in range(2):
                                    v.tensor_tensor(atl[:, pp, a, 0, :], atl[:, pp, a, 0, :],
                                                    m0, op=ALU.mult)
                                    v.tensor_tensor(atl[:, pp, a, 1, :], atl[:, pp, a, 1, :],
                                                    m1, op=ALU.mult)
                                return v.drain()
                            t_mk_ = st3("v", [("l_ex", t_ex_), ("l_mk", t_lm2)], em_lmk, "l_mk")
                            def em_lav(te, qt=qt, pj=pj, pp=pp):
                                for a in range(2):
                                    for kc in range(2):
                                        r8 = qt + kc
                                        ii = te.matmul(psAVl[:, a, 0:65],
                                                       atl[:, pp, a, kc, :],
                                                       vtl[:, r8, 2 * pj + a, :],
                                                       start=(kc == 0), stop=(kc == 1),
                                                       skip_group_check=True)
                                return ii
                            t_av_ = st3("t", [("l_mk", t_mk_)], em_lav, "l_av")
                            t_lav[ui] = t_av_
                            def em_lrc(v, pj=pj):
                                return v.reciprocal(rsl[:, 2 * pj:2 * pj + 2],
                                                    psAVl[:, :, 64:65])
                            t_rc_ = st3("v", [("l_av", t_av_)], em_lrc, "l_rc")
                            def em_lae(s, pj=pj):
                                for a in range(2):
                                    ii = s.activation(out=aol[:, 0, 0:64][:, 0:64] if False
                                                      else aol.ap().rearrange(
                                                          "p a b -> p (a b)")
                                                      [:, (2 * pj + a) * 64:
                                                       (2 * pj + a + 1) * 64],
                                                      in_=psAVl[:, a, 0:64],
                                                      func=AF.Identity,
                                                      scale=rsl[:, 2 * pj + a:2 * pj + a + 1])
                                return ii
                            t_ae_ = st3("s", [("l_rc", t_rc_)], em_lae, "l_ae")
                        def em_lat(te, qt=qt):
                            for c in range(4):
                                ii = te.transpose(ptL[:, c * 128:(c + 1) * 128],
                                                  aol.ap().rearrange("p a b -> p (a b)")
                                                  [:, c * 128:(c + 1) * 128], ident[:])
                            return ii
                        w = [("l_ae", t_ae_)]
                        if prev_x is not None:
                            w.append(("l_xd", prev_x))
                        t_lat_ = st3("t", w, em_lat, "l_tr")
                        def em_late(s):
                            return s.copy(aolT.ap().rearrange("p a b -> p (a b)"),
                                          ptL[:, 0:512])
                        t_late_ = st3("s", [("l_tr", t_lat_)], em_late, "l_qe")
                        def em_lop(te, qt=qt):
                            for half in range(2):
                                for c in range(4):
                                    ii = te.matmul(psOl[:, half, :],
                                                   aolT[:, c, :],
                                                   wlob[:, c, half * 512:(half + 1) * 512],
                                                   start=(c == 0), stop=(c == 3))
                            return ii
                        t_lop_ = st3("t", [("l_qe", t_late_), ("l_gw", t_wlo)], em_lop, "l_mm")
                        def em_lxd(v, gt=gt):
                            v.scalar_tensor_tensor(out=xres[:, gt, 0:512],
                                                   in0=psOl[:, 0, :], scalar=0.5,
                                                   in1=xres[:, gt, 0:512],
                                                   op0=ALU.mult, op1=ALU.add)
                            v.drain()
                            v.scalar_tensor_tensor(out=xres[:, gt, 512:1024],
                                                   in0=psOl[:, 1, :], scalar=0.5,
                                                   in1=xres[:, gt, 512:1024],
                                                   op0=ALU.mult, op1=ALU.add)
                            return v.drain()
                        prev_x = st3("v", [("l_mm", t_lop_)], em_lxd, "l_xd")
                        t_xadd[gt] = prev_x

                with nc.Block() as block:
                    def mk3(ekey):
                        def run(eng):
                            for (e, waits, emit, sem_name, inc_n) in steps3:
                                if e != ekey:
                                    continue
                                for (wsem, wval) in waits:
                                    eng.wait_ge(sems[wsem], wval)
                                last = emit(eng)
                                last.then_inc(sems[sem_name], inc_n)
                        return run
                    _eng_used = {e for (e, *_rest) in steps3}
                    if "v" in _eng_used:
                        block.vector(mk3("v"))
                    if "s" in _eng_used:
                        block.scalar(mk3("s"))
                    if "g" in _eng_used:
                        block.gpsimd(mk3("g"))
                    if "t" in _eng_used:
                        block.tensor(mk3("t"))
                    if "y" in _eng_used:
                        block.sync(mk3("y"))
                for k in sc3:
                    cnt[k] = sc3[k]

            ph4 = Ph(sems, cnt)
            with (
                nc.sbuf_tensor(f"{tag}_hb", [128, 2, D], F32) as hb,
                nc.Block() as block,
            ):
                @block.gpsimd
                def _(g_):
                    g_.wait_ge(sems["h_dw"], HEAVY_DONE)
                    for t in range(NT):
                        if t >= 2:
                            g_.wait_ge(sems["f_xd"], ph4.tok("f_xd", t - 1))
                        ph4.inc(g_.indirect_dma_start(
                            out=hb[:, t % 2, :], out_offset=None, in_=heavy_d,
                            in_offset=bass.IndirectOffsetOnAxis(ap=gIv[:, t:t + 1], axis=0)),
                            "f_g0" if t % 2 == 0 else "f_g1", 16)

                @block.vector
                def _(v):
                    for t in range(NT):
                        sfg = "f_g0" if t % 2 == 0 else "f_g1"
                        v.wait_ge(sems[sfg], ph4.tok(sfg, 16 * (t // 2 + 1)))
                        v.wait_ge(sems["l_xd"], t_xadd[t])
                        v.scalar_tensor_tensor(out=xres[:, t, :], in0=hb[:, t % 2, :],
                                               scalar=0.5, in1=xres[:, t, :],
                                               op0=ALU.mult, op1=ALU.add)
                        ph4.inc(v.drain(), "f_xd", 1)

        return ("f_xd", ph4.base["f_xd"])


    pre = X_READY

    def chain(res):
        return (res[0], res[1] + NT)

    if "s1" in want:
        pre = chain(emit_l0_ffn("s1", pre, x_in))
    if "s2" in want:
        pre = chain(emit_dense_attn("s2", pre, wqkv_l0, wot_l0, rope=False, snapshot_x=True))
        snap_tok = ("x_sn", cnt.get("x_sn", 0))
    if "s3" in want:
        pre = chain(emit_l0_ffn("s3", pre, xsnap_d, xwait2=snap_tok))
    if "s4" in want:
        pre = chain(emit_l1_ffn("s4", pre))
    if "s5" in want:
        pre = chain(emit_dense_attn("s5", pre, wqkv_l1, wot_l1, rope=True, snapshot_x=False))
    if "s6" in want:
        pre = chain(emit_l1_ffn("s6", pre))

    ph_o = Ph(sems, cnt)
    last_sem, last_top = pre
    with nc.Block() as block:
        @block.sync
        def _(sync):
            for t in range(NT):
                if last_sem == "s_in":
                    sync.wait_ge(sems["s_in"], last_top)
                else:
                    sync.wait_ge(sems[last_sem], max(0, last_top - NT + t + 1))
                ph_o.inc(sync.dma_start(out=y_out[t * 128:(t + 1) * 128, :],
                                        in_=xres[:, t, :]), "s_out", 16)
            sync.wait_ge(sems["s_out"], ph_o.tok("s_out", 16 * NT))

    stack.close()
    return nc


def _rope_tables(S):
    theta = 1.0 / (10000.0 ** (np.arange(0, 64, 2, dtype=np.float64) / 64.0))
    m = np.arange(S, dtype=np.float64)
    ang = np.outer(m, theta)
    ang = np.concatenate([ang, ang], axis=1)
    return np.cos(ang).astype(np.float32), np.sin(ang).astype(np.float32)


def _pack_qkv(WQ, WK, WV):
    out = np.empty((8, D, 384), dtype=np.float32)
    for g in range(8):
        sl = slice(g * 128, (g + 1) * 128)
        out[g, :, 0:128] = WQ.T[:, sl]
        out[g, :, 128:256] = WK.T[:, sl]
        out[g, :, 256:384] = WV.T[:, sl]
    return out


def make_in_maps(inputs, S=4096):
    a = {k: np.asarray(v) for k, v in inputs.items()}
    x = np.ascontiguousarray(a["L0_attn_in_features"][:, :S, :], dtype=np.float32)
    w1qT = np.ascontiguousarray(a["L1_ffn_weight_q"].T)
    w1s = np.ascontiguousarray(np.broadcast_to(
        a["L1_ffn_weight_scale"].astype(np.float32)[None, :], (128, D)))
    w1b = np.ascontiguousarray(np.broadcast_to(
        a["L1_ffn_bias"].astype(np.float32)[None, :], (128, D)))
    wqkv0 = _pack_qkv(a["L0_attn_q_proj_weight"], a["L0_attn_k_proj_weight"],
                      a["L0_attn_v_proj_weight"])
    wot0 = np.ascontiguousarray(a["L0_attn_o_proj_weight"].T.astype(np.float32))
    wqkv1 = _pack_qkv(a["L1_attn_W_Q"], a["L1_attn_W_K"], a["L1_attn_W_V"])
    wot1 = np.ascontiguousarray(a["L1_attn_W_O"].T.astype(np.float32))
    rc, rsn = _rope_tables(S)

    def bc(v):
        return np.ascontiguousarray(np.broadcast_to(np.asarray(v, dtype=np.float32)[None, :],
                                                    (128, D)))
    wkv = a["L0_ffn_Wkv"]
    kidx = []
    vidx = []
    for h in range(8):
        kidx.extend(range(h * 128, h * 128 + 64))
        vidx.extend(range(h * 128 + 64, h * 128 + 128))
    perm = []
    for j in range(4):
        perm.extend(kidx[2 * j * 64:(2 * j + 2) * 64])
    for j in range(4):
        perm.extend(vidx[2 * j * 64:(2 * j + 2) * 64])
    whkvT = np.ascontiguousarray(wkv[perm, :].T.astype(np.float32))
    nullkv = np.asarray(a["L0_ffn_null_kv"], dtype=np.float32)
    nullk_fm = np.ascontiguousarray(nullkv[0].T)
    nullv1 = np.concatenate([nullkv[1], np.ones((8, 1), np.float32)], axis=1)
    com = {"w1qT": w1qT, "w1scale": w1s, "w1bias": w1b,
           "wqkv_l0": wqkv0, "wot_l0": wot0, "wqkv_l1": wqkv1, "wot_l1": wot1,
           "ropec": rc, "ropes": rsn,
           "rtq_b": bc(a["L0_ffn_rt_q"][0]), "rtkv_b": bc(a["L0_ffn_rt_kv"][0]),
           "lnw_b": bc(a["L0_ffn_ln_w"]), "lnb_b": bc(a["L0_ffn_ln_b"]),
           "gamma_b": bc(a["L0_ffn_gamma"]), "nullq_b": bc(a["L0_ffn_null_q"]),
           "wlqkvT": np.ascontiguousarray(a["L0_ffn_Wqkv"].T.astype(np.float32)),
           "wloT": np.ascontiguousarray(a["L0_ffn_Wout_l"].T.astype(np.float32)),
           "whqT": np.ascontiguousarray(a["L0_ffn_Wq"].T.astype(np.float32)),
           "whkvT": whkvT,
           "whoT": np.ascontiguousarray(a["L0_ffn_Wout_h"].T.astype(np.float32)),
           "nullk_dup": np.ascontiguousarray(np.vstack([nullk_fm, nullk_fm])), "nullv1": nullv1}
    return [dict(com, x_in=x[b]) for b in range(8)]


def kernel(**inputs):
    nc = build_program(S=4096, want=("s1", "s2", "s3", "s4", "s5", "s6"))
    in_maps = make_in_maps(inputs, S=4096)
    res = run_bass_kernel_spmd(nc, in_maps, core_ids=list(range(8)))
    out = np.stack([res.results[b]["y_out"] for b in range(8)], axis=0)
    return out.astype(np.float32)
